# version 21
# baseline (speedup 1.0000x reference)
import numpy as np
from contextlib import ExitStack
import concourse.bass as bass
import concourse.mybir as mybir
from concourse.bass_utils import run_bass_kernel_spmd

F32 = mybir.dt.float32
BF16 = mybir.dt.bfloat16
I32 = mybir.dt.int32
AF = mybir.ActivationFunctionType
ALU = mybir.AluOpType
AX = mybir.AxisListType

GELU_C1 = 0.7978845608028654
GELU_C3 = 0.7978845608028654 * 0.044715
GELU_K = GELU_C1 / GELU_C3

LN_EPS = 1e-5
RMS_EPS = 1e-8
MASK_FILL = -10000.0
ROPE_BASE = 10000.0


class Cfg:
    def __init__(self, D=2048, NMLP=16384, NH=16, NKV=4, T_own=2048, halo=128,
                 n_super=256, G_super=4):
        self.D = D
        self.NMLP = NMLP
        self.NH = NH
        self.NKV = NKV
        self.DK = 128
        self.DKV = NKV * self.DK
        self.T_own = T_own
        self.halo = halo
        self.T = T_own + halo
        self.nT = self.T // 128
        self.nHT = D // 128
        self.nNT = NMLP // 128
        self.n_super = n_super
        self.G = G_super * n_super
        self.ffn1_tc = self._chunks(0, self.T)
        self.ffn2_tc = self._chunks(halo, T_own)

    def _chunks(self, start, total):
        out = []
        pos = start
        left = total
        while left > 0:
            c = min(768, left)
            out.append((pos, c))
            pos += c
            left -= c
        return out

    @staticmethod
    def pieces(length):
        n = (length + 511) // 512
        assert length % n == 0
        out = []
        pos = 0
        for i in range(n):
            ln = length // n
            out.append((pos, ln))
            pos += ln
        return out


CFG = Cfg()



ENG_OF_Q = {"qS": "sp", "qA": "act", "qG": "pool"}


class PB:

    def __init__(self, nc):
        self.nc = nc
        self.ctx = ExitStack()
        self.sems = {}
        self.cnt = {}
        self.pend = {k: [] for k in ("pe", "act", "dve", "pool", "sp")}
        self.lastw = {}
        for k in ("pe", "act", "dve", "pool", "qS", "qA", "qG"):
            self.sems[k] = self.ctx.enter_context(nc.semaphore(k))
            self.cnt[k] = 0

    def ins(self, eng, make, sem=None, after=()):
        for s, v in after:
            self.wait(eng, s, v)
        if sem is None:
            sem = eng
        if sem is False:
            self.pend[eng].append(lambda e, make=make: make(e))
            return None
        def f(e, make=make, sem=sem):
            make(e).then_inc(self.sems[sem], 1)
        self.pend[eng].append(f)
        self.cnt[sem] += 1
        return self.cnt[sem]

    def raw(self, eng, make, after=()):
        for s, v in after:
            self.wait(eng, s, v)
        self.pend[eng].append(lambda e, make=make: make(e))

    def dma(self, q, out_ap, in_ap, after=(), bound=True, **kw):
        eng = ENG_OF_Q[q]
        if bound:
            self.wait(eng, q, self.cnt[q])
        for s, v in after:
            self.wait(eng, s, v)
        def f(e, o=out_ap, i=in_ap, kw=kw, q=q):
            e.dma_start(o, i, **kw).then_inc(self.sems[q], 16)
        self.pend[eng].append(f)
        self.cnt[q] += 16
        return self.cnt[q]

    def wait(self, eng, sem, val, exact=False):
        if val is None or val <= 0:
            return
        if sem in ("qS", "qA", "qG") and not exact:
            val = self.cnt[sem]
        key = (eng, sem)
        if self.lastw.get(key, 0) >= val:
            return
        self.lastw[key] = val
        def f(e, s=sem, v=val):
            e.wait_ge(self.sems[s], v)
        self.pend[eng].append(f)

    def flush(self):
        pend, self.pend = self.pend, {k: [] for k in self.pend}
        if not any(pend.values()):
            return
        with self.nc.Block() as blk:
            if pend["sp"]:
                @blk.sync
                def _(e, ops=pend["sp"]):
                    for f in ops:
                        f(e)
            if pend["pe"]:
                @blk.tensor
                def _(e, ops=pend["pe"]):
                    for f in ops:
                        f(e)
            if pend["act"]:
                @blk.scalar
                def _(e, ops=pend["act"]):
                    for f in ops:
                        f(e)
            if pend["dve"]:
                @blk.vector
                def _(e, ops=pend["dve"]):
                    for f in ops:
                        f(e)
            if pend["pool"]:
                @blk.gpsimd
                def _(e, ops=pend["pool"]):
                    for f in ops:
                        f(e)
        for s in ("pe", "act", "dve", "pool"):
            for e2 in ("pe", "act", "dve", "pool", "sp"):
                self.lastw[(e2, s)] = self.cnt[s]

    def drain_queues(self):
        for q in ("qS", "qA", "qG"):
            self.wait(ENG_OF_Q[q], q, self.cnt[q])

    def sb(self, name, shape, dtype):
        return self.ctx.enter_context(self.nc.sbuf_tensor(name, shape, dtype))

    def ps(self, name, shape, dtype=F32):
        return self.ctx.enter_context(self.nc.psum_tensor(name, shape, dtype))




def build_program(cfg, first_tile_masked):
    nc = bass.Bass("TRN2", target_bir_lowering=False)
    D, NMLP, T = cfg.D, cfg.NMLP, cfg.T
    nHT, nNT, nT = cfg.nHT, cfg.nNT, cfg.nT
    NH, NKV, DK, DKV = cfg.NH, cfg.NKV, cfg.DK, cfg.DKV

    x_in = nc.dram_tensor("x", [T, D], F32, kind="ExternalInput").ap()
    gW = nc.dram_tensor("gW", [NMLP, D // 2], I32, kind="ExternalInput").ap()
    gS = nc.dram_tensor("gS", [NMLP, D // 32], F32, kind="ExternalInput").ap()
    uW = nc.dram_tensor("uW", [NMLP, D // 2], I32, kind="ExternalInput").ap()
    uS = nc.dram_tensor("uS", [NMLP, D // 32], F32, kind="ExternalInput").ap()
    dW = nc.dram_tensor("dW", [D, NMLP // 2], I32, kind="ExternalInput").ap()
    dS = nc.dram_tensor("dS", [D, NMLP // 32], F32, kind="ExternalInput").ap()
    Wq = nc.dram_tensor("Wq", [D, D], F32, kind="ExternalInput").ap()
    Wk = nc.dram_tensor("Wk", [DKV, D], F32, kind="ExternalInput").ap()
    Wv = nc.dram_tensor("Wv", [DKV, D], F32, kind="ExternalInput").ap()
    Wo = nc.dram_tensor("Wo", [D, D], F32, kind="ExternalInput").ap()
    bq_in = nc.dram_tensor("bq", [1, D], F32, kind="ExternalInput").ap()
    bk_in = nc.dram_tensor("bk", [1, DKV], F32, kind="ExternalInput").ap()
    bv_in = nc.dram_tensor("bv", [1, DKV], F32, kind="ExternalInput").ap()
    bo_in = nc.dram_tensor("bo", [1, D], F32, kind="ExternalInput").ap()
    qn4_in = nc.dram_tensor("qn4", [128, 512], F32, kind="ExternalInput").ap()
    kn4_in = nc.dram_tensor("kn4", [128, DKV], F32, kind="ExternalInput").ap()
    scale_in = nc.dram_tensor("scl", [128, 1], F32, kind="ExternalInput").ap()
    ident_in = nc.dram_tensor("ident", [128, 128], BF16, kind="ExternalInput").ap()
    e0_in = nc.dram_tensor("e0", [128, 128], BF16, kind="ExternalInput").ap()
    m01a_in = nc.dram_tensor("m01a", [128, 256], F32, kind="ExternalInput").ap()
    fila_in = nc.dram_tensor("fila", [128, 256], F32, kind="ExternalInput").ap()
    m01b_in = nc.dram_tensor("m01b", [128, 256], F32, kind="ExternalInput").ap()
    filb_in = nc.dram_tensor("filb", [128, 256], F32, kind="ExternalInput").ap()
    cos_in = nc.dram_tensor("cost", [128, nT * 256], F32, kind="ExternalInput").ap()
    sin_in = nc.dram_tensor("sint", [128, nT * 256], F32, kind="ExternalInput").ap()
    out = nc.dram_tensor("out", [cfg.T_own, D], F32, kind="ExternalOutput").ap()

    Wg_nat = nc.dram_tensor("WgN", [NMLP * (D // 128), 128], BF16, kind="Internal").ap()
    Wu_nat = nc.dram_tensor("WuN", [NMLP * (D // 128), 128], BF16, kind="Internal").ap()
    Wd_nat = nc.dram_tensor("WdN", [D, NMLP], BF16, kind="Internal").ap()
    Wq_nat = nc.dram_tensor("WqN", [D * (D // 128), 128], BF16, kind="Internal").ap()
    Wk_nat = nc.dram_tensor("WkN", [DKV, D], BF16, kind="Internal").ap()
    Wv_nat = nc.dram_tensor("WvN", [DKV, D], BF16, kind="Internal").ap()
    Wo_nat = nc.dram_tensor("WoN", [D * (D // 128), 128], BF16, kind="Internal").ap()
    xaTd = nc.dram_tensor("xaTd", [D, T], BF16, kind="Internal").ap()
    qTd = nc.dram_tensor("qTd", [T, NH * 128], BF16, kind="Internal").ap()
    aoTd = nc.dram_tensor("aoTd", [T, NH * 128], BF16, kind="Internal").ap()
    R1 = nc.dram_tensor("R1", [T, D], BF16, kind="Internal").ap()
    R2 = nc.dram_tensor("R2", [T, D], BF16, kind="Internal").ap()

    pb = PB(nc)

    ident = pb.sb("identS", [128, 128], BF16)
    e0 = pb.sb("e0S", [128, 128], BF16)
    scaleb = pb.sb("scaleS", [128, 1], F32)
    pb.ident = ident

    const_vals = sorted({LN_EPS, RMS_EPS, 1.0 / D, 1.0 / DK, GELU_C3, 1.0, 0.0})
    for ci, v in enumerate(const_vals):
        ct = pb.sb(f"cst{ci}", [128, 1], F32)
        pb.ins("pool", lambda e, t=ct, v=v: e.memset(t[:], float(v)), sem="pool")
        nc.const_aps.register(F32, float(v), ct[:, 0:1])

    pb.dma("qS", ident[:], ident_in)
    pb.dma("qS", e0[:], e0_in)
    pb.dma("qS", scaleb[:], scale_in)
    pb.drain_queues()
    pb.flush()

    dq = DequantDriver(pb, cfg, [
        (gW, gS, Wg_nat, NMLP, D, "g", True),
        (uW, uS, Wu_nat, NMLP, D, "u", True),
        (dW, dS, Wd_nat, D, NMLP, "d", False),
    ])
    dq.finish()
    pb.drain_queues()
    pb.flush()
    dq.ctx.close()

    emit_ffn_pass(pb, cfg, src=x_in, src_is_f32=True, dst=R1, dst_is_f32=False,
                  dst_row_off=0, tcs=cfg.ffn1_tc, WgT=Wg_nat, WuT=Wu_nat, WdT=Wd_nat, tag="f1")
    pb.flush()

    emit_attn(pb, cfg, R1=R1, R2=R2, Wq=Wq, Wk=Wk, Wv=Wv, Wo=Wo,
              Wq_nat=Wq_nat, Wk_nat=Wk_nat, Wv_nat=Wv_nat, Wo_nat=Wo_nat, xaTd=xaTd,
              qTd=qTd, aoTd=aoTd,
              bq_in=bq_in, bk_in=bk_in, bv_in=bv_in, bo_in=bo_in,
              qn4_in=qn4_in, kn4_in=kn4_in, scaleb=scaleb,
              m01a_in=m01a_in, fila_in=fila_in, m01b_in=m01b_in, filb_in=filb_in,
              cos_in=cos_in, sin_in=sin_in, e0=e0, ident=ident)
    pb.flush()

    emit_ffn_pass(pb, cfg, src=R2, src_is_f32=False, dst=out, dst_is_f32=True,
                  dst_row_off=-cfg.halo, tcs=cfg.ffn2_tc, WgT=Wg_nat, WuT=Wu_nat, WdT=Wd_nat, tag="f2")

    pb.drain_queues()
    pb.flush()
    pb.ctx.close()
    return nc


class DequantDriver:

    def __init__(self, pb, cfg, jobs):
        self.pb = pb
        self.cfg = cfg
        self.ctx = ExitStack()
        sb = lambda n, s, d: self.ctx.enter_context(pb.nc.sbuf_tensor(n, s, d))
        self.pk = [sb(f"dq_pk{i}", [128, 1024], I32) for i in range(2)]
        self.sc = [sb(f"dq_sc{i}", [128, 64], F32) for i in range(2)]
        self.blo = sb("dq_blo", [128, 1024], I32)
        self.btmp = sb("dq_btmp", [128, 1024], I32)
        self.wn = [sb(f"dq_wn{i}", [128, 2048], BF16) for i in range(2)]
        self.ld_done = {0: 0, 1: 0}
        self.w_written = {0: 0, 1: 0}
        self.step = 0
        self.jobs = []
        self.emitted = []
        self.covered = []
        for packed, scales, w_nat, N, K, tag, blocked in jobs:
            KCH = min(K, 2048)
            k_chunks = K // KCH
            n_tiles = N // 128
            if blocked:
                work = [(nt, kc) for nt in range(n_tiles) for kc in range(k_chunks)]
            else:
                work = [(nt, kc) for kc in range(k_chunks) for nt in range(n_tiles)]
            self.jobs.append(dict(packed=packed, scales=scales, w_nat=w_nat, D=K,
                                  KCH=KCH, words=KCH // 2, k_chunks=k_chunks,
                                  blocked=blocked, work=work))
            self.emitted.append(0)
            self.covered.append([])

    def ensure(self, ji, n):
        J = self.jobs[ji]
        n = min(n, len(J["work"]))
        while self.emitted[ji] < n:
            nt, kc = J["work"][self.emitted[ji]]
            self._tile(J, nt, kc)
            self.covered[ji].append(self.pb.cnt["qA"])
            self.emitted[ji] += 1

    def cover_count(self, ji, items):
        items = min(items, len(self.jobs[ji]["work"]))
        if items <= 0:
            return 0
        return self.covered[ji][items - 1]

    def finish(self):
        for ji in range(len(self.jobs)):
            self.ensure(ji, 10 ** 9)

    def _tile(self, J, nt, kc):
        pb = self.pb
        KCH, words = J["KCH"], J["words"]
        bp = self.step % 2
        self.step += 1
        row0 = nt * 128
        pk, sc, wn = self.pk[bp], self.sc[bp], self.wn[bp]
        blo, btmp = self.blo, self.btmp
        W = words
        pb.dma("qS", pk[:, 0:W], J["packed"][row0:row0 + 128, kc * W:(kc + 1) * W],
               after=[("dve", self.ld_done[bp])])
        pb.dma("qS", sc[:, 0:KCH // 32], J["scales"][row0:row0 + 128, kc * (KCH // 32):(kc + 1) * (KCH // 32)],
               bound=False)
        pb.ins("dve", lambda E, sc=sc, K2=KCH // 32: E.tensor_scalar(sc[:, 0:K2], sc[:, 0:K2], float(2.0 ** 126), None, ALU.mult),
               after=[("qS", pb.cnt["qS"]), ("dve", pb.cnt["dve"])])
        scv = sc[:, 0:KCH // 32].unsqueeze(2).broadcast_to([128, KCH // 32, 16])
        we = wn[:, 0:KCH:2].rearrange("p (a b) -> p a b", b=16)
        wo = wn[:, 1:KCH:2].rearrange("p (a b) -> p a b", b=16)
        pb.ins("dve", lambda E, pk=pk, blo=blo, W=W: E.tensor_scalar(blo[:, 0:W], pk[:, 0:W], 8, 28, ALU.bitwise_and, ALU.logical_shift_left),
               after=[("dve", pb.cnt["dve"])])
        pb.ins("dve", lambda E, pk=pk, btmp=btmp, W=W: E.tensor_scalar(btmp[:, 0:W], pk[:, 0:W], 7, 22, ALU.bitwise_and, ALU.logical_shift_left),
               after=[("dve", pb.cnt["dve"])])
        pb.ins("dve", lambda E, blo=blo, btmp=btmp, W=W: E.tensor_tensor(blo[:, 0:W], blo[:, 0:W], btmp[:, 0:W], ALU.bitwise_or),
               after=[("dve", pb.cnt["dve"])])
        pb.ins("dve", lambda E, we=we, scv=scv, blo=blo, W=W: E.tensor_tensor(we, blo[:, 0:W].bitcast(F32), scv, ALU.mult),
               after=[("dve", pb.cnt["dve"]), ("qA", self.w_written[bp])])
        pb.ins("dve", lambda E, pk=pk, blo=blo, W=W: E.tensor_scalar(blo[:, 0:W], pk[:, 0:W], 0x80, 24, ALU.bitwise_and, ALU.logical_shift_left),
               after=[("dve", pb.cnt["dve"])])
        pb.ins("dve", lambda E, pk=pk, btmp=btmp, W=W: E.tensor_scalar(btmp[:, 0:W], pk[:, 0:W], 0x70, 18, ALU.bitwise_and, ALU.logical_shift_left),
               after=[("dve", pb.cnt["dve"])])
        pb.ins("dve", lambda E, blo=blo, btmp=btmp, W=W: E.tensor_tensor(blo[:, 0:W], blo[:, 0:W], btmp[:, 0:W], ALU.bitwise_or),
               after=[("dve", pb.cnt["dve"])])
        self.ld_done[bp] = pb.ins("dve", lambda E, wo=wo, scv=scv, blo=blo, W=W: E.tensor_tensor(wo, blo[:, 0:W].bitcast(F32), scv, ALU.mult),
                                  after=[("dve", pb.cnt["dve"])])
        if J["blocked"]:
            NS = self.cfg.n_super
            nht_full = J["D"] // 128
            nht = KCH // 128
            sblk = row0 // NS
            o = row0 % NS
            dst = J["w_nat"].rearrange("(s ht n) c -> s ht n c", ht=nht_full, n=NS)[
                sblk, kc * nht:(kc + 1) * nht, o:o + 128, :]
            self.w_written[bp] = pb.dma("qA", dst.transpose([1, 0, 2]),
                                        wn[:, 0:KCH].rearrange("p (ht c) -> p ht c", c=128),
                                        after=[("dve", self.ld_done[bp])])
        else:
            self.w_written[bp] = pb.dma("qA", J["w_nat"][row0:row0 + 128, kc * KCH:(kc + 1) * KCH], wn[:, 0:KCH],
                                        after=[("dve", self.ld_done[bp])])


def emit_ffn_pass(pb, cfg, src, src_is_f32, dst, dst_is_f32, dst_row_off, tcs,
                  WgT, WuT, WdT, tag, dq=None):
    nc = pb.nc
    D, NMLP = cfg.D, cfg.NMLP
    nHT = cfg.nHT
    NS = cfg.n_super
    G = cfg.G
    n_gsup = G // NS
    nG = NMLP // G
    maxtc = max(l for _, l in tcs)

    with ExitStack() as st:
        sbl = lambda n, s, d: st.enter_context(nc.sbuf_tensor(n, s, d))
        psl = lambda n, s, d=F32: st.enter_context(nc.psum_tensor(n, s, d))
        xT = sbl(f"xT{tag}", [128, nHT * maxtc], BF16)
        yacc = sbl(f"y{tag}", [128, (maxtc // 128) * D], F32)
        inter = sbl(f"int{tag}", [128, (G // 128) * maxtc], BF16)
        wg = [sbl(f"wg{tag}{i}", [128, nHT * NS], BF16) for i in range(2)]
        wu = [sbl(f"wu{tag}{i}", [128, nHT * NS], BF16) for i in range(2)]
        wd = sbl(f"wd{tag}", [128, (G // 128) * D], BF16)
        xin = [sbl(f"lx{tag}{i}", [128, D], F32 if src_is_f32 else BF16) for i in range(2)]
        xc = sbl(f"lc{tag}", [128, D], F32)
        xbf = [sbl(f"lb{tag}{i}", [128, D], BF16) for i in range(2)]
        stt = sbl(f"ls{tag}", [128, 4], F32)
        rres = xin
        xc2 = sbl(f"l2{tag}", [128, D], F32) if dst_is_f32 else None
        rout = [xc, xc2] if dst_is_f32 else xbf
        gu_sc = sbl(f"gu{tag}", [128, 512], F32)
        gu_t2 = sbl(f"gt{tag}", [128, 512], F32)
        gu_th = sbl(f"gh{tag}", [128, 512], F32)
        gu_gl = sbl(f"gg{tag}", [128, 512], F32)
        gps = [psl(f"gp{tag}{i}", [128, 512]) for i in range(3)]
        ups = [psl(f"up{tag}{i}", [128, 512]) for i in range(3)]
        dps = [psl(f"dp{tag}{i}", [128, 512]) for i in range(2)]
        trp = [dps[i][:, 0:64].bitcast(BF16) for i in range(2)]

        ident = pb.ident
        pe_xT_reads = 0
        ln_in_free = {0: 0, 1: 0}
        xbf_free = {0: 0, 1: 0}
        tr_evac = {0: 0, 1: 0}
        wgu_free = {0: 0, 1: 0}
        wgu_loaded = {0: 0, 1: 0}
        wd_loaded = 0
        wd_free = 0
        gps_pe = {0: 0, 1: 0, 2: 0}
        gps_dve = {0: 0, 1: 0, 2: 0}
        ups_pe = {0: 0, 1: 0, 2: 0}
        ups_dve = {0: 0, 1: 0, 2: 0}
        dps_pe = {0: 0, 1: 0}
        dps_dve = {0: 0, 1: 0}
        inter_pe_reads = 0
        rr_free = {0: 0, 1: 0}
        ro_free = {0: 0, 1: 0}
        y_stt_done = 0

        n_sup_total = NMLP // NS
        for tci, (row0, tclen) in enumerate(tcs):
            ntt = tclen // 128
            pieces = cfg.pieces(tclen)
            for i in range(ntt):
                bp = i % 2
                r = row0 + i * 128
                cd = pb.dma("qA", xin[bp][:], src[r:r + 128, :],
                            after=[("dve", ln_in_free[bp]), ("qA", ro_free[bp])])
                pb.ins("dve", lambda e, b=bp: e.reduce_sum(stt[:, 0:1], xin[b][:], axis=AX.X),
                       after=[("qA", cd), ("dve", pb.cnt["dve"]), ("act", pb.cnt["act"])])
                pb.ins("dve", lambda e: e.tensor_scalar(stt[:, 0:1], stt[:, 0:1], 1.0 / D, None, ALU.mult),
                       after=[("dve", pb.cnt["dve"])])
                ln_in_free[bp] = pb.ins("dve", lambda e, b=bp: e.tensor_scalar(xc[:], xin[b][:], stt[:, 0:1], None, ALU.subtract),
                                        after=[("dve", pb.cnt["dve"]), ("act", pb.cnt["act"]),
                                               ("qA", max(ro_free[0], ro_free[1]))])
                pb.ins("act", lambda e, b=bp: e.activation(xbf[b][:], xc[:], AF.Square, accum_out=stt[:, 1:2]),
                       after=[("dve", pb.cnt["dve"]), ("pe", xbf_free[bp]), ("qA", ro_free[bp])])
                pb.ins("act", lambda e: e.activation(stt[:, 2:3], stt[:, 1:2], AF.Sqrt, bias=LN_EPS, scale=1.0 / D),
                       after=[("act", pb.cnt["act"])])
                pb.ins("dve", lambda e: e.reciprocal(stt[:, 3:4], stt[:, 2:3]),
                       after=[("act", pb.cnt["act"]), ("dve", pb.cnt["dve"])])
                pb.ins("act", lambda e, b=bp: e.activation(xbf[b][:], xc[:], AF.Copy, scale=stt[:, 3:4]),
                       after=[("dve", pb.cnt["dve"])])
                xbf_done = pb.cnt["act"]
                for ht in range(nHT):
                    tp = (i * nHT + ht) % 2
                    xbf_free[bp] = pb.ins(
                        "pe", lambda e, ht=ht, tp=tp, b=bp: e.transpose(trp[tp], xbf[b][:, ht * 128:(ht + 1) * 128], ident[:]),
                        after=[("act", xbf_done), ("dve", tr_evac[tp]), ("pe", pe_xT_reads),
                               ("dve", max(dps_dve[0], dps_dve[1]))])
                    tr_evac[tp] = pb.ins(
                        "dve", lambda e, ht=ht, i=i, tp=tp, tclen=tclen: e.tensor_copy(
                            xT[:, ht * tclen + i * 128: ht * tclen + i * 128 + 128], trp[tp]),
                        after=[("pe", pb.cnt["pe"]), ("dve", pb.cnt["dve"])])
            ln_done_dve = pb.cnt["dve"]

            for g in range(NMLP // G):
                if dq is not None and tci == 0:
                    need_d = 16 * ((g * G) // dq.jobs[2]["KCH"] + 1)
                    dq.ensure(2, need_d)
                    pb.wait("sp", "qA", dq.cover_count(2, need_d), exact=True)
                for ntg in range(G // 128):
                    wd_loaded = pb.dma("qS", wd[:, ntg * D:(ntg + 1) * D],
                                       WdT[:, g * G + ntg * 128: g * G + ntg * 128 + 128],
                                       transpose=True, bound=(ntg == 0),
                                       after=[("pe", wd_free)] if ntg == 0 else [])
                for si in range(n_gsup):
                    s = g * n_gsup + si
                    sp_ = s % 2
                    if dq is not None and tci == 0:
                        kpn = dq.jobs[0]["k_chunks"] * (NS // 128)
                        margin = min((s + 6) * kpn, len(dq.jobs[0]["work"]))
                        dq.ensure(0, margin)
                        dq.ensure(1, margin)
                        dq.ensure(2, s * len(dq.jobs[2]["work"]) // (NMLP // NS) + 32)
                        need = (s + 1) * kpn
                        pb.wait("sp", "qA", max(dq.cover_count(0, need), dq.cover_count(1, need)), exact=True)
                    rows = nHT * NS
                    pb.dma("qS", wg[sp_][:], WgT[s * rows:(s + 1) * rows, :],
                           transpose=True, after=[("pe", wgu_free[sp_])])
                    wgu_loaded[sp_] = pb.dma("qS", wu[sp_][:], WuT[s * rows:(s + 1) * rows, :],
                                             transpose=True, bound=False)
                    for ntl in range(NS // 128):
                        ng128 = si * (NS // 128) + ntl
                        for pidx, (poff, plen) in enumerate(pieces):
                            pp = (ng128 * len(pieces) + pidx) % 3
                            for ht in range(nHT):
                                pb.ins("pe", lambda e, ht=ht, ntl=ntl, pp=pp, sp_=sp_, poff=poff, plen=plen, tclen=tclen:
                                       e.matmul(gps[pp][:, 0:plen],
                                                wg[sp_][:, ht * NS + ntl * 128: ht * NS + ntl * 128 + 128],
                                                xT[:, ht * tclen + poff: ht * tclen + poff + plen],
                                                start=(ht == 0), stop=(ht == nHT - 1)),
                                       sem=(False if ht < nHT - 1 else "pe"),
                                       after=([("qS", wgu_loaded[sp_]), ("dve", max(ln_done_dve, gps_dve[pp]))]
                                              if ht == 0 else []))
                            g_pe = pb.cnt["pe"]
                            gps_pe[pp] = g_pe
                            for ht in range(nHT):
                                pb.ins("pe", lambda e, ht=ht, ntl=ntl, pp=pp, sp_=sp_, poff=poff, plen=plen, tclen=tclen:
                                       e.matmul(ups[pp][:, 0:plen],
                                                wu[sp_][:, ht * NS + ntl * 128: ht * NS + ntl * 128 + 128],
                                                xT[:, ht * tclen + poff: ht * tclen + poff + plen],
                                                start=(ht == 0), stop=(ht == nHT - 1)),
                                       sem=(False if ht < nHT - 1 else "pe"),
                                       after=[("dve", ups_dve[pp])] if ht == 0 else [])
                            u_pe = pb.cnt["pe"]
                            ups_pe[pp] = u_pe
                            pe_xT_reads = u_pe
                            wgu_free[sp_] = u_pe
                            pb.ins("act", lambda e, pp=pp, plen=plen: e.activation(gu_sc[:, 0:plen], gps[pp][:, 0:plen], AF.Square),
                                   after=[("pe", g_pe), ("dve", pb.cnt["dve"]), ("act", pb.cnt["act"])])
                            pb.ins("dve", lambda e, pp=pp, plen=plen: e.scalar_tensor_tensor(
                                       gu_t2[:, 0:plen], gu_sc[:, 0:plen], GELU_K, gps[pp][:, 0:plen], ALU.add, ALU.mult),
                                   after=[("act", pb.cnt["act"]), ("dve", pb.cnt["dve"])])
                            pb.ins("act", lambda e, plen=plen: e.activation(gu_th[:, 0:plen], gu_t2[:, 0:plen], AF.Tanh, scale=GELU_C3),
                                   after=[("dve", pb.cnt["dve"]), ("act", pb.cnt["act"])])
                            pb.ins("dve", lambda e, pp=pp, plen=plen: e.scalar_tensor_tensor(
                                       gu_gl[:, 0:plen], gu_th[:, 0:plen], 1.0, gps[pp][:, 0:plen], ALU.add, ALU.mult),
                                   after=[("act", pb.cnt["act"]), ("dve", pb.cnt["dve"])])
                            gps_dve[pp] = pb.cnt["dve"]
                            ups_dve[pp] = pb.ins("dve", lambda e, pp=pp, plen=plen, ng128=ng128, poff=poff, tclen=tclen:
                                                 e.scalar_tensor_tensor(
                                                     inter[:, ng128 * tclen + poff: ng128 * tclen + poff + plen],
                                                     gu_gl[:, 0:plen], 0.5, ups[pp][:, 0:plen], ALU.mult, ALU.mult),
                                                 after=[("pe", u_pe), ("dve", pb.cnt["dve"]),
                                                        ("pe", inter_pe_reads if (s % n_gsup == 0 and ntl == 0) else 0)])
                inter_done = pb.cnt["dve"]
                for ts in range(ntt):
                    for hc in range(D // 512):
                        pp = (ts * (D // 512) + hc) % 2
                        for ntg in range(G // 128):
                            pb.ins("pe", lambda e, ts=ts, hc=hc, ntg=ntg, pp=pp, tclen=tclen:
                                   e.matmul(dps[pp][:],
                                            inter[:, ntg * tclen + ts * 128: ntg * tclen + ts * 128 + 128],
                                            wd[:, ntg * D + hc * 512: ntg * D + hc * 512 + 512],
                                            start=(ntg == 0), stop=(ntg == G // 128 - 1)),
                                   sem=(False if ntg < G // 128 - 1 else "pe"),
                                   after=([("dve", inter_done), ("qS", wd_loaded), ("dve", dps_dve[pp])]
                                          if ntg == 0 else []))
                        d_pe = pb.cnt["pe"]
                        dps_pe[pp] = d_pe
                        if g == 0:
                            dps_dve[pp] = pb.ins("dve", lambda e, ts=ts, hc=hc, pp=pp: e.tensor_copy(
                                                     yacc[:, ts * D + hc * 512: ts * D + hc * 512 + 512], dps[pp][:]),
                                                 after=[("pe", d_pe), ("dve", max(pb.cnt["dve"], y_stt_done))])
                        else:
                            dps_dve[pp] = pb.ins("dve", lambda e, ts=ts, hc=hc, pp=pp: e.tensor_tensor(
                                                     yacc[:, ts * D + hc * 512: ts * D + hc * 512 + 512],
                                                     yacc[:, ts * D + hc * 512: ts * D + hc * 512 + 512],
                                                     dps[pp][:], ALU.add),
                                                 after=[("pe", d_pe), ("dve", pb.cnt["dve"])])
                inter_pe_reads = pb.cnt["pe"]
                wd_free = pb.cnt["pe"]
            y_done = pb.cnt["dve"]

            for ts in range(ntt):
                bp = ts % 2
                r = row0 + ts * 128
                cd = pb.dma("qA", rres[bp][:], src[r:r + 128, :],
                            after=[("dve", ln_in_free[bp]), ("qA", ro_free[bp])])
                ln_in_free[bp] = pb.ins("dve", lambda e, ts=ts, bp=bp: e.scalar_tensor_tensor(
                                            rout[bp][:], yacc[:, ts * D:(ts + 1) * D], 0.5, rres[bp][:], ALU.mult, ALU.add),
                                        after=[("qA", cd), ("dve", y_done), ("qA", ro_free[bp]),
                                               ("dve", pb.cnt["dve"]), ("act", pb.cnt["act"]), ("pe", xbf_free[bp])])
                ro_free[bp] = pb.dma("qA", dst[r + dst_row_off:r + dst_row_off + 128, :], rout[bp][:],
                                     after=[("dve", pb.cnt["dve"])])
            y_stt_done = pb.cnt["dve"]
            if dq is not None and tci == 0:
                dq.finish()
        pb.drain_queues()
        pb.flush()


def emit_attn(pb, cfg, R1, R2, Wq, Wk, Wv, Wo, Wq_nat, Wk_nat, Wv_nat, Wo_nat, xaTd,
              qTd, aoTd,
              bq_in, bk_in, bv_in, bo_in, qn4_in, kn4_in, scaleb,
              m01a_in, fila_in, m01b_in, filb_in, cos_in, sin_in, e0, ident):
    nc = pb.nc
    D, NH, NKV, DK, DKV = cfg.D, cfg.NH, cfg.NKV, cfg.DK, cfg.DKV
    nHT, nT, T = cfg.nHT, cfg.nT, cfg.T
    SB = min(4, NH)
    n_sb = NH // SB

    with ExitStack() as outer:
        sbo = lambda n, s, d: outer.enter_context(nc.sbuf_tensor(n, s, d))
        kT = sbo("a_kT", [128, NKV * T], BF16)
        vsb = sbo("a_v", [128, nT * DKV], BF16)
        bqT = sbo("a_bqT", [128, D], BF16)
        bkT = sbo("a_bkT", [128, DKV], BF16)
        bvT = sbo("a_bvT", [128, DKV], BF16)
        boT = sbo("a_boT", [128, D], BF16)
        qn4 = sbo("a_qn4", [128, 512], F32)
        kn4 = sbo("a_kn4", [128, DKV], F32)
        m01a = sbo("a_m01a", [128, 256], F32)
        fila = sbo("a_fila", [128, 256], F32)
        m01b = sbo("a_m01b", [128, 256], F32)
        filb = sbo("a_filb", [128, 256], F32)

        pb.dma("qS", qn4[:], qn4_in)
        pb.dma("qS", kn4[:], kn4_in, bound=False)
        pb.dma("qS", m01a[:], m01a_in, bound=False)
        pb.dma("qS", fila[:], fila_in, bound=False)
        pb.dma("qS", m01b[:], m01b_in, bound=False)
        pb.dma("qS", filb[:], filb_in, bound=False)
        with ExitStack() as stp:
            b32 = stp.enter_context(nc.sbuf_tensor("a_b32", [128, D], F32))
            for btile, srcb, width in ((bqT, bq_in, D), (bkT, bk_in, DKV),
                                       (bvT, bv_in, DKV), (boT, bo_in, D)):
                pb.ins("dve", lambda e, t=btile: e.memset(t[:], 0.0),
                       after=[("dve", pb.cnt["dve"])])
                pb.dma("qS", b32[0:1, 0:width], srcb, after=[("dve", pb.cnt["dve"])])
                pb.ins("dve", lambda e, t=btile, w=width: e.tensor_copy(t[0:1, 0:w], b32[0:1, 0:w]),
                       after=[("qS", pb.cnt["qS"]), ("dve", pb.cnt["dve"])])
            pb.drain_queues()
            pb.flush()

        with ExitStack() as st0:
            sb0 = lambda n, s, d: st0.enter_context(nc.sbuf_tensor(n, s, d))
            wnat = [sb0(f"a_wn{i}", [128, D], BF16) for i in range(2)]
            wn_free = {0: 0, 1: 0}
            for wsrc, wdst, rows, blocked in ((Wq, Wq_nat, D, True), (Wo, Wo_nat, D, True),
                                              (Wk, Wk_nat, DKV, False), (Wv, Wv_nat, DKV, False)):
                for rt in range(rows // 128):
                    bp = rt % 2
                    pb.dma("qG", wnat[bp][:], wsrc[rt * 128:(rt + 1) * 128, :],
                           after=[("qS", wn_free[bp])])
                    if blocked:
                        ch = (rt * 128) // 512
                        o = (rt * 128) % 512
                        dst = wdst.rearrange("(a ht j) c -> a ht j c", ht=nHT, j=512)[ch, :, o:o + 128, :]
                        dst = dst.transpose([1, 0, 2])
                        wn_free[bp] = pb.dma("qS", dst, wnat[bp][:].rearrange("p (ht c) -> p ht c", c=128),
                                             after=[("qG", pb.cnt["qG"])])
                    else:
                        wn_free[bp] = pb.dma("qS", wdst[rt * 128:(rt + 1) * 128, :], wnat[bp][:],
                                             after=[("qG", pb.cnt["qG"])])

            wkT = sb0("a_wkT", [128, nHT * DKV], BF16)
            wvT = sb0("a_wvT", [128, nHT * DKV], BF16)
            for ht in range(nHT):
                pb.dma("qS", wkT[:, ht * DKV:(ht + 1) * DKV], Wk_nat[:, ht * 128:(ht + 1) * 128],
                       transpose=True, bound=(ht == 0))
            for ht in range(nHT):
                pb.dma("qS", wvT[:, ht * DKV:(ht + 1) * DKV], Wv_nat[:, ht * 128:(ht + 1) * 128],
                       transpose=True, bound=(ht == 0))

            xin = [sb0(f"a_xi{i}", [128, D], BF16) for i in range(2)]
            xc = sb0("a_xc", [128, D], F32)
            xbf = [sb0(f"a_xb{i}", [128, D], BF16) for i in range(2)]
            stt = sb0("a_st", [128, 4], F32)
            xstg = [sb0(f"a_xs{i}", [128, nHT * 128], BF16) for i in range(2)]
            trp = [st0.enter_context(nc.psum_tensor(f"a_tr0{i}", [128, 128], BF16)) for i in range(2)]

            ln_in_free = {0: 0, 1: 0}
            xbf_free = {0: 0, 1: 0}
            xstg_written = {0: 0, 1: 0}
            tr_evac = {0: 0, 1: 0}
            for i in range(nT):
                bp = i % 2
                r = i * 128
                cd = pb.dma("qA", xin[bp][:], R1[r:r + 128, :], after=[("dve", ln_in_free[bp])])
                pb.ins("dve", lambda e, b=bp: e.reduce_sum(stt[:, 0:1], xin[b][:], axis=AX.X),
                       after=[("qA", cd), ("dve", pb.cnt["dve"]), ("act", pb.cnt["act"])])
                pb.ins("dve", lambda e: e.tensor_scalar(stt[:, 0:1], stt[:, 0:1], 1.0 / D, None, ALU.mult),
                       after=[("dve", pb.cnt["dve"])])
                ln_in_free[bp] = pb.ins("dve", lambda e, b=bp: e.tensor_scalar(xc[:], xin[b][:], stt[:, 0:1], None, ALU.subtract),
                                        after=[("dve", pb.cnt["dve"]), ("act", pb.cnt["act"])])
                pb.ins("act", lambda e, b=bp: e.activation(xbf[b][:], xc[:], AF.Square, accum_out=stt[:, 1:2]),
                       after=[("dve", pb.cnt["dve"]), ("pe", xbf_free[bp])])
                pb.ins("act", lambda e: e.activation(stt[:, 2:3], stt[:, 1:2], AF.Sqrt, bias=LN_EPS, scale=1.0 / D),
                       after=[("act", pb.cnt["act"])])
                pb.ins("dve", lambda e: e.reciprocal(stt[:, 3:4], stt[:, 2:3]),
                       after=[("act", pb.cnt["act"]), ("dve", pb.cnt["dve"])])
                pb.ins("act", lambda e, b=bp: e.activation(xbf[b][:], xc[:], AF.Copy, scale=stt[:, 3:4]),
                       after=[("dve", pb.cnt["dve"])])
                xbf_done = pb.cnt["act"]
                for ht in range(nHT):
                    tp = (i * nHT + ht) % 2
                    xbf_free[bp] = pb.ins(
                        "pe", lambda e, ht=ht, tp=tp, b=bp: e.transpose(trp[tp][:], xbf[b][:, ht * 128:(ht + 1) * 128], ident[:]),
                        after=[("act", xbf_done), ("act", tr_evac[tp])])
                    tr_evac[tp] = pb.ins(
                        "act", lambda e, ht=ht, bp=bp, tp=tp: e.copy(
                            xstg[bp][:, ht * 128:(ht + 1) * 128], trp[tp][:]),
                        after=[("pe", pb.cnt["pe"]), ("qA", xstg_written[bp])])
                xdst = xaTd.rearrange("(ht p) t -> p ht t", p=128)[:, :, i * 128:(i + 1) * 128]
                xstg_written[bp] = pb.dma("qA", xdst,
                                          xstg[bp][:].rearrange("p (ht n) -> p ht n", n=128),
                                          after=[("act", tr_evac[0]), ("act", tr_evac[1])])
            pb.drain_queues()
            pb.flush()

            qn_nat = sb0("a_qn", [128, 512], F32)
            qrot = sb0("a_qrot", [128, 512], BF16)
            rst = sb0("a_rst", [128, 40], F32)
            rop = sb0("a_rop", [128, max(512, 2 * DKV)], F32)
            cos4 = sb0("a_cos", [128, 256], F32)
            sin4 = sb0("a_sin", [128, 256], F32)
            xas = [sb0(f"a_xa{i}", [128, nHT * 128], BF16) for i in range(2)]
            qps = st0.enter_context(nc.psum_tensor("a_qps", [128, 512], F32))

            xa_free = {0: 0, 1: 0}
            qrot_pe = 0
            tr_evac = {0: 0, 1: 0}
            kv_dve = 0
            for i in range(nT):
                bp = i % 2
                xsrc = xaTd.rearrange("(ht p) t -> p ht t", p=128)[:, :, i * 128:(i + 1) * 128]
                cd = pb.dma("qA", xas[bp][:].rearrange("p (ht n) -> p ht n", n=128), xsrc,
                            after=[("pe", xa_free[bp])])
                pb.dma("qA", cos4[:], cos_in[:, i * 256:(i + 1) * 256], after=[("dve", kv_dve)], bound=False)
                cs = pb.dma("qA", sin4[:], sin_in[:, i * 256:(i + 1) * 256], bound=False)
                for ht in range(nHT):
                    pb.ins("pe", lambda e, ht=ht, b=bp: e.matmul(qps[:, 0:DKV],
                               xas[b][:, ht * 128:(ht + 1) * 128],
                               wkT[:, ht * DKV:(ht + 1) * DKV],
                               start=(ht == 0), stop=False),
                           sem=False, after=[("qA", cd), ("qS", pb.cnt["qS"]), ("act", pb.cnt["act"])] if ht == 0 else [])
                pb.ins("pe", lambda e: e.matmul(qps[:, 0:DKV], e0[:], bkT[:, 0:DKV], start=False, stop=True))
                k_pe = pb.cnt["pe"]
                pb.ins("act", lambda e: e.copy(qn_nat[:, 0:DKV], qps[:, 0:DKV]),
                       after=[("pe", k_pe), ("dve", pb.cnt["dve"]), ("act", pb.cnt["act"])])
                pb.ins("act", lambda e: e.activation(rop[:, 0:DKV], qn_nat[:, 0:DKV], AF.Square),
                       after=[("act", pb.cnt["act"]), ("dve", pb.cnt["dve"])])
                pb.ins("dve", lambda e: e.reduce_sum(rst[:, 0:NKV].unsqueeze(2),
                                                     rop[:, 0:DKV].rearrange("p (h d) -> p h d", d=128), axis=AX.X),
                       after=[("act", pb.cnt["act"]), ("dve", pb.cnt["dve"])])
                pb.ins("act", lambda e: e.activation(rst[:, 4:4 + NKV], rst[:, 0:NKV], AF.Sqrt,
                                                     bias=RMS_EPS, scale=1.0 / DK),
                       after=[("dve", pb.cnt["dve"]), ("act", pb.cnt["act"])])
                pb.ins("dve", lambda e: e.reciprocal(rst[:, 0:NKV], rst[:, 4:4 + NKV]),
                       after=[("act", pb.cnt["act"]), ("dve", pb.cnt["dve"])])
                pb.ins("dve", lambda e: e.tensor_tensor(
                           qn_nat[:, 0:DKV].rearrange("p (h d) -> p h d", d=128),
                           qn_nat[:, 0:DKV].rearrange("p (h d) -> p h d", d=128),
                           rst[:, 0:NKV].unsqueeze(2).broadcast_to([128, NKV, 128]), ALU.mult),
                       after=[("dve", pb.cnt["dve"])])
                pb.ins("dve", lambda e: e.tensor_tensor(qn_nat[:, 0:DKV], qn_nat[:, 0:DKV], kn4[:, 0:DKV], ALU.mult),
                       after=[("dve", pb.cnt["dve"])])
                q1 = qn_nat[:, 0:DKV:2]
                q2 = qn_nat[:, 1:DKV:2]
                hw = DKV // 2
                pb.ins("dve", lambda e, q1=q1, hw=hw: e.tensor_tensor(rop[:, 0:hw], q1, cos4[:, 0:hw], ALU.mult),
                       after=[("qA", cs), ("dve", pb.cnt["dve"])])
                pb.ins("dve", lambda e, q2=q2, hw=hw: e.tensor_tensor(rop[:, hw:2 * hw], q2, sin4[:, 0:hw], ALU.mult))
                pb.ins("dve", lambda e, q1=q1, hw=hw: e.tensor_tensor(rop[:, 2 * hw:3 * hw], q1, sin4[:, 0:hw], ALU.mult))
                pb.ins("dve", lambda e, q2=q2, hw=hw: e.tensor_tensor(rop[:, 3 * hw:4 * hw], q2, cos4[:, 0:hw], ALU.mult),
                       after=[("dve", pb.cnt["dve"])])
                pb.ins("dve", lambda e, hw=hw: e.tensor_tensor(qrot[:, 0:DKV:2], rop[:, 0:hw], rop[:, hw:2 * hw], ALU.subtract),
                       after=[("dve", pb.cnt["dve"]), ("pe", qrot_pe)])
                pb.ins("dve", lambda e, hw=hw: e.tensor_tensor(qrot[:, 1:DKV:2], rop[:, 2 * hw:3 * hw], rop[:, 3 * hw:4 * hw], ALU.add),
                       after=[("dve", pb.cnt["dve"])])
                rope_done = pb.cnt["dve"]
                for kv in range(NKV):
                    tp = (i * NKV + kv) % 2
                    qrot_pe = pb.ins("pe", lambda e, kv=kv, tp=tp: e.transpose(trp[tp][:], qrot[:, kv * 128:(kv + 1) * 128], ident[:]),
                                     after=[("dve", rope_done), ("dve", tr_evac[tp])])
                    tr_evac[tp] = pb.ins("dve", lambda e, kv=kv, i=i, tp=tp: e.tensor_copy(
                                             kT[:, kv * T + i * 128: kv * T + i * 128 + 128], trp[tp][:]),
                                         after=[("pe", pb.cnt["pe"]), ("dve", pb.cnt["dve"])])
                kv_dve = pb.cnt["dve"]
                for ht in range(nHT):
                    pb.ins("pe", lambda e, ht=ht, b=bp: e.matmul(qps[:, 0:DKV],
                               xas[b][:, ht * 128:(ht + 1) * 128],
                               wvT[:, ht * DKV:(ht + 1) * DKV],
                               start=(ht == 0), stop=False),
                           sem=False, after=[("act", pb.cnt["act"])] if ht == 0 else [])
                xa_free[bp] = pb.ins("pe", lambda e: e.matmul(qps[:, 0:DKV], e0[:], bvT[:, 0:DKV], start=False, stop=True))
                pb.ins("act", lambda e, i=i: e.copy(vsb[:, i * DKV:(i + 1) * DKV], qps[:, 0:DKV]),
                       after=[("pe", pb.cnt["pe"]), ("act", pb.cnt["act"])])
            pb.drain_queues()
            pb.flush()

        with ExitStack() as sta:
            sba = lambda n, s, d: sta.enter_context(nc.sbuf_tensor(n, s, d))
            wqTs = sba("a_wq", [128, nHT * 512], BF16)
            xas = [sba(f"a_x3{i}", [128, nHT * 128], BF16) for i in range(2)]
            qn_nat = [sba(f"a_qn3{i}", [128, 512], F32) for i in range(2)]
            qrot = [sba(f"a_qr3{i}", [128, 512], BF16) for i in range(2)]
            qstg = [sba(f"a_qs3{i}", [128, 512], BF16) for i in range(2)]
            rst = [sba(f"a_rs3{i}", [128, 40], F32) for i in range(2)]
            rop = [sba(f"a_ro3{i}", [128, 512], F32) for i in range(2)]
            cos4 = [sba(f"a_co3{i}", [128, 256], F32) for i in range(2)]
            sin4 = [sba(f"a_si3{i}", [128, 256], F32) for i in range(2)]
            trp = [sta.enter_context(nc.psum_tensor(f"a_tr3{i}", [128, 128], BF16)) for i in range(2)]
            qps = [sta.enter_context(nc.psum_tensor(f"a_qp3{i}", [128, 512], F32)) for i in range(2)]

            tr_evac = {0: 0, 1: 0}
            qrot_pe = {0: 0, 1: 0}
            xa_free = {0: 0, 1: 0}
            qps_act = {0: 0, 1: 0}
            qstg_written = {0: 0, 1: 0}

            def a3a_mms(ch, bt, ib, wq_c):
                xsrc = xaTd.rearrange("(ht p) t -> p ht t", p=128)[:, :, bt * 128:(bt + 1) * 128]
                cdx = pb.dma("qA", xas[ib][:].rearrange("p (ht n) -> p ht n", n=128), xsrc,
                             after=[("pe", xa_free[ib])])
                pb.dma("qA", cos4[ib][:], cos_in[:, bt * 256:(bt + 1) * 256],
                       after=[("dve", qrot_pe.get("c" + str(ib), 0))], bound=False)
                pb.dma("qA", sin4[ib][:], sin_in[:, bt * 256:(bt + 1) * 256], bound=False)
                cs = pb.cnt["qA"]
                for ht in range(nHT):
                    pb.ins("pe", lambda e, ht=ht, ib=ib: e.matmul(
                               qps[ib][:],
                               xas[ib][:, ht * 128:(ht + 1) * 128],
                               wqTs[:, ht * 512:(ht + 1) * 512],
                               start=(ht == 0), stop=False),
                           sem=False,
                           after=[("qS", wq_c), ("qA", cdx), ("act", qps_act[ib])] if ht == 0 else [])
                pb.ins("pe", lambda e, ch=ch, ib=ib: e.matmul(qps[ib][:], e0[:], bqT[:, ch * 512:(ch + 1) * 512], start=False, stop=True))
                xa_free[ib] = pb.cnt["pe"]
                return pb.cnt["pe"], cs

            def a3a_chain(ch, bt, ib, q_pe, cs):
                pb.ins("act", lambda e, ib=ib: e.copy(qn_nat[ib][:], qps[ib][:]),
                       after=[("pe", q_pe), ("dve", pb.cnt["dve"]), ("act", pb.cnt["act"])])
                qps_act[ib] = pb.cnt["act"]
                pb.ins("act", lambda e, ib=ib: e.activation(rop[ib][:], qn_nat[ib][:], AF.Square),
                       after=[("act", pb.cnt["act"]), ("dve", pb.cnt["dve"])])
                pb.ins("dve", lambda e, ib=ib: e.reduce_sum(rst[ib][:, 0:4].unsqueeze(2),
                                                            rop[ib][:].rearrange("p (h d) -> p h d", d=128), axis=AX.X),
                       after=[("act", pb.cnt["act"]), ("dve", pb.cnt["dve"])])
                pb.ins("act", lambda e, ib=ib: e.activation(rst[ib][:, 4:8], rst[ib][:, 0:4], AF.Sqrt,
                                                            bias=RMS_EPS, scale=1.0 / DK),
                       after=[("dve", pb.cnt["dve"]), ("act", pb.cnt["act"])])
                pb.ins("dve", lambda e, ib=ib: e.reciprocal(rst[ib][:, 0:4], rst[ib][:, 4:8]),
                       after=[("act", pb.cnt["act"]), ("dve", pb.cnt["dve"])])
                pb.ins("dve", lambda e, ib=ib: e.tensor_tensor(
                           qn_nat[ib][:].rearrange("p (h d) -> p h d", d=128),
                           qn_nat[ib][:].rearrange("p (h d) -> p h d", d=128),
                           rst[ib][:, 0:4].unsqueeze(2).broadcast_to([128, 4, 128]), ALU.mult),
                       after=[("dve", pb.cnt["dve"])])
                pb.ins("dve", lambda e, ib=ib: e.tensor_tensor(qn_nat[ib][:], qn_nat[ib][:], qn4[:], ALU.mult),
                       after=[("dve", pb.cnt["dve"])])
                q1 = qn_nat[ib][:, 0:512:2]
                q2 = qn_nat[ib][:, 1:512:2]
                pb.ins("dve", lambda e, q1=q1, ib=ib: e.tensor_tensor(rop[ib][:, 0:256], q1, cos4[ib][:], ALU.mult),
                       after=[("qA", cs), ("dve", pb.cnt["dve"])])
                pb.ins("dve", lambda e, q2=q2, ib=ib: e.tensor_tensor(rop[ib][:, 256:512], q2, sin4[ib][:], ALU.mult))
                pb.ins("dve", lambda e, ib=ib: e.tensor_tensor(rop[ib][:, 0:256], rop[ib][:, 0:256], rop[ib][:, 256:512], ALU.subtract),
                       after=[("dve", pb.cnt["dve"]), ("pe", qrot_pe[ib])])
                pb.ins("dve", lambda e, ib=ib: e.tensor_copy(qrot[ib][:, 0:512:2], rop[ib][:, 0:256]),
                       after=[("dve", pb.cnt["dve"])])
                pb.ins("dve", lambda e, q1=q1, ib=ib: e.tensor_tensor(rop[ib][:, 0:256], q1, sin4[ib][:], ALU.mult),
                       after=[("dve", pb.cnt["dve"])])
                pb.ins("dve", lambda e, q2=q2, ib=ib: e.tensor_tensor(rop[ib][:, 256:512], q2, cos4[ib][:], ALU.mult))
                pb.ins("dve", lambda e, ib=ib: e.tensor_tensor(qrot[ib][:, 1:512:2], rop[ib][:, 0:256], rop[ib][:, 256:512], ALU.add),
                       after=[("dve", pb.cnt["dve"])])
                rope_done = pb.cnt["dve"]
                qrot_pe["c" + str(ib)] = rope_done
                for h4 in range(4):
                    tp = h4 % 2
                    qrot_pe[ib] = pb.ins("pe", lambda e, h4=h4, tp=tp, ib=ib: e.transpose(
                                             trp[tp][:], qrot[ib][:, h4 * 128:(h4 + 1) * 128], ident[:]),
                                         after=[("dve", rope_done), ("act", tr_evac[tp])])
                    tr_evac[tp] = pb.ins("act", lambda e, h4=h4, tp=tp, ib=ib: e.copy(
                                             qstg[ib][:, h4 * 128:(h4 + 1) * 128], trp[tp][:]),
                                         after=[("pe", pb.cnt["pe"]), ("act", pb.cnt["act"]),
                                                ("qA", qstg_written[ib])])
                qstg_written[ib] = pb.dma("qA", qTd[bt * 128:(bt + 1) * 128, ch * 512:(ch + 1) * 512], qstg[ib][:],
                                          after=[("act", tr_evac[0]), ("act", tr_evac[1])])

            for ch in range(NH // 4):
                rows = nHT * 512
                pb.dma("qS", wqTs[:], Wq_nat[ch * rows:(ch + 1) * rows, :],
                       transpose=True, after=[("pe", pb.cnt["pe"])])
                wq_c = pb.cnt["qS"]
                pend = None
                for bt in range(cfg.halo // 128, nT):
                    ib = bt % 2
                    cur = (ch, bt, ib) + a3a_mms(ch, bt, ib, wq_c)
                    if pend is not None:
                        a3a_chain(pend[0], pend[1], pend[2], pend[3], pend[4])
                    pend = cur
                a3a_chain(pend[0], pend[1], pend[2], pend[3], pend[4])
            pb.drain_queues()
            pb.flush()

        with ExitStack() as stb:
            sbb = lambda n, s, d: stb.enter_context(nc.sbuf_tensor(n, s, d))
            qTt = [sbb(f"a_qT3{i}", [128, NH * 128], BF16) for i in range(2)]
            aoT = sbb("a_aoT", [128, NH * 128], BF16)
            sm1 = [sbb(f"a_sm1{i}", [128, SB * 256], F32) for i in range(2)]
            smE = [sbb(f"a_smE{i}", [128, SB * 256], F32) for i in range(2)]
            smA = [sbb(f"a_smA{i}", [128, SB * 256], BF16) for i in range(2)]
            rst = [sbb(f"a_rsb{i}", [128, 40], F32) for i in range(2)]
            atc = [sbb(f"a_at{i}", [128, 128], BF16) for i in range(2)]
            trp = [stb.enter_context(nc.psum_tensor(f"a_trb{i}", [128, 128], BF16)) for i in range(2)]
            sps = [stb.enter_context(nc.psum_tensor(f"a_sp3{i}", [128, SB * 256], F32)) for i in range(2)]
            ops_ = [stb.enter_context(nc.psum_tensor(f"a_op{i}", [128, 128], F32)) for i in range(2)]

            tr_evac = {0: 0, 1: 0}
            smA_pe = {0: 0, 1: 0}
            sps_dve = {0: 0, 1: 0}
            ops_act = {0: 0, 1: 0}
            ao_written = 0
            qT_free = {0: 0, 1: 0}
            qT_loaded = {0: 0, 1: 0}

            def a3b_smm(bt, sb_, ib):
                if sb_ == 0:
                    qT_loaded[bt % 2] = pb.dma("qS", qTt[bt % 2][:], qTd[bt * 128:(bt + 1) * 128, :],
                                               after=[("pe", qT_free[bt % 2])])
                w0 = (bt - 1) * 128
                for h8 in range(SB):
                    h = sb_ * SB + h8
                    kv = h // (NH // NKV)
                    pb.ins("pe", lambda e, h=h, kv=kv, h8=h8, ib=ib, w0=w0, tb=bt % 2: e.matmul(
                               sps[ib][:, h8 * 256:(h8 + 1) * 256], qTt[tb][:, h * 128:(h + 1) * 128],
                               kT[:, kv * T + w0: kv * T + w0 + 256], start=True, stop=True),
                           sem=(False if h8 < SB - 1 else "pe"),
                           after=[("qS", qT_loaded[bt % 2]), ("dve", sps_dve[ib])] if h8 == 0 else [])
                qT_free[bt % 2] = pb.cnt["pe"]
                return pb.cnt["pe"]

            def a3b_chain_av(bt, sb_, ib, s_pe):
                m01 = m01b if bt == cfg.halo // 128 else m01a
                fil = filb if bt == cfg.halo // 128 else fila
                sm1v = sm1[ib][:].rearrange("p (h k) -> p h k", k=256)
                smEv = smE[ib][:].rearrange("p (h k) -> p h k", k=256)
                smAv = smA[ib][:].rearrange("p (h k) -> p h k", k=256)
                pb.ins("dve", lambda e, m01=m01, ib=ib, sm1v=sm1v: e.scalar_tensor_tensor(
                           sm1v, sps[ib][:].rearrange("p (h k) -> p h k", k=256), scaleb[:, 0:1],
                           m01[:].unsqueeze(1).broadcast_to([128, SB, 256]), ALU.mult, ALU.mult),
                       after=[("pe", s_pe), ("dve", pb.cnt["dve"]), ("act", pb.cnt["act"])])
                sps_dve[ib] = pb.cnt["dve"]
                pb.ins("dve", lambda e, fil=fil, sm1v=sm1v: e.tensor_tensor(
                           sm1v, sm1v, fil[:].unsqueeze(1).broadcast_to([128, SB, 256]), ALU.add),
                       after=[("dve", pb.cnt["dve"])])
                pb.ins("dve", lambda e, ib=ib, sm1v=sm1v: e.reduce_max(rst[ib][:, 8:8 + SB].unsqueeze(2), sm1v, axis=AX.X, negate=True),
                       after=[("dve", pb.cnt["dve"])])
                pb.ins("dve", lambda e, ib=ib, sm1v=sm1v: e.tensor_tensor(
                           sm1v, sm1v, rst[ib][:, 8:8 + SB].unsqueeze(2).broadcast_to([128, SB, 256]), ALU.add),
                       after=[("dve", pb.cnt["dve"])])
                pb.ins("act", lambda e, ib=ib: e.activation(smE[ib][:], sm1[ib][:], AF.Exp),
                       after=[("dve", pb.cnt["dve"]), ("act", pb.cnt["act"])])
                pb.ins("dve", lambda e, ib=ib, smEv=smEv: e.reduce_sum(rst[ib][:, 16:16 + SB].unsqueeze(2), smEv, axis=AX.X),
                       after=[("act", pb.cnt["act"]), ("dve", pb.cnt["dve"])])
                pb.ins("dve", lambda e, ib=ib: e.reciprocal(rst[ib][:, 24:24 + SB], rst[ib][:, 16:16 + SB]),
                       after=[("dve", pb.cnt["dve"])])
                pb.ins("dve", lambda e, ib=ib, smAv=smAv, smEv=smEv: e.tensor_tensor(
                           smAv, smEv, rst[ib][:, 24:24 + SB].unsqueeze(2).broadcast_to([128, SB, 256]), ALU.mult),
                       after=[("dve", pb.cnt["dve"]), ("pe", smA_pe[ib])])
                A_done = pb.cnt["dve"]
                for h8 in range(SB):
                    h = sb_ * SB + h8
                    kv = h // (NH // NKV)
                    for half in range(2):
                        tp = half
                        smA_pe[ib] = pb.ins("pe", lambda e, h8=h8, half=half, tp=tp, ib=ib: e.transpose(
                                                trp[tp][:], smA[ib][:, h8 * 256 + half * 128: h8 * 256 + (half + 1) * 128], ident[:]),
                                            after=[("dve", A_done), ("act", tr_evac[tp])])
                        tr_evac[tp] = pb.ins("act", lambda e, half=half, tp=tp: e.copy(atc[half][:], trp[tp][:]),
                                             after=[("pe", pb.cnt["pe"]), ("act", pb.cnt["act"])])
                    at_done = pb.cnt["act"]
                    op = h % 2
                    for half in range(2):
                        blk_ = bt - 1 + half
                        pb.ins("pe", lambda e, half=half, blk_=blk_, kv=kv, op=op: e.matmul(
                                   ops_[op][:], vsb[:, blk_ * DKV + kv * 128: blk_ * DKV + kv * 128 + 128],
                                   atc[half][:], start=(half == 0), stop=(half == 1)),
                               sem=(False if half == 0 else "pe"),
                               after=[("act", at_done), ("act", ops_act[op])] if half == 0 else [])
                    ops_act[op] = pb.ins("act", lambda e, h=h, op=op: e.copy(aoT[:, h * 128:(h + 1) * 128], ops_[op][:]),
                                         after=[("pe", pb.cnt["pe"]), ("act", pb.cnt["act"]),
                                                ("qA", ao_written if h == 0 else 0)])

            batches = [(bt, sb_) for bt in range(cfg.halo // 128, nT) for sb_ in range(n_sb)]
            pend = None
            for idx, (bt, sb_) in enumerate(batches):
                ib = idx % 2
                s_pe = a3b_smm(bt, sb_, ib)
                if pend is not None:
                    a3b_chain_av(*pend)
                    if pend[1] == n_sb - 1:
                        ao_written = pb.dma("qA", aoTd[pend[0] * 128:(pend[0] + 1) * 128, :], aoT[:],
                                            after=[("act", pb.cnt["act"])])
                pend = (bt, sb_, ib, s_pe)
            a3b_chain_av(*pend)
            ao_written = pb.dma("qA", aoTd[pend[0] * 128:(pend[0] + 1) * 128, :], aoT[:],
                                after=[("act", pb.cnt["act"])])
            pb.drain_queues()
            pb.flush()

        with ExitStack() as stc:
            sbc = lambda n, s, d: stc.enter_context(nc.sbuf_tensor(n, s, d))
            woS = sbc("a_wo", [128, nHT * 512], BF16)
            aoc = [sbc(f"a_ao{i}", [128, NH * 128], BF16) for i in range(2)]
            rres = [sbc(f"a_rr{i}", [128, 512], BF16) for i in range(2)]
            rout = [sbc(f"a_rx{i}", [128, 512], BF16) for i in range(2)]
            pps = [stc.enter_context(nc.psum_tensor(f"a_pp{i}", [128, 512], F32)) for i in range(2)]

            ao_free = {0: 0, 1: 0}
            rr_free = {0: 0, 1: 0}
            ro_free = {0: 0, 1: 0}
            pps_dve = {0: 0, 1: 0}
            for hc in range(D // 512):
                rows = nHT * 512
                pb.dma("qS", woS[:], Wo_nat[hc * rows:(hc + 1) * rows, :],
                       transpose=True, after=[("pe", pb.cnt["pe"])])
                wo_c = pb.cnt["qS"]
                for b in range(cfg.halo // 128, nT):
                    bp = b % 2
                    r = b * 128
                    cda = pb.dma("qS", aoc[bp][:], aoTd[r:r + 128, :], after=[("pe", ao_free[bp])])
                    cd = pb.dma("qA", rres[bp][:], R1[r:r + 128, hc * 512:(hc + 1) * 512],
                                after=[("dve", rr_free[bp])])
                    pp = bp
                    for dot in range(nHT):
                        pb.ins("pe", lambda e, dot=dot, pp=pp, bp=bp: e.matmul(
                                   pps[pp][:], aoc[bp][:, dot * 128:(dot + 1) * 128],
                                   woS[:, dot * 512:(dot + 1) * 512],
                                   start=(dot == 0), stop=False),
                               sem=False,
                               after=[("qS", max(wo_c, cda)), ("dve", pps_dve[pp])] if dot == 0 else [])
                    pb.ins("pe", lambda e, hc=hc, pp=pp: e.matmul(pps[pp][:], e0[:], boT[:, hc * 512:(hc + 1) * 512],
                                                                  start=False, stop=True))
                    ao_free[bp] = pb.cnt["pe"]
                    rr_free[bp] = pps_dve[pp] = pb.ins(
                        "dve", lambda e, pp=pp, bp=bp: e.scalar_tensor_tensor(
                            rout[bp][:], pps[pp][:], 1.0, rres[bp][:], ALU.mult, ALU.add),
                        after=[("pe", pb.cnt["pe"]), ("qA", cd), ("dve", pb.cnt["dve"]), ("qA", ro_free[bp])])
                    ro_free[bp] = pb.dma("qA", R2[r:r + 128, hc * 512:(hc + 1) * 512], rout[bp][:],
                                         after=[("dve", pb.cnt["dve"])])
            pb.drain_queues()
            pb.flush()



def _host_inputs(cfg, core, inputs, n_cores=8):
    import ml_dtypes
    D, T, NH, NKV, DK = cfg.D, cfg.T, cfg.NH, cfg.NKV, cfg.DK
    x = inputs["L0_attn_x"]
    B, S, _ = x.shape
    per_seq = (n_cores // B)
    seq = core // per_seq
    start = (core % per_seq) * cfg.T_own
    lo = start - cfg.halo
    if lo < 0:
        pad = np.zeros((-lo, D), dtype=np.float32)
        xs = np.concatenate([pad, x[seq, 0:start + cfg.T_own]], axis=0)
    else:
        xs = x[seq, lo:start + cfg.T_own]
    xs = np.ascontiguousarray(xs, dtype=np.float32)

    i_ = np.arange(128)[:, None]
    j_ = np.arange(256)[None, :]
    okA = (j_ - i_ >= 64) & (j_ - i_ <= 128)
    okB = okA & (j_ >= 128)
    m01a = okA.astype(np.float32)
    fila = np.where(okA, 0.0, MASK_FILL).astype(np.float32)
    first = (start == 0)
    m01b = (okB if first else okA).astype(np.float32)
    filb = np.where(okB if first else okA, 0.0, MASK_FILL).astype(np.float32)

    pos = np.arange(lo, start + cfg.T_own, dtype=np.int64)
    pos = np.maximum(pos, 0).astype(np.float32)
    inv_freq = (1.0 / (ROPE_BASE ** (np.arange(0, DK, 2, dtype=np.float32) / DK))).astype(np.float32)
    freqs = (pos[:, None] * inv_freq[None, :]).astype(np.float32)
    cost = np.cos(freqs).astype(np.float32)
    sint = np.sin(freqs).astype(np.float32)
    nTt = cfg.nT
    cost = np.tile(cost.reshape(nTt, 128, 1, 64), (1, 1, 4, 1)).transpose(1, 0, 2, 3).reshape(128, nTt * 256)
    sint = np.tile(sint.reshape(nTt, 128, 1, 64), (1, 1, 4, 1)).transpose(1, 0, 2, 3).reshape(128, nTt * 256)

    qn4 = np.broadcast_to(np.tile(inputs["L0_attn_qn_w"], 4)[None, :], (128, 512))
    kn4 = np.broadcast_to(np.tile(inputs["L0_attn_kn_w"], NKV)[None, :], (128, NKV * DK))
    scl = np.broadcast_to(np.asarray(inputs["L0_attn_scale"], dtype=np.float32).reshape(1, 1), (128, 1))

    e0 = np.zeros((128, 128), dtype=np.float32)
    e0[0, :] = 1.0

    return {
        "x": xs,
        "gW": inputs["L0_ffn_gate_W_packed"], "gS": inputs["L0_ffn_gate_scales"],
        "uW": inputs["L0_ffn_up_W_packed"], "uS": inputs["L0_ffn_up_scales"],
        "dW": inputs["L0_ffn_down_W_packed"], "dS": inputs["L0_ffn_down_scales"],
        "Wq": inputs["L0_attn_Wq"], "Wk": inputs["L0_attn_Wk"],
        "Wv": inputs["L0_attn_Wv"], "Wo": inputs["L0_attn_Wo"],
        "bq": inputs["L0_attn_bq"].reshape(1, -1), "bk": inputs["L0_attn_bk"].reshape(1, -1),
        "bv": inputs["L0_attn_bv"].reshape(1, -1), "bo": inputs["L0_attn_bo"].reshape(1, -1),
        "qn4": np.ascontiguousarray(qn4, dtype=np.float32),
        "kn4": np.ascontiguousarray(kn4, dtype=np.float32),
        "scl": np.ascontiguousarray(scl, dtype=np.float32),
        "ident": np.eye(128, dtype=np.float32).astype(ml_dtypes.bfloat16),
        "e0": e0.astype(ml_dtypes.bfloat16),
        "m01a": m01a, "fila": fila, "m01b": m01b, "filb": filb,
        "cost": np.ascontiguousarray(cost), "sint": np.ascontiguousarray(sint),
    }


def kernel(**inputs):
    cfg = CFG
    n_cores = 8
    in_maps = [_host_inputs(cfg, c, inputs) for c in range(n_cores)]
    nc = build_program(cfg, None)
    res = run_bass_kernel_spmd(nc, in_maps, core_ids=list(range(n_cores)))
    x = inputs["L0_attn_x"]
    B, S, D = x.shape
    out = np.empty((B, S, D), dtype=np.float32)
    per_seq = n_cores // B
    for c in range(n_cores):
        seq = c // per_seq
        start = (c % per_seq) * cfg.T_own
        out[seq, start:start + cfg.T_own] = res.results[c]["out"]
    return out


# version 23
# speedup vs baseline: 1.0912x; 1.0912x over previous
import numpy as np
from contextlib import ExitStack
import concourse.bass as bass
import concourse.mybir as mybir
from concourse.bass_utils import run_bass_kernel_spmd

F32 = mybir.dt.float32
BF16 = mybir.dt.bfloat16
I32 = mybir.dt.int32
AF = mybir.ActivationFunctionType
ALU = mybir.AluOpType
AX = mybir.AxisListType

GELU_C1 = 0.7978845608028654
GELU_C3 = 0.7978845608028654 * 0.044715
GELU_K = GELU_C1 / GELU_C3

LN_EPS = 1e-5
RMS_EPS = 1e-8
MASK_FILL = -10000.0
ROPE_BASE = 10000.0


class Cfg:
    def __init__(self, D=2048, NMLP=16384, NH=16, NKV=4, T_own=2048, halo=128,
                 n_super=256, G_super=4):
        self.D = D
        self.NMLP = NMLP
        self.NH = NH
        self.NKV = NKV
        self.DK = 128
        self.DKV = NKV * self.DK
        self.T_own = T_own
        self.halo = halo
        self.T = T_own + halo
        self.nT = self.T // 128
        self.nHT = D // 128
        self.nNT = NMLP // 128
        self.n_super = n_super
        self.G = G_super * n_super
        self.ffn1_tc = self._chunks(0, self.T)
        self.ffn2_tc = self._chunks(halo, T_own)

    def _chunks(self, start, total):
        out = []
        pos = start
        left = total
        while left > 0:
            c = min(768, left)
            out.append((pos, c))
            pos += c
            left -= c
        return out

    @staticmethod
    def pieces(length):
        n = (length + 511) // 512
        assert length % n == 0
        out = []
        pos = 0
        for i in range(n):
            ln = length // n
            out.append((pos, ln))
            pos += ln
        return out


CFG = Cfg()



ENG_OF_Q = {"qS": "sp", "qA": "act", "qG": "pool"}


class PB:

    def __init__(self, nc):
        self.nc = nc
        self.ctx = ExitStack()
        self.sems = {}
        self.cnt = {}
        self.pend = {k: [] for k in ("pe", "act", "dve", "pool", "sp")}
        self.lastw = {}
        for k in ("pe", "act", "dve", "pool", "qS", "qA", "qG"):
            self.sems[k] = self.ctx.enter_context(nc.semaphore(k))
            self.cnt[k] = 0

    def ins(self, eng, make, sem=None, after=()):
        for s, v in after:
            self.wait(eng, s, v)
        if sem is None:
            sem = eng
        if sem is False:
            self.pend[eng].append(lambda e, make=make: make(e))
            return None
        def f(e, make=make, sem=sem):
            make(e).then_inc(self.sems[sem], 1)
        self.pend[eng].append(f)
        self.cnt[sem] += 1
        return self.cnt[sem]

    def raw(self, eng, make, after=()):
        for s, v in after:
            self.wait(eng, s, v)
        self.pend[eng].append(lambda e, make=make: make(e))

    def dma(self, q, out_ap, in_ap, after=(), bound=True, **kw):
        eng = ENG_OF_Q[q]
        if bound:
            self.wait(eng, q, self.cnt[q])
        for s, v in after:
            self.wait(eng, s, v)
        def f(e, o=out_ap, i=in_ap, kw=kw, q=q):
            e.dma_start(o, i, **kw).then_inc(self.sems[q], 16)
        self.pend[eng].append(f)
        self.cnt[q] += 16
        return self.cnt[q]

    def wait(self, eng, sem, val, exact=False):
        if val is None or val <= 0:
            return
        if sem in ("qS", "qA", "qG") and not exact:
            val = self.cnt[sem]
        key = (eng, sem)
        if self.lastw.get(key, 0) >= val:
            return
        self.lastw[key] = val
        def f(e, s=sem, v=val):
            e.wait_ge(self.sems[s], v)
        self.pend[eng].append(f)

    def flush(self):
        pend, self.pend = self.pend, {k: [] for k in self.pend}
        if not any(pend.values()):
            return
        with self.nc.Block() as blk:
            if pend["sp"]:
                @blk.sync
                def _(e, ops=pend["sp"]):
                    for f in ops:
                        f(e)
            if pend["pe"]:
                @blk.tensor
                def _(e, ops=pend["pe"]):
                    for f in ops:
                        f(e)
            if pend["act"]:
                @blk.scalar
                def _(e, ops=pend["act"]):
                    for f in ops:
                        f(e)
            if pend["dve"]:
                @blk.vector
                def _(e, ops=pend["dve"]):
                    for f in ops:
                        f(e)
            if pend["pool"]:
                @blk.gpsimd
                def _(e, ops=pend["pool"]):
                    for f in ops:
                        f(e)
        for s in ("pe", "act", "dve", "pool"):
            for e2 in ("pe", "act", "dve", "pool", "sp"):
                self.lastw[(e2, s)] = self.cnt[s]

    def drain_queues(self):
        for q in ("qS", "qA", "qG"):
            self.wait(ENG_OF_Q[q], q, self.cnt[q])

    def sb(self, name, shape, dtype):
        return self.ctx.enter_context(self.nc.sbuf_tensor(name, shape, dtype))

    def ps(self, name, shape, dtype=F32):
        return self.ctx.enter_context(self.nc.psum_tensor(name, shape, dtype))




def build_program(cfg, first_tile_masked):
    nc = bass.Bass("TRN2", target_bir_lowering=False)
    D, NMLP, T = cfg.D, cfg.NMLP, cfg.T
    nHT, nNT, nT = cfg.nHT, cfg.nNT, cfg.nT
    NH, NKV, DK, DKV = cfg.NH, cfg.NKV, cfg.DK, cfg.DKV

    x_in = nc.dram_tensor("x", [T, D], F32, kind="ExternalInput").ap()
    gW = nc.dram_tensor("gW", [NMLP, D // 2], I32, kind="ExternalInput").ap()
    gS = nc.dram_tensor("gS", [NMLP, D // 32], F32, kind="ExternalInput").ap()
    uW = nc.dram_tensor("uW", [NMLP, D // 2], I32, kind="ExternalInput").ap()
    uS = nc.dram_tensor("uS", [NMLP, D // 32], F32, kind="ExternalInput").ap()
    dW = nc.dram_tensor("dW", [D, NMLP // 2], I32, kind="ExternalInput").ap()
    dS = nc.dram_tensor("dS", [D, NMLP // 32], F32, kind="ExternalInput").ap()
    Wq = nc.dram_tensor("Wq", [D, D], F32, kind="ExternalInput").ap()
    Wk = nc.dram_tensor("Wk", [DKV, D], F32, kind="ExternalInput").ap()
    Wv = nc.dram_tensor("Wv", [DKV, D], F32, kind="ExternalInput").ap()
    Wo = nc.dram_tensor("Wo", [D, D], F32, kind="ExternalInput").ap()
    bq_in = nc.dram_tensor("bq", [1, D], F32, kind="ExternalInput").ap()
    bk_in = nc.dram_tensor("bk", [1, DKV], F32, kind="ExternalInput").ap()
    bv_in = nc.dram_tensor("bv", [1, DKV], F32, kind="ExternalInput").ap()
    bo_in = nc.dram_tensor("bo", [1, D], F32, kind="ExternalInput").ap()
    qn4_in = nc.dram_tensor("qn4", [128, 512], F32, kind="ExternalInput").ap()
    kn4_in = nc.dram_tensor("kn4", [128, DKV], F32, kind="ExternalInput").ap()
    scale_in = nc.dram_tensor("scl", [128, 1], F32, kind="ExternalInput").ap()
    ident_in = nc.dram_tensor("ident", [128, 128], BF16, kind="ExternalInput").ap()
    e0_in = nc.dram_tensor("e0", [128, 128], BF16, kind="ExternalInput").ap()
    m01a_in = nc.dram_tensor("m01a", [128, 256], F32, kind="ExternalInput").ap()
    fila_in = nc.dram_tensor("fila", [128, 256], F32, kind="ExternalInput").ap()
    m01b_in = nc.dram_tensor("m01b", [128, 256], F32, kind="ExternalInput").ap()
    filb_in = nc.dram_tensor("filb", [128, 256], F32, kind="ExternalInput").ap()
    cos_in = nc.dram_tensor("cost", [128, nT * 256], F32, kind="ExternalInput").ap()
    sin_in = nc.dram_tensor("sint", [128, nT * 256], F32, kind="ExternalInput").ap()
    out = nc.dram_tensor("out", [cfg.T_own, D], F32, kind="ExternalOutput").ap()

    Wg_nat = nc.dram_tensor("WgN", [NMLP * (D // 128), 128], BF16, kind="Internal").ap()
    Wu_nat = nc.dram_tensor("WuN", [NMLP * (D // 128), 128], BF16, kind="Internal").ap()
    Wd_nat = nc.dram_tensor("WdN", [D, NMLP], BF16, kind="Internal").ap()
    Wq_nat = nc.dram_tensor("WqN", [D * (D // 128), 128], BF16, kind="Internal").ap()
    Wk_nat = nc.dram_tensor("WkN", [DKV, D], BF16, kind="Internal").ap()
    Wv_nat = nc.dram_tensor("WvN", [DKV, D], BF16, kind="Internal").ap()
    Wo_nat = nc.dram_tensor("WoN", [D * (D // 128), 128], BF16, kind="Internal").ap()
    xaTd = nc.dram_tensor("xaTd", [D, T], BF16, kind="Internal").ap()
    qTd = nc.dram_tensor("qTd", [T, NH * 128], BF16, kind="Internal").ap()
    aoTd = nc.dram_tensor("aoTd", [T, NH * 128], BF16, kind="Internal").ap()
    R1 = nc.dram_tensor("R1", [T, D], BF16, kind="Internal").ap()
    R2 = nc.dram_tensor("R2", [T, D], BF16, kind="Internal").ap()

    pb = PB(nc)

    ident = pb.sb("identS", [128, 128], BF16)
    e0 = pb.sb("e0S", [128, 128], BF16)
    scaleb = pb.sb("scaleS", [128, 1], F32)
    pb.ident = ident

    const_vals = sorted({LN_EPS, RMS_EPS, 1.0 / D, 1.0 / DK, GELU_C3, 1.0, 0.0})
    for ci, v in enumerate(const_vals):
        ct = pb.sb(f"cst{ci}", [128, 1], F32)
        pb.ins("pool", lambda e, t=ct, v=v: e.memset(t[:], float(v)), sem="pool")
        nc.const_aps.register(F32, float(v), ct[:, 0:1])

    pb.dma("qS", ident[:], ident_in)
    pb.dma("qS", e0[:], e0_in)
    pb.dma("qS", scaleb[:], scale_in)
    pb.drain_queues()
    pb.flush()

    dq = DequantDriver(pb, cfg, [
        (gW, gS, Wg_nat, NMLP, D, "g", True),
        (uW, uS, Wu_nat, NMLP, D, "u", True),
        (dW, dS, Wd_nat, D, NMLP, "d", False),
    ])
    dq.finish()
    pb.drain_queues()
    pb.flush()
    dq.ctx.close()

    emit_ffn_pass(pb, cfg, src=x_in, src_is_f32=True, dst=R1, dst_is_f32=False,
                  dst_row_off=0, tcs=cfg.ffn1_tc, WgT=Wg_nat, WuT=Wu_nat, WdT=Wd_nat, tag="f1")
    pb.flush()

    emit_attn(pb, cfg, R1=R1, R2=R2, Wq=Wq, Wk=Wk, Wv=Wv, Wo=Wo,
              Wq_nat=Wq_nat, Wk_nat=Wk_nat, Wv_nat=Wv_nat, Wo_nat=Wo_nat, xaTd=xaTd,
              qTd=qTd, aoTd=aoTd,
              bq_in=bq_in, bk_in=bk_in, bv_in=bv_in, bo_in=bo_in,
              qn4_in=qn4_in, kn4_in=kn4_in, scaleb=scaleb,
              m01a_in=m01a_in, fila_in=fila_in, m01b_in=m01b_in, filb_in=filb_in,
              cos_in=cos_in, sin_in=sin_in, e0=e0, ident=ident)
    pb.flush()

    emit_ffn_pass(pb, cfg, src=R2, src_is_f32=False, dst=out, dst_is_f32=True,
                  dst_row_off=-cfg.halo, tcs=cfg.ffn2_tc, WgT=Wg_nat, WuT=Wu_nat, WdT=Wd_nat, tag="f2")

    pb.drain_queues()
    pb.flush()
    pb.ctx.close()
    return nc


class DequantDriver:

    def __init__(self, pb, cfg, jobs):
        self.pb = pb
        self.cfg = cfg
        self.ctx = ExitStack()
        sb = lambda n, s, d: self.ctx.enter_context(pb.nc.sbuf_tensor(n, s, d))
        self.pk = [sb(f"dq_pk{i}", [128, 1024], I32) for i in range(2)]
        self.sc = [sb(f"dq_sc{i}", [128, 64], F32) for i in range(2)]
        self.blo = sb("dq_blo", [128, 1024], I32)
        self.btmp = sb("dq_btmp", [128, 1024], I32)
        self.wn = [sb(f"dq_wn{i}", [128, 2048], BF16) for i in range(2)]
        self.ld_done = {0: 0, 1: 0}
        self.w_written = {0: 0, 1: 0}
        self.step = 0
        self.jobs = []
        self.emitted = []
        self.covered = []
        for packed, scales, w_nat, N, K, tag, blocked in jobs:
            KCH = min(K, 2048)
            k_chunks = K // KCH
            n_tiles = N // 128
            if blocked:
                work = [(nt, kc) for nt in range(n_tiles) for kc in range(k_chunks)]
            else:
                work = [(nt, kc) for kc in range(k_chunks) for nt in range(n_tiles)]
            self.jobs.append(dict(packed=packed, scales=scales, w_nat=w_nat, D=K,
                                  KCH=KCH, words=KCH // 2, k_chunks=k_chunks,
                                  blocked=blocked, work=work))
            self.emitted.append(0)
            self.covered.append([])

    def ensure(self, ji, n):
        J = self.jobs[ji]
        n = min(n, len(J["work"]))
        while self.emitted[ji] < n:
            nt, kc = J["work"][self.emitted[ji]]
            self._tile(J, nt, kc)
            self.covered[ji].append(self.pb.cnt["qA"])
            self.emitted[ji] += 1

    def cover_count(self, ji, items):
        items = min(items, len(self.jobs[ji]["work"]))
        if items <= 0:
            return 0
        return self.covered[ji][items - 1]

    def finish(self):
        for ji in range(len(self.jobs)):
            self.ensure(ji, 10 ** 9)

    def _tile(self, J, nt, kc):
        pb = self.pb
        KCH, words = J["KCH"], J["words"]
        bp = self.step % 2
        self.step += 1
        row0 = nt * 128
        pk, sc, wn = self.pk[bp], self.sc[bp], self.wn[bp]
        blo, btmp = self.blo, self.btmp
        W = words
        pb.dma("qS", pk[:, 0:W], J["packed"][row0:row0 + 128, kc * W:(kc + 1) * W],
               after=[("dve", self.ld_done[bp])])
        pb.dma("qS", sc[:, 0:KCH // 32], J["scales"][row0:row0 + 128, kc * (KCH // 32):(kc + 1) * (KCH // 32)],
               bound=False)
        pb.ins("dve", lambda E, sc=sc, K2=KCH // 32: E.tensor_scalar(sc[:, 0:K2], sc[:, 0:K2], float(2.0 ** 126), None, ALU.mult),
               after=[("qS", pb.cnt["qS"]), ("dve", pb.cnt["dve"])])
        scv = sc[:, 0:KCH // 32].unsqueeze(2).broadcast_to([128, KCH // 32, 16])
        we = wn[:, 0:KCH:2].rearrange("p (a b) -> p a b", b=16)
        wo = wn[:, 1:KCH:2].rearrange("p (a b) -> p a b", b=16)
        pb.ins("dve", lambda E, pk=pk, blo=blo, W=W: E.tensor_scalar(blo[:, 0:W], pk[:, 0:W], 8, 28, ALU.bitwise_and, ALU.logical_shift_left),
               after=[("dve", pb.cnt["dve"])])
        pb.ins("dve", lambda E, pk=pk, btmp=btmp, W=W: E.tensor_scalar(btmp[:, 0:W], pk[:, 0:W], 7, 22, ALU.bitwise_and, ALU.logical_shift_left),
               after=[("dve", pb.cnt["dve"])])
        pb.ins("dve", lambda E, blo=blo, btmp=btmp, W=W: E.tensor_tensor(blo[:, 0:W], blo[:, 0:W], btmp[:, 0:W], ALU.bitwise_or),
               after=[("dve", pb.cnt["dve"])])
        pb.ins("dve", lambda E, we=we, scv=scv, blo=blo, W=W: E.tensor_tensor(we, blo[:, 0:W].bitcast(F32), scv, ALU.mult),
               after=[("dve", pb.cnt["dve"]), ("qA", self.w_written[bp])])
        pb.ins("dve", lambda E, pk=pk, blo=blo, W=W: E.tensor_scalar(blo[:, 0:W], pk[:, 0:W], 0x80, 24, ALU.bitwise_and, ALU.logical_shift_left),
               after=[("dve", pb.cnt["dve"])])
        pb.ins("dve", lambda E, pk=pk, btmp=btmp, W=W: E.tensor_scalar(btmp[:, 0:W], pk[:, 0:W], 0x70, 18, ALU.bitwise_and, ALU.logical_shift_left),
               after=[("dve", pb.cnt["dve"])])
        pb.ins("dve", lambda E, blo=blo, btmp=btmp, W=W: E.tensor_tensor(blo[:, 0:W], blo[:, 0:W], btmp[:, 0:W], ALU.bitwise_or),
               after=[("dve", pb.cnt["dve"])])
        self.ld_done[bp] = pb.ins("dve", lambda E, wo=wo, scv=scv, blo=blo, W=W: E.tensor_tensor(wo, blo[:, 0:W].bitcast(F32), scv, ALU.mult),
                                  after=[("dve", pb.cnt["dve"])])
        if J["blocked"]:
            NS = self.cfg.n_super
            nht_full = J["D"] // 128
            nht = KCH // 128
            sblk = row0 // NS
            o = row0 % NS
            dst = J["w_nat"].rearrange("(s ht n) c -> s ht n c", ht=nht_full, n=NS)[
                sblk, kc * nht:(kc + 1) * nht, o:o + 128, :]
            self.w_written[bp] = pb.dma("qA", dst.transpose([1, 0, 2]),
                                        wn[:, 0:KCH].rearrange("p (ht c) -> p ht c", c=128),
                                        after=[("dve", self.ld_done[bp])])
        else:
            self.w_written[bp] = pb.dma("qA", J["w_nat"][row0:row0 + 128, kc * KCH:(kc + 1) * KCH], wn[:, 0:KCH],
                                        after=[("dve", self.ld_done[bp])])


def emit_ffn_pass(pb, cfg, src, src_is_f32, dst, dst_is_f32, dst_row_off, tcs,
                  WgT, WuT, WdT, tag, dq=None):
    nc = pb.nc
    D, NMLP = cfg.D, cfg.NMLP
    nHT = cfg.nHT
    NS = cfg.n_super
    G = cfg.G
    n_gsup = G // NS
    nG = NMLP // G
    maxtc = max(l for _, l in tcs)

    with ExitStack() as st:
        sbl = lambda n, s, d: st.enter_context(nc.sbuf_tensor(n, s, d))
        psl = lambda n, s, d=F32: st.enter_context(nc.psum_tensor(n, s, d))
        xT = sbl(f"xT{tag}", [128, nHT * maxtc], BF16)
        yacc = sbl(f"y{tag}", [128, (maxtc // 128) * D], F32)
        inter = sbl(f"int{tag}", [128, (G // 128) * maxtc], BF16)
        wg = [sbl(f"wg{tag}{i}", [128, nHT * NS], BF16) for i in range(2)]
        wu = [sbl(f"wu{tag}{i}", [128, nHT * NS], BF16) for i in range(2)]
        wd = sbl(f"wd{tag}", [128, (G // 128) * D], BF16)
        xin = [sbl(f"lx{tag}{i}", [128, D], F32 if src_is_f32 else BF16) for i in range(2)]
        xc = sbl(f"lc{tag}", [128, D], F32)
        xbf = [sbl(f"lb{tag}{i}", [128, D], BF16) for i in range(2)]
        stt = sbl(f"ls{tag}", [128, 4], F32)
        rres = xin
        xc2 = sbl(f"l2{tag}", [128, D], F32) if dst_is_f32 else None
        rout = [xc, xc2] if dst_is_f32 else xbf
        g_sb = [sbl(f"ge{tag}{i}", [128, 512], F32) for i in range(2)]
        u_sb = [sbl(f"ue{tag}{i}", [128, 512], F32) for i in range(2)]
        gu_sc = sbl(f"gu{tag}", [128, 512], F32)
        gu_t2 = sbl(f"gt{tag}", [128, 512], F32)
        gu_th = sbl(f"gh{tag}", [128, 512], F32)
        gu_gl = sbl(f"gg{tag}", [128, 512], F32)
        gps = [psl(f"gp{tag}{i}", [128, 512]) for i in range(3)]
        ups = [psl(f"up{tag}{i}", [128, 512]) for i in range(3)]
        dps = [psl(f"dp{tag}{i}", [128, 512]) for i in range(2)]
        trp = [dps[i][:, 0:64].bitcast(BF16) for i in range(2)]

        ident = pb.ident
        pe_xT_reads = 0
        ln_in_free = {0: 0, 1: 0}
        xbf_free = {0: 0, 1: 0}
        tr_evac = {0: 0, 1: 0}
        wgu_free = {0: 0, 1: 0}
        wgu_loaded = {0: 0, 1: 0}
        wd_loaded = 0
        wd_free = 0
        gps_pe = {0: 0, 1: 0, 2: 0}
        gps_dve = {0: 0, 1: 0, 2: 0}
        ups_pe = {0: 0, 1: 0, 2: 0}
        ups_dve = {0: 0, 1: 0, 2: 0}
        ups_dve2 = {0: 0, 1: 0, 2: 0}
        dps_pe = {0: 0, 1: 0}
        dps_dve = {0: 0, 1: 0}
        inter_pe_reads = 0
        rr_free = {0: 0, 1: 0}
        ro_free = {0: 0, 1: 0}
        y_stt_done = 0

        n_sup_total = NMLP // NS
        for tci, (row0, tclen) in enumerate(tcs):
            ntt = tclen // 128
            pieces = cfg.pieces(tclen)
            for i in range(ntt):
                bp = i % 2
                r = row0 + i * 128
                cd = pb.dma("qA", xin[bp][:], src[r:r + 128, :],
                            after=[("dve", ln_in_free[bp]), ("qA", ro_free[bp])])
                pb.ins("dve", lambda e, b=bp: e.reduce_sum(stt[:, 0:1], xin[b][:], axis=AX.X),
                       after=[("qA", cd), ("dve", pb.cnt["dve"]), ("act", pb.cnt["act"])])
                pb.ins("dve", lambda e: e.tensor_scalar(stt[:, 0:1], stt[:, 0:1], 1.0 / D, None, ALU.mult),
                       after=[("dve", pb.cnt["dve"])])
                ln_in_free[bp] = pb.ins("dve", lambda e, b=bp: e.tensor_scalar(xc[:], xin[b][:], stt[:, 0:1], None, ALU.subtract),
                                        after=[("dve", pb.cnt["dve"]), ("act", pb.cnt["act"]),
                                               ("qA", max(ro_free[0], ro_free[1]))])
                pb.ins("act", lambda e, b=bp: e.activation(xbf[b][:], xc[:], AF.Square, accum_out=stt[:, 1:2]),
                       after=[("dve", pb.cnt["dve"]), ("pe", xbf_free[bp]), ("qA", ro_free[bp])])
                pb.ins("act", lambda e: e.activation(stt[:, 2:3], stt[:, 1:2], AF.Sqrt, bias=LN_EPS, scale=1.0 / D),
                       after=[("act", pb.cnt["act"])])
                pb.ins("dve", lambda e: e.reciprocal(stt[:, 3:4], stt[:, 2:3]),
                       after=[("act", pb.cnt["act"]), ("dve", pb.cnt["dve"])])
                pb.ins("act", lambda e, b=bp: e.activation(xbf[b][:], xc[:], AF.Copy, scale=stt[:, 3:4]),
                       after=[("dve", pb.cnt["dve"])])
                xbf_done = pb.cnt["act"]
                for ht in range(nHT):
                    tp = (i * nHT + ht) % 2
                    xbf_free[bp] = pb.ins(
                        "pe", lambda e, ht=ht, tp=tp, b=bp: e.transpose(trp[tp], xbf[b][:, ht * 128:(ht + 1) * 128], ident[:]),
                        after=[("act", xbf_done), ("dve", tr_evac[tp]), ("pe", pe_xT_reads),
                               ("dve", max(dps_dve[0], dps_dve[1]))])
                    tr_evac[tp] = pb.ins(
                        "dve", lambda e, ht=ht, i=i, tp=tp, tclen=tclen: e.tensor_copy(
                            xT[:, ht * tclen + i * 128: ht * tclen + i * 128 + 128], trp[tp]),
                        after=[("pe", pb.cnt["pe"]), ("dve", pb.cnt["dve"])])
            ln_done_dve = pb.cnt["dve"]

            for g in range(NMLP // G):
                if dq is not None and tci == 0:
                    need_d = 16 * ((g * G) // dq.jobs[2]["KCH"] + 1)
                    dq.ensure(2, need_d)
                    pb.wait("sp", "qA", dq.cover_count(2, need_d), exact=True)
                for si in range(n_gsup):
                    s = g * n_gsup + si
                    sp_ = s % 2
                    if si == 1 or n_gsup == 1:
                        for ntg in range(G // 128):
                            wd_loaded = pb.dma("qS", wd[:, ntg * D:(ntg + 1) * D],
                                               WdT[:, g * G + ntg * 128: g * G + ntg * 128 + 128],
                                               transpose=True, bound=(ntg == 0),
                                               after=[("pe", wd_free)] if ntg == 0 else [])
                    if dq is not None and tci == 0:
                        kpn = dq.jobs[0]["k_chunks"] * (NS // 128)
                        margin = min((s + 6) * kpn, len(dq.jobs[0]["work"]))
                        dq.ensure(0, margin)
                        dq.ensure(1, margin)
                        dq.ensure(2, s * len(dq.jobs[2]["work"]) // (NMLP // NS) + 32)
                        need = (s + 1) * kpn
                        pb.wait("sp", "qA", max(dq.cover_count(0, need), dq.cover_count(1, need)), exact=True)
                    rows = nHT * NS
                    pb.dma("qS", wg[sp_][:], WgT[s * rows:(s + 1) * rows, :],
                           transpose=True, after=[("pe", wgu_free[sp_])])
                    wgu_loaded[sp_] = pb.dma("qS", wu[sp_][:], WuT[s * rows:(s + 1) * rows, :],
                                             transpose=True, bound=False)
                    for ntl in range(NS // 128):
                        ng128 = si * (NS // 128) + ntl
                        for pidx, (poff, plen) in enumerate(pieces):
                            pp = (ng128 * len(pieces) + pidx) % 3
                            for ht in range(nHT):
                                pb.ins("pe", lambda e, ht=ht, ntl=ntl, pp=pp, sp_=sp_, poff=poff, plen=plen, tclen=tclen:
                                       e.matmul(gps[pp][:, 0:plen],
                                                wg[sp_][:, ht * NS + ntl * 128: ht * NS + ntl * 128 + 128],
                                                xT[:, ht * tclen + poff: ht * tclen + poff + plen],
                                                start=(ht == 0), stop=(ht == nHT - 1)),
                                       sem=(False if ht < nHT - 1 else "pe"),
                                       after=([("qS", wgu_loaded[sp_]), ("dve", ln_done_dve), ("act", gps_dve[pp])]
                                              if ht == 0 else []))
                            g_pe = pb.cnt["pe"]
                            gps_pe[pp] = g_pe
                            for ht in range(nHT):
                                pb.ins("pe", lambda e, ht=ht, ntl=ntl, pp=pp, sp_=sp_, poff=poff, plen=plen, tclen=tclen:
                                       e.matmul(ups[pp][:, 0:plen],
                                                wu[sp_][:, ht * NS + ntl * 128: ht * NS + ntl * 128 + 128],
                                                xT[:, ht * tclen + poff: ht * tclen + poff + plen],
                                                start=(ht == 0), stop=(ht == nHT - 1)),
                                       sem=(False if ht < nHT - 1 else "pe"),
                                       after=[("act", ups_dve2[pp])] if ht == 0 else [])
                            u_pe = pb.cnt["pe"]
                            ups_pe[pp] = u_pe
                            pe_xT_reads = u_pe
                            wgu_free[sp_] = u_pe
                            cb = (ng128 * len(pieces) + pidx) % 2
                            pb.ins("act", lambda e, pp=pp, plen=plen, cb=cb: e.copy(g_sb[cb][:, 0:plen], gps[pp][:, 0:plen]),
                                   after=[("pe", g_pe), ("dve", pb.cnt["dve"]), ("act", pb.cnt["act"])])
                            gps_dve[pp] = pb.cnt["act"]
                            pb.ins("act", lambda e, pp=pp, plen=plen, cb=cb: e.copy(u_sb[cb][:, 0:plen], ups[pp][:, 0:plen]),
                                   after=[("pe", u_pe), ("dve", pb.cnt["dve"]), ("act", pb.cnt["act"])])
                            ups_dve2[pp] = pb.cnt["act"]
                            pb.ins("act", lambda e, plen=plen, cb=cb: e.activation(gu_sc[:, 0:plen], g_sb[cb][:, 0:plen], AF.Square),
                                   after=[("act", pb.cnt["act"])])
                            pb.ins("dve", lambda e, plen=plen, cb=cb: e.scalar_tensor_tensor(
                                       gu_t2[:, 0:plen], gu_sc[:, 0:plen], GELU_K, g_sb[cb][:, 0:plen], ALU.add, ALU.mult),
                                   after=[("act", pb.cnt["act"]), ("dve", pb.cnt["dve"])])
                            pb.ins("act", lambda e, plen=plen: e.activation(gu_th[:, 0:plen], gu_t2[:, 0:plen], AF.Tanh, scale=GELU_C3),
                                   after=[("dve", pb.cnt["dve"]), ("act", pb.cnt["act"])])
                            pb.ins("dve", lambda e, plen=plen, cb=cb: e.scalar_tensor_tensor(
                                       gu_gl[:, 0:plen], gu_th[:, 0:plen], 1.0, g_sb[cb][:, 0:plen], ALU.add, ALU.mult),
                                   after=[("act", pb.cnt["act"]), ("dve", pb.cnt["dve"])])
                            ups_dve[pp] = pb.ins("dve", lambda e, plen=plen, ng128=ng128, poff=poff, tclen=tclen, cb=cb:
                                                 e.scalar_tensor_tensor(
                                                     inter[:, ng128 * tclen + poff: ng128 * tclen + poff + plen],
                                                     gu_gl[:, 0:plen], 0.5, u_sb[cb][:, 0:plen], ALU.mult, ALU.mult),
                                                 after=[("pe", u_pe), ("dve", pb.cnt["dve"]),
                                                        ("pe", inter_pe_reads if (s % n_gsup == 0 and ntl == 0) else 0)])
                inter_done = pb.cnt["dve"]
                for ts in range(ntt):
                    for hc in range(D // 512):
                        pp = (ts * (D // 512) + hc) % 2
                        for ntg in range(G // 128):
                            pb.ins("pe", lambda e, ts=ts, hc=hc, ntg=ntg, pp=pp, tclen=tclen:
                                   e.matmul(dps[pp][:],
                                            inter[:, ntg * tclen + ts * 128: ntg * tclen + ts * 128 + 128],
                                            wd[:, ntg * D + hc * 512: ntg * D + hc * 512 + 512],
                                            start=(ntg == 0), stop=(ntg == G // 128 - 1)),
                                   sem=(False if ntg < G // 128 - 1 else "pe"),
                                   after=([("dve", inter_done), ("qS", wd_loaded), ("dve", dps_dve[pp])]
                                          if ntg == 0 else []))
                        d_pe = pb.cnt["pe"]
                        dps_pe[pp] = d_pe
                        if g == 0:
                            dps_dve[pp] = pb.ins("dve", lambda e, ts=ts, hc=hc, pp=pp: e.tensor_copy(
                                                     yacc[:, ts * D + hc * 512: ts * D + hc * 512 + 512], dps[pp][:]),
                                                 after=[("pe", d_pe), ("dve", max(pb.cnt["dve"], y_stt_done))])
                        else:
                            dps_dve[pp] = pb.ins("dve", lambda e, ts=ts, hc=hc, pp=pp: e.tensor_tensor(
                                                     yacc[:, ts * D + hc * 512: ts * D + hc * 512 + 512],
                                                     yacc[:, ts * D + hc * 512: ts * D + hc * 512 + 512],
                                                     dps[pp][:], ALU.add),
                                                 after=[("pe", d_pe), ("dve", pb.cnt["dve"])])
                inter_pe_reads = pb.cnt["pe"]
                wd_free = pb.cnt["pe"]
            y_done = pb.cnt["dve"]

            for ts in range(ntt):
                bp = ts % 2
                r = row0 + ts * 128
                cd = pb.dma("qA", rres[bp][:], src[r:r + 128, :],
                            after=[("dve", ln_in_free[bp]), ("qA", ro_free[bp])])
                ln_in_free[bp] = pb.ins("dve", lambda e, ts=ts, bp=bp: e.scalar_tensor_tensor(
                                            rout[bp][:], yacc[:, ts * D:(ts + 1) * D], 0.5, rres[bp][:], ALU.mult, ALU.add),
                                        after=[("qA", cd), ("dve", y_done), ("qA", ro_free[bp]),
                                               ("dve", pb.cnt["dve"]), ("act", pb.cnt["act"]), ("pe", xbf_free[bp])])
                ro_free[bp] = pb.dma("qA", dst[r + dst_row_off:r + dst_row_off + 128, :], rout[bp][:],
                                     after=[("dve", pb.cnt["dve"])])
            y_stt_done = pb.cnt["dve"]
            if dq is not None and tci == 0:
                dq.finish()
        pb.drain_queues()
        pb.flush()


def emit_attn(pb, cfg, R1, R2, Wq, Wk, Wv, Wo, Wq_nat, Wk_nat, Wv_nat, Wo_nat, xaTd,
              qTd, aoTd,
              bq_in, bk_in, bv_in, bo_in, qn4_in, kn4_in, scaleb,
              m01a_in, fila_in, m01b_in, filb_in, cos_in, sin_in, e0, ident):
    nc = pb.nc
    D, NH, NKV, DK, DKV = cfg.D, cfg.NH, cfg.NKV, cfg.DK, cfg.DKV
    nHT, nT, T = cfg.nHT, cfg.nT, cfg.T
    SB = min(4, NH)
    n_sb = NH // SB

    with ExitStack() as outer:
        sbo = lambda n, s, d: outer.enter_context(nc.sbuf_tensor(n, s, d))
        kT = sbo("a_kT", [128, NKV * T], BF16)
        vsb = sbo("a_v", [128, nT * DKV], BF16)
        bqT = sbo("a_bqT", [128, D], BF16)
        bkT = sbo("a_bkT", [128, DKV], BF16)
        bvT = sbo("a_bvT", [128, DKV], BF16)
        boT = sbo("a_boT", [128, D], BF16)
        qn4 = sbo("a_qn4", [128, 512], F32)
        kn4 = sbo("a_kn4", [128, DKV], F32)
        m01a = sbo("a_m01a", [128, 256], F32)
        fila = sbo("a_fila", [128, 256], F32)
        m01b = sbo("a_m01b", [128, 256], F32)
        filb = sbo("a_filb", [128, 256], F32)

        pb.dma("qS", qn4[:], qn4_in)
        pb.dma("qS", kn4[:], kn4_in, bound=False)
        pb.dma("qS", m01a[:], m01a_in, bound=False)
        pb.dma("qS", fila[:], fila_in, bound=False)
        pb.dma("qS", m01b[:], m01b_in, bound=False)
        pb.dma("qS", filb[:], filb_in, bound=False)
        with ExitStack() as stp:
            b32 = stp.enter_context(nc.sbuf_tensor("a_b32", [128, D], F32))
            for btile, srcb, width in ((bqT, bq_in, D), (bkT, bk_in, DKV),
                                       (bvT, bv_in, DKV), (boT, bo_in, D)):
                pb.ins("dve", lambda e, t=btile: e.memset(t[:], 0.0),
                       after=[("dve", pb.cnt["dve"])])
                pb.dma("qS", b32[0:1, 0:width], srcb, after=[("dve", pb.cnt["dve"])])
                pb.ins("dve", lambda e, t=btile, w=width: e.tensor_copy(t[0:1, 0:w], b32[0:1, 0:w]),
                       after=[("qS", pb.cnt["qS"]), ("dve", pb.cnt["dve"])])
            pb.drain_queues()
            pb.flush()

        with ExitStack() as st0:
            sb0 = lambda n, s, d: st0.enter_context(nc.sbuf_tensor(n, s, d))
            wnat = [sb0(f"a_wn{i}", [128, D], BF16) for i in range(2)]
            wn_free = {0: 0, 1: 0}
            for wsrc, wdst, rows, blocked in ((Wq, Wq_nat, D, True), (Wo, Wo_nat, D, True),
                                              (Wk, Wk_nat, DKV, False), (Wv, Wv_nat, DKV, False)):
                for rt in range(rows // 128):
                    bp = rt % 2
                    pb.dma("qG", wnat[bp][:], wsrc[rt * 128:(rt + 1) * 128, :],
                           after=[("qS", wn_free[bp])])
                    if blocked:
                        ch = (rt * 128) // 512
                        o = (rt * 128) % 512
                        dst = wdst.rearrange("(a ht j) c -> a ht j c", ht=nHT, j=512)[ch, :, o:o + 128, :]
                        dst = dst.transpose([1, 0, 2])
                        wn_free[bp] = pb.dma("qS", dst, wnat[bp][:].rearrange("p (ht c) -> p ht c", c=128),
                                             after=[("qG", pb.cnt["qG"])])
                    else:
                        wn_free[bp] = pb.dma("qS", wdst[rt * 128:(rt + 1) * 128, :], wnat[bp][:],
                                             after=[("qG", pb.cnt["qG"])])

            wkT = sb0("a_wkT", [128, nHT * DKV], BF16)
            wvT = sb0("a_wvT", [128, nHT * DKV], BF16)
            for ht in range(nHT):
                pb.dma("qS", wkT[:, ht * DKV:(ht + 1) * DKV], Wk_nat[:, ht * 128:(ht + 1) * 128],
                       transpose=True, bound=(ht == 0))
            for ht in range(nHT):
                pb.dma("qS", wvT[:, ht * DKV:(ht + 1) * DKV], Wv_nat[:, ht * 128:(ht + 1) * 128],
                       transpose=True, bound=(ht == 0))

            xin = [sb0(f"a_xi{i}", [128, D], BF16) for i in range(2)]
            xc = sb0("a_xc", [128, D], F32)
            xbf = [sb0(f"a_xb{i}", [128, D], BF16) for i in range(2)]
            stt = sb0("a_st", [128, 4], F32)
            xstg = [sb0(f"a_xs{i}", [128, nHT * 128], BF16) for i in range(2)]
            trp = [st0.enter_context(nc.psum_tensor(f"a_tr0{i}", [128, 128], BF16)) for i in range(2)]

            ln_in_free = {0: 0, 1: 0}
            xbf_free = {0: 0, 1: 0}
            xstg_written = {0: 0, 1: 0}
            tr_evac = {0: 0, 1: 0}
            for i in range(nT):
                bp = i % 2
                r = i * 128
                cd = pb.dma("qA", xin[bp][:], R1[r:r + 128, :], after=[("dve", ln_in_free[bp])])
                pb.ins("dve", lambda e, b=bp: e.reduce_sum(stt[:, 0:1], xin[b][:], axis=AX.X),
                       after=[("qA", cd), ("dve", pb.cnt["dve"]), ("act", pb.cnt["act"])])
                pb.ins("dve", lambda e: e.tensor_scalar(stt[:, 0:1], stt[:, 0:1], 1.0 / D, None, ALU.mult),
                       after=[("dve", pb.cnt["dve"])])
                ln_in_free[bp] = pb.ins("dve", lambda e, b=bp: e.tensor_scalar(xc[:], xin[b][:], stt[:, 0:1], None, ALU.subtract),
                                        after=[("dve", pb.cnt["dve"]), ("act", pb.cnt["act"])])
                pb.ins("act", lambda e, b=bp: e.activation(xbf[b][:], xc[:], AF.Square, accum_out=stt[:, 1:2]),
                       after=[("dve", pb.cnt["dve"]), ("pe", xbf_free[bp])])
                pb.ins("act", lambda e: e.activation(stt[:, 2:3], stt[:, 1:2], AF.Sqrt, bias=LN_EPS, scale=1.0 / D),
                       after=[("act", pb.cnt["act"])])
                pb.ins("dve", lambda e: e.reciprocal(stt[:, 3:4], stt[:, 2:3]),
                       after=[("act", pb.cnt["act"]), ("dve", pb.cnt["dve"])])
                pb.ins("act", lambda e, b=bp: e.activation(xbf[b][:], xc[:], AF.Copy, scale=stt[:, 3:4]),
                       after=[("dve", pb.cnt["dve"])])
                xbf_done = pb.cnt["act"]
                for ht in range(nHT):
                    tp = (i * nHT + ht) % 2
                    xbf_free[bp] = pb.ins(
                        "pe", lambda e, ht=ht, tp=tp, b=bp: e.transpose(trp[tp][:], xbf[b][:, ht * 128:(ht + 1) * 128], ident[:]),
                        after=[("act", xbf_done), ("act", tr_evac[tp])])
                    tr_evac[tp] = pb.ins(
                        "act", lambda e, ht=ht, bp=bp, tp=tp: e.copy(
                            xstg[bp][:, ht * 128:(ht + 1) * 128], trp[tp][:]),
                        after=[("pe", pb.cnt["pe"]), ("qA", xstg_written[bp])])
                xdst = xaTd.rearrange("(ht p) t -> p ht t", p=128)[:, :, i * 128:(i + 1) * 128]
                xstg_written[bp] = pb.dma("qA", xdst,
                                          xstg[bp][:].rearrange("p (ht n) -> p ht n", n=128),
                                          after=[("act", tr_evac[0]), ("act", tr_evac[1])])
            pb.drain_queues()
            pb.flush()

            qn_nat = sb0("a_qn", [128, 512], F32)
            qrot = sb0("a_qrot", [128, 512], BF16)
            rst = sb0("a_rst", [128, 40], F32)
            rop = sb0("a_rop", [128, max(512, 2 * DKV)], F32)
            cos4 = sb0("a_cos", [128, 256], F32)
            sin4 = sb0("a_sin", [128, 256], F32)
            xas = [sb0(f"a_xa{i}", [128, nHT * 128], BF16) for i in range(2)]
            qps = st0.enter_context(nc.psum_tensor("a_qps", [128, 512], F32))

            xa_free = {0: 0, 1: 0}
            qrot_pe = 0
            tr_evac = {0: 0, 1: 0}
            kv_dve = 0
            for i in range(nT):
                bp = i % 2
                xsrc = xaTd.rearrange("(ht p) t -> p ht t", p=128)[:, :, i * 128:(i + 1) * 128]
                cd = pb.dma("qA", xas[bp][:].rearrange("p (ht n) -> p ht n", n=128), xsrc,
                            after=[("pe", xa_free[bp])])
                pb.dma("qA", cos4[:], cos_in[:, i * 256:(i + 1) * 256], after=[("dve", kv_dve)], bound=False)
                cs = pb.dma("qA", sin4[:], sin_in[:, i * 256:(i + 1) * 256], bound=False)
                for ht in range(nHT):
                    pb.ins("pe", lambda e, ht=ht, b=bp: e.matmul(qps[:, 0:DKV],
                               xas[b][:, ht * 128:(ht + 1) * 128],
                               wkT[:, ht * DKV:(ht + 1) * DKV],
                               start=(ht == 0), stop=False),
                           sem=False, after=[("qA", cd), ("qS", pb.cnt["qS"]), ("act", pb.cnt["act"])] if ht == 0 else [])
                pb.ins("pe", lambda e: e.matmul(qps[:, 0:DKV], e0[:], bkT[:, 0:DKV], start=False, stop=True))
                k_pe = pb.cnt["pe"]
                pb.ins("act", lambda e: e.copy(qn_nat[:, 0:DKV], qps[:, 0:DKV]),
                       after=[("pe", k_pe), ("dve", pb.cnt["dve"]), ("act", pb.cnt["act"])])
                pb.ins("act", lambda e: e.activation(rop[:, 0:DKV], qn_nat[:, 0:DKV], AF.Square),
                       after=[("act", pb.cnt["act"]), ("dve", pb.cnt["dve"])])
                pb.ins("dve", lambda e: e.reduce_sum(rst[:, 0:NKV].unsqueeze(2),
                                                     rop[:, 0:DKV].rearrange("p (h d) -> p h d", d=128), axis=AX.X),
                       after=[("act", pb.cnt["act"]), ("dve", pb.cnt["dve"])])
                pb.ins("act", lambda e: e.activation(rst[:, 4:4 + NKV], rst[:, 0:NKV], AF.Sqrt,
                                                     bias=RMS_EPS, scale=1.0 / DK),
                       after=[("dve", pb.cnt["dve"]), ("act", pb.cnt["act"])])
                pb.ins("dve", lambda e: e.reciprocal(rst[:, 0:NKV], rst[:, 4:4 + NKV]),
                       after=[("act", pb.cnt["act"]), ("dve", pb.cnt["dve"])])
                pb.ins("dve", lambda e: e.tensor_tensor(
                           qn_nat[:, 0:DKV].rearrange("p (h d) -> p h d", d=128),
                           qn_nat[:, 0:DKV].rearrange("p (h d) -> p h d", d=128),
                           rst[:, 0:NKV].unsqueeze(2).broadcast_to([128, NKV, 128]), ALU.mult),
                       after=[("dve", pb.cnt["dve"])])
                pb.ins("dve", lambda e: e.tensor_tensor(qn_nat[:, 0:DKV], qn_nat[:, 0:DKV], kn4[:, 0:DKV], ALU.mult),
                       after=[("dve", pb.cnt["dve"])])
                q1 = qn_nat[:, 0:DKV:2]
                q2 = qn_nat[:, 1:DKV:2]
                hw = DKV // 2
                pb.ins("dve", lambda e, q1=q1, hw=hw: e.tensor_tensor(rop[:, 0:hw], q1, cos4[:, 0:hw], ALU.mult),
                       after=[("qA", cs), ("dve", pb.cnt["dve"])])
                pb.ins("dve", lambda e, q2=q2, hw=hw: e.tensor_tensor(rop[:, hw:2 * hw], q2, sin4[:, 0:hw], ALU.mult))
                pb.ins("dve", lambda e, q1=q1, hw=hw: e.tensor_tensor(rop[:, 2 * hw:3 * hw], q1, sin4[:, 0:hw], ALU.mult))
                pb.ins("dve", lambda e, q2=q2, hw=hw: e.tensor_tensor(rop[:, 3 * hw:4 * hw], q2, cos4[:, 0:hw], ALU.mult),
                       after=[("dve", pb.cnt["dve"])])
                pb.ins("dve", lambda e, hw=hw: e.tensor_tensor(qrot[:, 0:DKV:2], rop[:, 0:hw], rop[:, hw:2 * hw], ALU.subtract),
                       after=[("dve", pb.cnt["dve"]), ("pe", qrot_pe)])
                pb.ins("dve", lambda e, hw=hw: e.tensor_tensor(qrot[:, 1:DKV:2], rop[:, 2 * hw:3 * hw], rop[:, 3 * hw:4 * hw], ALU.add),
                       after=[("dve", pb.cnt["dve"])])
                rope_done = pb.cnt["dve"]
                for kv in range(NKV):
                    tp = (i * NKV + kv) % 2
                    qrot_pe = pb.ins("pe", lambda e, kv=kv, tp=tp: e.transpose(trp[tp][:], qrot[:, kv * 128:(kv + 1) * 128], ident[:]),
                                     after=[("dve", rope_done), ("dve", tr_evac[tp])])
                    tr_evac[tp] = pb.ins("dve", lambda e, kv=kv, i=i, tp=tp: e.tensor_copy(
                                             kT[:, kv * T + i * 128: kv * T + i * 128 + 128], trp[tp][:]),
                                         after=[("pe", pb.cnt["pe"]), ("dve", pb.cnt["dve"])])
                kv_dve = pb.cnt["dve"]
                for ht in range(nHT):
                    pb.ins("pe", lambda e, ht=ht, b=bp: e.matmul(qps[:, 0:DKV],
                               xas[b][:, ht * 128:(ht + 1) * 128],
                               wvT[:, ht * DKV:(ht + 1) * DKV],
                               start=(ht == 0), stop=False),
                           sem=False, after=[("act", pb.cnt["act"])] if ht == 0 else [])
                xa_free[bp] = pb.ins("pe", lambda e: e.matmul(qps[:, 0:DKV], e0[:], bvT[:, 0:DKV], start=False, stop=True))
                pb.ins("act", lambda e, i=i: e.copy(vsb[:, i * DKV:(i + 1) * DKV], qps[:, 0:DKV]),
                       after=[("pe", pb.cnt["pe"]), ("act", pb.cnt["act"])])
            pb.drain_queues()
            pb.flush()

        with ExitStack() as sta:
            sba = lambda n, s, d: sta.enter_context(nc.sbuf_tensor(n, s, d))
            wqTs = sba("a_wq", [128, nHT * 512], BF16)
            xas = [sba(f"a_x3{i}", [128, nHT * 128], BF16) for i in range(2)]
            qn_nat = [sba(f"a_qn3{i}", [128, 512], F32) for i in range(2)]
            qrot = [sba(f"a_qr3{i}", [128, 512], BF16) for i in range(2)]
            qstg = [sba(f"a_qs3{i}", [128, 512], BF16) for i in range(2)]
            rst = [sba(f"a_rs3{i}", [128, 40], F32) for i in range(2)]
            rop = [sba(f"a_ro3{i}", [128, 512], F32) for i in range(2)]
            cos4 = [sba(f"a_co3{i}", [128, 256], F32) for i in range(2)]
            sin4 = [sba(f"a_si3{i}", [128, 256], F32) for i in range(2)]
            trp = [sta.enter_context(nc.psum_tensor(f"a_tr3{i}", [128, 128], BF16)) for i in range(2)]
            qps = [sta.enter_context(nc.psum_tensor(f"a_qp3{i}", [128, 512], F32)) for i in range(2)]

            tr_evac = {0: 0, 1: 0}
            qrot_pe = {0: 0, 1: 0}
            xa_free = {0: 0, 1: 0}
            qps_act = {0: 0, 1: 0}
            qstg_written = {0: 0, 1: 0}

            def a3a_mms(ch, bt, ib, wq_c):
                xsrc = xaTd.rearrange("(ht p) t -> p ht t", p=128)[:, :, bt * 128:(bt + 1) * 128]
                cdx = pb.dma("qA", xas[ib][:].rearrange("p (ht n) -> p ht n", n=128), xsrc,
                             after=[("pe", xa_free[ib])])
                pb.dma("qA", cos4[ib][:], cos_in[:, bt * 256:(bt + 1) * 256],
                       after=[("dve", qrot_pe.get("c" + str(ib), 0))], bound=False)
                pb.dma("qA", sin4[ib][:], sin_in[:, bt * 256:(bt + 1) * 256], bound=False)
                cs = pb.cnt["qA"]
                for ht in range(nHT):
                    pb.ins("pe", lambda e, ht=ht, ib=ib: e.matmul(
                               qps[ib][:],
                               xas[ib][:, ht * 128:(ht + 1) * 128],
                               wqTs[:, ht * 512:(ht + 1) * 512],
                               start=(ht == 0), stop=False),
                           sem=False,
                           after=[("qS", wq_c), ("qA", cdx), ("act", qps_act[ib])] if ht == 0 else [])
                pb.ins("pe", lambda e, ch=ch, ib=ib: e.matmul(qps[ib][:], e0[:], bqT[:, ch * 512:(ch + 1) * 512], start=False, stop=True))
                xa_free[ib] = pb.cnt["pe"]
                return pb.cnt["pe"], cs

            def a3a_chain(ch, bt, ib, q_pe, cs):
                pb.ins("act", lambda e, ib=ib: e.copy(qn_nat[ib][:], qps[ib][:]),
                       after=[("pe", q_pe), ("dve", pb.cnt["dve"]), ("act", pb.cnt["act"])])
                qps_act[ib] = pb.cnt["act"]
                pb.ins("act", lambda e, ib=ib: e.activation(rop[ib][:], qn_nat[ib][:], AF.Square),
                       after=[("act", pb.cnt["act"]), ("dve", pb.cnt["dve"])])
                pb.ins("dve", lambda e, ib=ib: e.reduce_sum(rst[ib][:, 0:4].unsqueeze(2),
                                                            rop[ib][:].rearrange("p (h d) -> p h d", d=128), axis=AX.X),
                       after=[("act", pb.cnt["act"]), ("dve", pb.cnt["dve"])])
                pb.ins("act", lambda e, ib=ib: e.activation(rst[ib][:, 4:8], rst[ib][:, 0:4], AF.Sqrt,
                                                            bias=RMS_EPS, scale=1.0 / DK),
                       after=[("dve", pb.cnt["dve"]), ("act", pb.cnt["act"])])
                pb.ins("dve", lambda e, ib=ib: e.reciprocal(rst[ib][:, 0:4], rst[ib][:, 4:8]),
                       after=[("act", pb.cnt["act"]), ("dve", pb.cnt["dve"])])
                pb.ins("dve", lambda e, ib=ib: e.tensor_tensor(
                           qn_nat[ib][:].rearrange("p (h d) -> p h d", d=128),
                           qn_nat[ib][:].rearrange("p (h d) -> p h d", d=128),
                           rst[ib][:, 0:4].unsqueeze(2).broadcast_to([128, 4, 128]), ALU.mult),
                       after=[("dve", pb.cnt["dve"])])
                pb.ins("dve", lambda e, ib=ib: e.tensor_tensor(qn_nat[ib][:], qn_nat[ib][:], qn4[:], ALU.mult),
                       after=[("dve", pb.cnt["dve"])])
                q1 = qn_nat[ib][:, 0:512:2]
                q2 = qn_nat[ib][:, 1:512:2]
                pb.ins("dve", lambda e, q1=q1, ib=ib: e.tensor_tensor(rop[ib][:, 0:256], q1, cos4[ib][:], ALU.mult),
                       after=[("qA", cs), ("dve", pb.cnt["dve"])])
                pb.ins("dve", lambda e, q2=q2, ib=ib: e.tensor_tensor(rop[ib][:, 256:512], q2, sin4[ib][:], ALU.mult))
                pb.ins("dve", lambda e, ib=ib: e.tensor_tensor(rop[ib][:, 0:256], rop[ib][:, 0:256], rop[ib][:, 256:512], ALU.subtract),
                       after=[("dve", pb.cnt["dve"]), ("pe", qrot_pe[ib])])
                pb.ins("dve", lambda e, ib=ib: e.tensor_copy(qrot[ib][:, 0:512:2], rop[ib][:, 0:256]),
                       after=[("dve", pb.cnt["dve"])])
                pb.ins("dve", lambda e, q1=q1, ib=ib: e.tensor_tensor(rop[ib][:, 0:256], q1, sin4[ib][:], ALU.mult),
                       after=[("dve", pb.cnt["dve"])])
                pb.ins("dve", lambda e, q2=q2, ib=ib: e.tensor_tensor(rop[ib][:, 256:512], q2, cos4[ib][:], ALU.mult))
                pb.ins("dve", lambda e, ib=ib: e.tensor_tensor(qrot[ib][:, 1:512:2], rop[ib][:, 0:256], rop[ib][:, 256:512], ALU.add),
                       after=[("dve", pb.cnt["dve"])])
                rope_done = pb.cnt["dve"]
                qrot_pe["c" + str(ib)] = rope_done
                for h4 in range(4):
                    tp = h4 % 2
                    qrot_pe[ib] = pb.ins("pe", lambda e, h4=h4, tp=tp, ib=ib: e.transpose(
                                             trp[tp][:], qrot[ib][:, h4 * 128:(h4 + 1) * 128], ident[:]),
                                         after=[("dve", rope_done), ("act", tr_evac[tp])])
                    tr_evac[tp] = pb.ins("act", lambda e, h4=h4, tp=tp, ib=ib: e.copy(
                                             qstg[ib][:, h4 * 128:(h4 + 1) * 128], trp[tp][:]),
                                         after=[("pe", pb.cnt["pe"]), ("act", pb.cnt["act"]),
                                                ("qA", qstg_written[ib])])
                qstg_written[ib] = pb.dma("qA", qTd[bt * 128:(bt + 1) * 128, ch * 512:(ch + 1) * 512], qstg[ib][:],
                                          after=[("act", tr_evac[0]), ("act", tr_evac[1])])

            for ch in range(NH // 4):
                rows = nHT * 512
                pb.dma("qS", wqTs[:], Wq_nat[ch * rows:(ch + 1) * rows, :],
                       transpose=True, after=[("pe", pb.cnt["pe"])])
                wq_c = pb.cnt["qS"]
                pend = None
                for bt in range(cfg.halo // 128, nT):
                    ib = bt % 2
                    cur = (ch, bt, ib) + a3a_mms(ch, bt, ib, wq_c)
                    if pend is not None:
                        a3a_chain(pend[0], pend[1], pend[2], pend[3], pend[4])
                    pend = cur
                a3a_chain(pend[0], pend[1], pend[2], pend[3], pend[4])
            pb.drain_queues()
            pb.flush()

        with ExitStack() as stb:
            sbb = lambda n, s, d: stb.enter_context(nc.sbuf_tensor(n, s, d))
            qTt = [sbb(f"a_qT3{i}", [128, NH * 128], BF16) for i in range(2)]
            aoT = sbb("a_aoT", [128, NH * 128], BF16)
            sm1 = [sbb(f"a_sm1{i}", [128, SB * 256], F32) for i in range(2)]
            smE = [sbb(f"a_smE{i}", [128, SB * 256], F32) for i in range(2)]
            smA = [sbb(f"a_smA{i}", [128, SB * 256], BF16) for i in range(2)]
            rst = [sbb(f"a_rsb{i}", [128, 40], F32) for i in range(2)]
            atc = [sbb(f"a_at{i}", [128, 128], BF16) for i in range(2)]
            trp = [stb.enter_context(nc.psum_tensor(f"a_trb{i}", [128, 128], BF16)) for i in range(2)]
            sps = [stb.enter_context(nc.psum_tensor(f"a_sp3{i}", [128, SB * 256], F32)) for i in range(2)]
            ops_ = [stb.enter_context(nc.psum_tensor(f"a_op{i}", [128, 128], F32)) for i in range(2)]

            tr_evac = {0: 0, 1: 0}
            smA_pe = {0: 0, 1: 0}
            sps_dve = {0: 0, 1: 0}
            ops_act = {0: 0, 1: 0}
            ao_written = 0
            qT_free = {0: 0, 1: 0}
            qT_loaded = {0: 0, 1: 0}

            def a3b_smm(bt, sb_, ib):
                if sb_ == 0:
                    qT_loaded[bt % 2] = pb.dma("qS", qTt[bt % 2][:], qTd[bt * 128:(bt + 1) * 128, :],
                                               after=[("pe", qT_free[bt % 2])])
                w0 = (bt - 1) * 128
                for h8 in range(SB):
                    h = sb_ * SB + h8
                    kv = h // (NH // NKV)
                    pb.ins("pe", lambda e, h=h, kv=kv, h8=h8, ib=ib, w0=w0, tb=bt % 2: e.matmul(
                               sps[ib][:, h8 * 256:(h8 + 1) * 256], qTt[tb][:, h * 128:(h + 1) * 128],
                               kT[:, kv * T + w0: kv * T + w0 + 256], start=True, stop=True),
                           sem=(False if h8 < SB - 1 else "pe"),
                           after=[("qS", qT_loaded[bt % 2]), ("dve", sps_dve[ib])] if h8 == 0 else [])
                qT_free[bt % 2] = pb.cnt["pe"]
                return pb.cnt["pe"]

            def a3b_chain_av(bt, sb_, ib, s_pe):
                m01 = m01b if bt == cfg.halo // 128 else m01a
                fil = filb if bt == cfg.halo // 128 else fila
                sm1v = sm1[ib][:].rearrange("p (h k) -> p h k", k=256)
                smEv = smE[ib][:].rearrange("p (h k) -> p h k", k=256)
                smAv = smA[ib][:].rearrange("p (h k) -> p h k", k=256)
                pb.ins("dve", lambda e, m01=m01, ib=ib, sm1v=sm1v: e.scalar_tensor_tensor(
                           sm1v, sps[ib][:].rearrange("p (h k) -> p h k", k=256), scaleb[:, 0:1],
                           m01[:].unsqueeze(1).broadcast_to([128, SB, 256]), ALU.mult, ALU.mult),
                       after=[("pe", s_pe), ("dve", pb.cnt["dve"]), ("act", pb.cnt["act"])])
                sps_dve[ib] = pb.cnt["dve"]
                pb.ins("dve", lambda e, fil=fil, sm1v=sm1v: e.tensor_tensor(
                           sm1v, sm1v, fil[:].unsqueeze(1).broadcast_to([128, SB, 256]), ALU.add),
                       after=[("dve", pb.cnt["dve"])])
                pb.ins("dve", lambda e, ib=ib, sm1v=sm1v: e.reduce_max(rst[ib][:, 8:8 + SB].unsqueeze(2), sm1v, axis=AX.X, negate=True),
                       after=[("dve", pb.cnt["dve"])])
                pb.ins("dve", lambda e, ib=ib, sm1v=sm1v: e.tensor_tensor(
                           sm1v, sm1v, rst[ib][:, 8:8 + SB].unsqueeze(2).broadcast_to([128, SB, 256]), ALU.add),
                       after=[("dve", pb.cnt["dve"])])
                pb.ins("act", lambda e, ib=ib: e.activation(smE[ib][:], sm1[ib][:], AF.Exp),
                       after=[("dve", pb.cnt["dve"]), ("act", pb.cnt["act"])])
                pb.ins("dve", lambda e, ib=ib, smEv=smEv: e.reduce_sum(rst[ib][:, 16:16 + SB].unsqueeze(2), smEv, axis=AX.X),
                       after=[("act", pb.cnt["act"]), ("dve", pb.cnt["dve"])])
                pb.ins("dve", lambda e, ib=ib: e.reciprocal(rst[ib][:, 24:24 + SB], rst[ib][:, 16:16 + SB]),
                       after=[("dve", pb.cnt["dve"])])
                pb.ins("dve", lambda e, ib=ib, smAv=smAv, smEv=smEv: e.tensor_tensor(
                           smAv, smEv, rst[ib][:, 24:24 + SB].unsqueeze(2).broadcast_to([128, SB, 256]), ALU.mult),
                       after=[("dve", pb.cnt["dve"]), ("pe", smA_pe[ib])])
                A_done = pb.cnt["dve"]
                for h8 in range(SB):
                    h = sb_ * SB + h8
                    kv = h // (NH // NKV)
                    for half in range(2):
                        tp = half
                        smA_pe[ib] = pb.ins("pe", lambda e, h8=h8, half=half, tp=tp, ib=ib: e.transpose(
                                                trp[tp][:], smA[ib][:, h8 * 256 + half * 128: h8 * 256 + (half + 1) * 128], ident[:]),
                                            after=[("dve", A_done), ("act", tr_evac[tp])])
                        tr_evac[tp] = pb.ins("act", lambda e, half=half, tp=tp: e.copy(atc[half][:], trp[tp][:]),
                                             after=[("pe", pb.cnt["pe"]), ("act", pb.cnt["act"])])
                    at_done = pb.cnt["act"]
                    op = h % 2
                    for half in range(2):
                        blk_ = bt - 1 + half
                        pb.ins("pe", lambda e, half=half, blk_=blk_, kv=kv, op=op: e.matmul(
                                   ops_[op][:], vsb[:, blk_ * DKV + kv * 128: blk_ * DKV + kv * 128 + 128],
                                   atc[half][:], start=(half == 0), stop=(half == 1)),
                               sem=(False if half == 0 else "pe"),
                               after=[("act", at_done), ("act", ops_act[op])] if half == 0 else [])
                    ops_act[op] = pb.ins("act", lambda e, h=h, op=op: e.copy(aoT[:, h * 128:(h + 1) * 128], ops_[op][:]),
                                         after=[("pe", pb.cnt["pe"]), ("act", pb.cnt["act"]),
                                                ("qA", ao_written if h == 0 else 0)])

            batches = [(bt, sb_) for bt in range(cfg.halo // 128, nT) for sb_ in range(n_sb)]
            pend = None
            for idx, (bt, sb_) in enumerate(batches):
                ib = idx % 2
                s_pe = a3b_smm(bt, sb_, ib)
                if pend is not None:
                    a3b_chain_av(*pend)
                    if pend[1] == n_sb - 1:
                        ao_written = pb.dma("qA", aoTd[pend[0] * 128:(pend[0] + 1) * 128, :], aoT[:],
                                            after=[("act", pb.cnt["act"])])
                pend = (bt, sb_, ib, s_pe)
            a3b_chain_av(*pend)
            ao_written = pb.dma("qA", aoTd[pend[0] * 128:(pend[0] + 1) * 128, :], aoT[:],
                                after=[("act", pb.cnt["act"])])
            pb.drain_queues()
            pb.flush()

        with ExitStack() as stc:
            sbc = lambda n, s, d: stc.enter_context(nc.sbuf_tensor(n, s, d))
            woS = sbc("a_wo", [128, nHT * 512], BF16)
            aoc = [sbc(f"a_ao{i}", [128, NH * 128], BF16) for i in range(2)]
            rres = [sbc(f"a_rr{i}", [128, 512], BF16) for i in range(2)]
            rout = [sbc(f"a_rx{i}", [128, 512], BF16) for i in range(2)]
            pps = [stc.enter_context(nc.psum_tensor(f"a_pp{i}", [128, 512], F32)) for i in range(2)]

            ao_free = {0: 0, 1: 0}
            rr_free = {0: 0, 1: 0}
            ro_free = {0: 0, 1: 0}
            pps_dve = {0: 0, 1: 0}
            for hc in range(D // 512):
                rows = nHT * 512
                pb.dma("qS", woS[:], Wo_nat[hc * rows:(hc + 1) * rows, :],
                       transpose=True, after=[("pe", pb.cnt["pe"])])
                wo_c = pb.cnt["qS"]
                for b in range(cfg.halo // 128, nT):
                    bp = b % 2
                    r = b * 128
                    cda = pb.dma("qS", aoc[bp][:], aoTd[r:r + 128, :], after=[("pe", ao_free[bp])])
                    cd = pb.dma("qA", rres[bp][:], R1[r:r + 128, hc * 512:(hc + 1) * 512],
                                after=[("dve", rr_free[bp])])
                    pp = bp
                    for dot in range(nHT):
                        pb.ins("pe", lambda e, dot=dot, pp=pp, bp=bp: e.matmul(
                                   pps[pp][:], aoc[bp][:, dot * 128:(dot + 1) * 128],
                                   woS[:, dot * 512:(dot + 1) * 512],
                                   start=(dot == 0), stop=False),
                               sem=False,
                               after=[("qS", max(wo_c, cda)), ("dve", pps_dve[pp])] if dot == 0 else [])
                    pb.ins("pe", lambda e, hc=hc, pp=pp: e.matmul(pps[pp][:], e0[:], boT[:, hc * 512:(hc + 1) * 512],
                                                                  start=False, stop=True))
                    ao_free[bp] = pb.cnt["pe"]
                    rr_free[bp] = pps_dve[pp] = pb.ins(
                        "dve", lambda e, pp=pp, bp=bp: e.scalar_tensor_tensor(
                            rout[bp][:], pps[pp][:], 1.0, rres[bp][:], ALU.mult, ALU.add),
                        after=[("pe", pb.cnt["pe"]), ("qA", cd), ("dve", pb.cnt["dve"]), ("qA", ro_free[bp])])
                    ro_free[bp] = pb.dma("qA", R2[r:r + 128, hc * 512:(hc + 1) * 512], rout[bp][:],
                                         after=[("dve", pb.cnt["dve"])])
            pb.drain_queues()
            pb.flush()



def _host_inputs(cfg, core, inputs, n_cores=8):
    import ml_dtypes
    D, T, NH, NKV, DK = cfg.D, cfg.T, cfg.NH, cfg.NKV, cfg.DK
    x = inputs["L0_attn_x"]
    B, S, _ = x.shape
    per_seq = (n_cores // B)
    seq = core // per_seq
    start = (core % per_seq) * cfg.T_own
    lo = start - cfg.halo
    if lo < 0:
        pad = np.zeros((-lo, D), dtype=np.float32)
        xs = np.concatenate([pad, x[seq, 0:start + cfg.T_own]], axis=0)
    else:
        xs = x[seq, lo:start + cfg.T_own]
    xs = np.ascontiguousarray(xs, dtype=np.float32)

    i_ = np.arange(128)[:, None]
    j_ = np.arange(256)[None, :]
    okA = (j_ - i_ >= 64) & (j_ - i_ <= 128)
    okB = okA & (j_ >= 128)
    m01a = okA.astype(np.float32)
    fila = np.where(okA, 0.0, MASK_FILL).astype(np.float32)
    first = (start == 0)
    m01b = (okB if first else okA).astype(np.float32)
    filb = np.where(okB if first else okA, 0.0, MASK_FILL).astype(np.float32)

    pos = np.arange(lo, start + cfg.T_own, dtype=np.int64)
    pos = np.maximum(pos, 0).astype(np.float32)
    inv_freq = (1.0 / (ROPE_BASE ** (np.arange(0, DK, 2, dtype=np.float32) / DK))).astype(np.float32)
    freqs = (pos[:, None] * inv_freq[None, :]).astype(np.float32)
    cost = np.cos(freqs).astype(np.float32)
    sint = np.sin(freqs).astype(np.float32)
    nTt = cfg.nT
    cost = np.tile(cost.reshape(nTt, 128, 1, 64), (1, 1, 4, 1)).transpose(1, 0, 2, 3).reshape(128, nTt * 256)
    sint = np.tile(sint.reshape(nTt, 128, 1, 64), (1, 1, 4, 1)).transpose(1, 0, 2, 3).reshape(128, nTt * 256)

    qn4 = np.broadcast_to(np.tile(inputs["L0_attn_qn_w"], 4)[None, :], (128, 512))
    kn4 = np.broadcast_to(np.tile(inputs["L0_attn_kn_w"], NKV)[None, :], (128, NKV * DK))
    scl = np.broadcast_to(np.asarray(inputs["L0_attn_scale"], dtype=np.float32).reshape(1, 1), (128, 1))

    e0 = np.zeros((128, 128), dtype=np.float32)
    e0[0, :] = 1.0

    return {
        "x": xs,
        "gW": inputs["L0_ffn_gate_W_packed"], "gS": inputs["L0_ffn_gate_scales"],
        "uW": inputs["L0_ffn_up_W_packed"], "uS": inputs["L0_ffn_up_scales"],
        "dW": inputs["L0_ffn_down_W_packed"], "dS": inputs["L0_ffn_down_scales"],
        "Wq": inputs["L0_attn_Wq"], "Wk": inputs["L0_attn_Wk"],
        "Wv": inputs["L0_attn_Wv"], "Wo": inputs["L0_attn_Wo"],
        "bq": inputs["L0_attn_bq"].reshape(1, -1), "bk": inputs["L0_attn_bk"].reshape(1, -1),
        "bv": inputs["L0_attn_bv"].reshape(1, -1), "bo": inputs["L0_attn_bo"].reshape(1, -1),
        "qn4": np.ascontiguousarray(qn4, dtype=np.float32),
        "kn4": np.ascontiguousarray(kn4, dtype=np.float32),
        "scl": np.ascontiguousarray(scl, dtype=np.float32),
        "ident": np.eye(128, dtype=np.float32).astype(ml_dtypes.bfloat16),
        "e0": e0.astype(ml_dtypes.bfloat16),
        "m01a": m01a, "fila": fila, "m01b": m01b, "filb": filb,
        "cost": np.ascontiguousarray(cost), "sint": np.ascontiguousarray(sint),
    }


def kernel(**inputs):
    cfg = CFG
    n_cores = 8
    in_maps = [_host_inputs(cfg, c, inputs) for c in range(n_cores)]
    nc = build_program(cfg, None)
    res = run_bass_kernel_spmd(nc, in_maps, core_ids=list(range(n_cores)))
    x = inputs["L0_attn_x"]
    B, S, D = x.shape
    out = np.empty((B, S, D), dtype=np.float32)
    per_seq = n_cores // B
    for c in range(n_cores):
        seq = c // per_seq
        start = (c % per_seq) * cfg.T_own
        out[seq, start:start + cfg.T_own] = res.results[c]["out"]
    return out


# version 24 (final)
# speedup vs baseline: 1.0938x; 1.0024x over previous
import numpy as np
from contextlib import ExitStack
import concourse.bass as bass
import concourse.mybir as mybir
from concourse.bass_utils import run_bass_kernel_spmd

F32 = mybir.dt.float32
BF16 = mybir.dt.bfloat16
I32 = mybir.dt.int32
AF = mybir.ActivationFunctionType
ALU = mybir.AluOpType
AX = mybir.AxisListType

GELU_C1 = 0.7978845608028654
GELU_C3 = 0.7978845608028654 * 0.044715
GELU_K = GELU_C1 / GELU_C3

LN_EPS = 1e-5
RMS_EPS = 1e-8
MASK_FILL = -10000.0
ROPE_BASE = 10000.0


class Cfg:
    def __init__(self, D=2048, NMLP=16384, NH=16, NKV=4, T_own=2048, halo=128,
                 n_super=256, G_super=4):
        self.D = D
        self.NMLP = NMLP
        self.NH = NH
        self.NKV = NKV
        self.DK = 128
        self.DKV = NKV * self.DK
        self.T_own = T_own
        self.halo = halo
        self.T = T_own + halo
        self.nT = self.T // 128
        self.nHT = D // 128
        self.nNT = NMLP // 128
        self.n_super = n_super
        self.G = G_super * n_super
        self.ffn1_tc = self._chunks(0, self.T)
        self.ffn2_tc = self._chunks(halo, T_own)

    def _chunks(self, start, total):
        out = []
        pos = start
        left = total
        while left > 0:
            c = min(768, left)
            out.append((pos, c))
            pos += c
            left -= c
        return out

    @staticmethod
    def pieces(length):
        n = (length + 511) // 512
        assert length % n == 0
        out = []
        pos = 0
        for i in range(n):
            ln = length // n
            out.append((pos, ln))
            pos += ln
        return out


CFG = Cfg()



ENG_OF_Q = {"qS": "sp", "qA": "act", "qG": "pool"}


class PB:

    def __init__(self, nc):
        self.nc = nc
        self.ctx = ExitStack()
        self.sems = {}
        self.cnt = {}
        self.pend = {k: [] for k in ("pe", "act", "dve", "pool", "sp")}
        self.lastw = {}
        for k in ("pe", "act", "dve", "pool", "qS", "qA", "qG"):
            self.sems[k] = self.ctx.enter_context(nc.semaphore(k))
            self.cnt[k] = 0

    def ins(self, eng, make, sem=None, after=()):
        for s, v in after:
            self.wait(eng, s, v)
        if sem is None:
            sem = eng
        if sem is False:
            self.pend[eng].append(lambda e, make=make: make(e))
            return None
        def f(e, make=make, sem=sem):
            make(e).then_inc(self.sems[sem], 1)
        self.pend[eng].append(f)
        self.cnt[sem] += 1
        return self.cnt[sem]

    def raw(self, eng, make, after=()):
        for s, v in after:
            self.wait(eng, s, v)
        self.pend[eng].append(lambda e, make=make: make(e))

    def dma(self, q, out_ap, in_ap, after=(), bound=True, **kw):
        eng = ENG_OF_Q[q]
        if bound:
            self.wait(eng, q, self.cnt[q])
        for s, v in after:
            self.wait(eng, s, v)
        def f(e, o=out_ap, i=in_ap, kw=kw, q=q):
            e.dma_start(o, i, **kw).then_inc(self.sems[q], 16)
        self.pend[eng].append(f)
        self.cnt[q] += 16
        return self.cnt[q]

    def wait(self, eng, sem, val, exact=False):
        if val is None or val <= 0:
            return
        if sem.endswith("!"):
            sem = sem[:-1]
            exact = True
        if sem in ("qS", "qA", "qG") and not exact:
            val = self.cnt[sem]
        key = (eng, sem)
        if self.lastw.get(key, 0) >= val:
            return
        self.lastw[key] = val
        def f(e, s=sem, v=val):
            e.wait_ge(self.sems[s], v)
        self.pend[eng].append(f)

    def flush(self):
        pend, self.pend = self.pend, {k: [] for k in self.pend}
        if not any(pend.values()):
            return
        with self.nc.Block() as blk:
            if pend["sp"]:
                @blk.sync
                def _(e, ops=pend["sp"]):
                    for f in ops:
                        f(e)
            if pend["pe"]:
                @blk.tensor
                def _(e, ops=pend["pe"]):
                    for f in ops:
                        f(e)
            if pend["act"]:
                @blk.scalar
                def _(e, ops=pend["act"]):
                    for f in ops:
                        f(e)
            if pend["dve"]:
                @blk.vector
                def _(e, ops=pend["dve"]):
                    for f in ops:
                        f(e)
            if pend["pool"]:
                @blk.gpsimd
                def _(e, ops=pend["pool"]):
                    for f in ops:
                        f(e)
        for s in ("pe", "act", "dve", "pool"):
            for e2 in ("pe", "act", "dve", "pool", "sp"):
                self.lastw[(e2, s)] = self.cnt[s]

    def drain_queues(self):
        for q in ("qS", "qA", "qG"):
            self.wait(ENG_OF_Q[q], q, self.cnt[q])

    def sb(self, name, shape, dtype):
        return self.ctx.enter_context(self.nc.sbuf_tensor(name, shape, dtype))

    def ps(self, name, shape, dtype=F32):
        return self.ctx.enter_context(self.nc.psum_tensor(name, shape, dtype))




def build_program(cfg, first_tile_masked):
    nc = bass.Bass("TRN2", target_bir_lowering=False)
    D, NMLP, T = cfg.D, cfg.NMLP, cfg.T
    nHT, nNT, nT = cfg.nHT, cfg.nNT, cfg.nT
    NH, NKV, DK, DKV = cfg.NH, cfg.NKV, cfg.DK, cfg.DKV

    x_in = nc.dram_tensor("x", [T, D], F32, kind="ExternalInput").ap()
    gW = nc.dram_tensor("gW", [NMLP, D // 2], I32, kind="ExternalInput").ap()
    gS = nc.dram_tensor("gS", [NMLP, D // 32], F32, kind="ExternalInput").ap()
    uW = nc.dram_tensor("uW", [NMLP, D // 2], I32, kind="ExternalInput").ap()
    uS = nc.dram_tensor("uS", [NMLP, D // 32], F32, kind="ExternalInput").ap()
    dW = nc.dram_tensor("dW", [D, NMLP // 2], I32, kind="ExternalInput").ap()
    dS = nc.dram_tensor("dS", [D, NMLP // 32], F32, kind="ExternalInput").ap()
    Wq = nc.dram_tensor("Wq", [D, D], F32, kind="ExternalInput").ap()
    Wk = nc.dram_tensor("Wk", [DKV, D], F32, kind="ExternalInput").ap()
    Wv = nc.dram_tensor("Wv", [DKV, D], F32, kind="ExternalInput").ap()
    Wo = nc.dram_tensor("Wo", [D, D], F32, kind="ExternalInput").ap()
    bq_in = nc.dram_tensor("bq", [1, D], F32, kind="ExternalInput").ap()
    bk_in = nc.dram_tensor("bk", [1, DKV], F32, kind="ExternalInput").ap()
    bv_in = nc.dram_tensor("bv", [1, DKV], F32, kind="ExternalInput").ap()
    bo_in = nc.dram_tensor("bo", [1, D], F32, kind="ExternalInput").ap()
    qn4_in = nc.dram_tensor("qn4", [128, 512], F32, kind="ExternalInput").ap()
    kn4_in = nc.dram_tensor("kn4", [128, DKV], F32, kind="ExternalInput").ap()
    scale_in = nc.dram_tensor("scl", [128, 1], F32, kind="ExternalInput").ap()
    ident_in = nc.dram_tensor("ident", [128, 128], BF16, kind="ExternalInput").ap()
    e0_in = nc.dram_tensor("e0", [128, 128], BF16, kind="ExternalInput").ap()
    m01a_in = nc.dram_tensor("m01a", [128, 256], F32, kind="ExternalInput").ap()
    fila_in = nc.dram_tensor("fila", [128, 256], F32, kind="ExternalInput").ap()
    m01b_in = nc.dram_tensor("m01b", [128, 256], F32, kind="ExternalInput").ap()
    filb_in = nc.dram_tensor("filb", [128, 256], F32, kind="ExternalInput").ap()
    cos_in = nc.dram_tensor("cost", [128, nT * 256], F32, kind="ExternalInput").ap()
    sin_in = nc.dram_tensor("sint", [128, nT * 256], F32, kind="ExternalInput").ap()
    out = nc.dram_tensor("out", [cfg.T_own, D], F32, kind="ExternalOutput").ap()

    Wg_nat = nc.dram_tensor("WgN", [NMLP * (D // 128), 128], BF16, kind="Internal").ap()
    Wu_nat = nc.dram_tensor("WuN", [NMLP * (D // 128), 128], BF16, kind="Internal").ap()
    Wd_nat = nc.dram_tensor("WdN", [D, NMLP], BF16, kind="Internal").ap()
    Wq_nat = nc.dram_tensor("WqN", [D * (D // 128), 128], BF16, kind="Internal").ap()
    Wk_nat = nc.dram_tensor("WkN", [DKV, D], BF16, kind="Internal").ap()
    Wv_nat = nc.dram_tensor("WvN", [DKV, D], BF16, kind="Internal").ap()
    Wo_nat = nc.dram_tensor("WoN", [D * (D // 128), 128], BF16, kind="Internal").ap()
    xaTd = nc.dram_tensor("xaTd", [D, T], BF16, kind="Internal").ap()
    qTd = nc.dram_tensor("qTd", [T, NH * 128], BF16, kind="Internal").ap()
    aoTd = nc.dram_tensor("aoTd", [T, NH * 128], BF16, kind="Internal").ap()
    R1 = nc.dram_tensor("R1", [T, D], BF16, kind="Internal").ap()
    R2 = nc.dram_tensor("R2", [T, D], BF16, kind="Internal").ap()

    pb = PB(nc)

    ident = pb.sb("identS", [128, 128], BF16)
    e0 = pb.sb("e0S", [128, 128], BF16)
    scaleb = pb.sb("scaleS", [128, 1], F32)
    pb.ident = ident

    const_vals = sorted({LN_EPS, RMS_EPS, 1.0 / D, 1.0 / DK, GELU_C3, 1.0, 0.0})
    for ci, v in enumerate(const_vals):
        ct = pb.sb(f"cst{ci}", [128, 1], F32)
        pb.ins("pool", lambda e, t=ct, v=v: e.memset(t[:], float(v)), sem="pool")
        nc.const_aps.register(F32, float(v), ct[:, 0:1])

    pb.dma("qS", ident[:], ident_in)
    pb.dma("qS", e0[:], e0_in)
    pb.dma("qS", scaleb[:], scale_in)
    pb.drain_queues()
    pb.flush()

    dq = DequantDriver(pb, cfg, [
        (gW, gS, Wg_nat, NMLP, D, "g", True),
        (uW, uS, Wu_nat, NMLP, D, "u", True),
        (dW, dS, Wd_nat, D, NMLP, "d", False),
    ])
    dq.finish()
    pb.drain_queues()
    pb.flush()
    dq.ctx.close()

    emit_ffn_pass(pb, cfg, src=x_in, src_is_f32=True, dst=R1, dst_is_f32=False,
                  dst_row_off=0, tcs=cfg.ffn1_tc, WgT=Wg_nat, WuT=Wu_nat, WdT=Wd_nat, tag="f1")
    pb.flush()

    emit_attn(pb, cfg, R1=R1, R2=R2, Wq=Wq, Wk=Wk, Wv=Wv, Wo=Wo,
              Wq_nat=Wq_nat, Wk_nat=Wk_nat, Wv_nat=Wv_nat, Wo_nat=Wo_nat, xaTd=xaTd,
              qTd=qTd, aoTd=aoTd,
              bq_in=bq_in, bk_in=bk_in, bv_in=bv_in, bo_in=bo_in,
              qn4_in=qn4_in, kn4_in=kn4_in, scaleb=scaleb,
              m01a_in=m01a_in, fila_in=fila_in, m01b_in=m01b_in, filb_in=filb_in,
              cos_in=cos_in, sin_in=sin_in, e0=e0, ident=ident)
    pb.flush()

    emit_ffn_pass(pb, cfg, src=R2, src_is_f32=False, dst=out, dst_is_f32=True,
                  dst_row_off=-cfg.halo, tcs=cfg.ffn2_tc, WgT=Wg_nat, WuT=Wu_nat, WdT=Wd_nat, tag="f2")

    pb.drain_queues()
    pb.flush()
    pb.ctx.close()
    return nc


class DequantDriver:

    def __init__(self, pb, cfg, jobs):
        self.pb = pb
        self.cfg = cfg
        self.ctx = ExitStack()
        sb = lambda n, s, d: self.ctx.enter_context(pb.nc.sbuf_tensor(n, s, d))
        self.pk = [sb(f"dq_pk{i}", [128, 1024], I32) for i in range(2)]
        self.sc = [sb(f"dq_sc{i}", [128, 64], F32) for i in range(2)]
        self.blo = sb("dq_blo", [128, 1024], I32)
        self.btmp = sb("dq_btmp", [128, 1024], I32)
        self.wn = [sb(f"dq_wn{i}", [128, 2048], BF16) for i in range(2)]
        self.ld_done = {0: 0, 1: 0}
        self.w_written = {0: 0, 1: 0}
        self.step = 0
        self.jobs = []
        self.emitted = []
        self.covered = []
        for packed, scales, w_nat, N, K, tag, blocked in jobs:
            KCH = min(K, 2048)
            k_chunks = K // KCH
            n_tiles = N // 128
            if blocked:
                work = [(nt, kc) for nt in range(n_tiles) for kc in range(k_chunks)]
            else:
                work = [(nt, kc) for kc in range(k_chunks) for nt in range(n_tiles)]
            self.jobs.append(dict(packed=packed, scales=scales, w_nat=w_nat, D=K,
                                  KCH=KCH, words=KCH // 2, k_chunks=k_chunks,
                                  blocked=blocked, work=work))
            self.emitted.append(0)
            self.covered.append([])

    def ensure(self, ji, n):
        J = self.jobs[ji]
        n = min(n, len(J["work"]))
        while self.emitted[ji] < n:
            nt, kc = J["work"][self.emitted[ji]]
            self._tile(J, nt, kc)
            self.covered[ji].append(self.pb.cnt["qA"])
            self.emitted[ji] += 1

    def cover_count(self, ji, items):
        items = min(items, len(self.jobs[ji]["work"]))
        if items <= 0:
            return 0
        return self.covered[ji][items - 1]

    def finish(self):
        for ji in range(len(self.jobs)):
            self.ensure(ji, 10 ** 9)

    def _tile(self, J, nt, kc):
        pb = self.pb
        KCH, words = J["KCH"], J["words"]
        bp = self.step % 2
        self.step += 1
        row0 = nt * 128
        pk, sc, wn = self.pk[bp], self.sc[bp], self.wn[bp]
        blo, btmp = self.blo, self.btmp
        W = words
        pb.dma("qS", pk[:, 0:W], J["packed"][row0:row0 + 128, kc * W:(kc + 1) * W],
               after=[("dve", self.ld_done[bp])])
        pb.dma("qS", sc[:, 0:KCH // 32], J["scales"][row0:row0 + 128, kc * (KCH // 32):(kc + 1) * (KCH // 32)],
               bound=False)
        pb.ins("dve", lambda E, sc=sc, K2=KCH // 32: E.tensor_scalar(sc[:, 0:K2], sc[:, 0:K2], float(2.0 ** 126), None, ALU.mult),
               after=[("qS", pb.cnt["qS"]), ("dve", pb.cnt["dve"])])
        scv = sc[:, 0:KCH // 32].unsqueeze(2).broadcast_to([128, KCH // 32, 16])
        we = wn[:, 0:KCH:2].rearrange("p (a b) -> p a b", b=16)
        wo = wn[:, 1:KCH:2].rearrange("p (a b) -> p a b", b=16)
        pb.ins("dve", lambda E, pk=pk, blo=blo, W=W: E.tensor_scalar(blo[:, 0:W], pk[:, 0:W], 8, 28, ALU.bitwise_and, ALU.logical_shift_left),
               after=[("dve", pb.cnt["dve"])])
        pb.ins("dve", lambda E, pk=pk, btmp=btmp, W=W: E.tensor_scalar(btmp[:, 0:W], pk[:, 0:W], 7, 22, ALU.bitwise_and, ALU.logical_shift_left),
               after=[("dve", pb.cnt["dve"])])
        pb.ins("dve", lambda E, blo=blo, btmp=btmp, W=W: E.tensor_tensor(blo[:, 0:W], blo[:, 0:W], btmp[:, 0:W], ALU.bitwise_or),
               after=[("dve", pb.cnt["dve"])])
        pb.ins("dve", lambda E, we=we, scv=scv, blo=blo, W=W: E.tensor_tensor(we, blo[:, 0:W].bitcast(F32), scv, ALU.mult),
               after=[("dve", pb.cnt["dve"]), ("qA!", self.w_written[bp])])
        pb.ins("dve", lambda E, pk=pk, blo=blo, W=W: E.tensor_scalar(blo[:, 0:W], pk[:, 0:W], 0x80, 24, ALU.bitwise_and, ALU.logical_shift_left),
               after=[("dve", pb.cnt["dve"])])
        pb.ins("dve", lambda E, pk=pk, btmp=btmp, W=W: E.tensor_scalar(btmp[:, 0:W], pk[:, 0:W], 0x70, 18, ALU.bitwise_and, ALU.logical_shift_left),
               after=[("dve", pb.cnt["dve"])])
        pb.ins("dve", lambda E, blo=blo, btmp=btmp, W=W: E.tensor_tensor(blo[:, 0:W], blo[:, 0:W], btmp[:, 0:W], ALU.bitwise_or),
               after=[("dve", pb.cnt["dve"])])
        self.ld_done[bp] = pb.ins("dve", lambda E, wo=wo, scv=scv, blo=blo, W=W: E.tensor_tensor(wo, blo[:, 0:W].bitcast(F32), scv, ALU.mult),
                                  after=[("dve", pb.cnt["dve"])])
        if J["blocked"]:
            NS = self.cfg.n_super
            nht_full = J["D"] // 128
            nht = KCH // 128
            sblk = row0 // NS
            o = row0 % NS
            dst = J["w_nat"].rearrange("(s ht n) c -> s ht n c", ht=nht_full, n=NS)[
                sblk, kc * nht:(kc + 1) * nht, o:o + 128, :]
            self.w_written[bp] = pb.dma("qA", dst.transpose([1, 0, 2]),
                                        wn[:, 0:KCH].rearrange("p (ht c) -> p ht c", c=128),
                                        after=[("dve", self.ld_done[bp])])
        else:
            self.w_written[bp] = pb.dma("qA", J["w_nat"][row0:row0 + 128, kc * KCH:(kc + 1) * KCH], wn[:, 0:KCH],
                                        after=[("dve", self.ld_done[bp])])


def emit_ffn_pass(pb, cfg, src, src_is_f32, dst, dst_is_f32, dst_row_off, tcs,
                  WgT, WuT, WdT, tag, dq=None):
    nc = pb.nc
    D, NMLP = cfg.D, cfg.NMLP
    nHT = cfg.nHT
    NS = cfg.n_super
    G = cfg.G
    n_gsup = G // NS
    nG = NMLP // G
    maxtc = max(l for _, l in tcs)

    with ExitStack() as st:
        sbl = lambda n, s, d: st.enter_context(nc.sbuf_tensor(n, s, d))
        psl = lambda n, s, d=F32: st.enter_context(nc.psum_tensor(n, s, d))
        xT = sbl(f"xT{tag}", [128, nHT * maxtc], BF16)
        yacc = sbl(f"y{tag}", [128, (maxtc // 128) * D], F32)
        inter = sbl(f"int{tag}", [128, (G // 128) * maxtc], BF16)
        wg = [sbl(f"wg{tag}{i}", [128, nHT * NS], BF16) for i in range(2)]
        wu = [sbl(f"wu{tag}{i}", [128, nHT * NS], BF16) for i in range(2)]
        wd = sbl(f"wd{tag}", [128, (G // 128) * D], BF16)
        xin = [sbl(f"lx{tag}{i}", [128, D], F32 if src_is_f32 else BF16) for i in range(2)]
        xc = sbl(f"lc{tag}", [128, D], F32)
        xbf = [sbl(f"lb{tag}{i}", [128, D], BF16) for i in range(2)]
        stt = sbl(f"ls{tag}", [128, 4], F32)
        rres = xin
        xc2 = sbl(f"l2{tag}", [128, D], F32) if dst_is_f32 else None
        rout = [xc, xc2] if dst_is_f32 else xbf
        g_sb = [sbl(f"ge{tag}{i}", [128, 512], F32) for i in range(2)]
        u_sb = [sbl(f"ue{tag}{i}", [128, 512], F32) for i in range(2)]
        gu_sc = sbl(f"gu{tag}", [128, 512], F32)
        gu_t2 = sbl(f"gt{tag}", [128, 512], F32)
        gu_th = sbl(f"gh{tag}", [128, 512], F32)
        gu_gl = sbl(f"gg{tag}", [128, 512], F32)
        gps = [psl(f"gp{tag}{i}", [128, 512]) for i in range(3)]
        ups = [psl(f"up{tag}{i}", [128, 512]) for i in range(3)]
        dps = [psl(f"dp{tag}{i}", [128, 512]) for i in range(2)]
        trp = [dps[i][:, 0:64].bitcast(BF16) for i in range(2)]

        ident = pb.ident
        pe_xT_reads = 0
        ln_in_free = {0: 0, 1: 0}
        xbf_free = {0: 0, 1: 0}
        tr_evac = {0: 0, 1: 0}
        wgu_free = {0: 0, 1: 0}
        wgu_loaded = {0: 0, 1: 0}
        wd_loaded = 0
        wd_free = 0
        gps_pe = {0: 0, 1: 0, 2: 0}
        gps_dve = {0: 0, 1: 0, 2: 0}
        ups_pe = {0: 0, 1: 0, 2: 0}
        ups_dve = {0: 0, 1: 0, 2: 0}
        ups_dve2 = {0: 0, 1: 0, 2: 0}
        dps_pe = {0: 0, 1: 0}
        dps_dve = {0: 0, 1: 0}
        inter_pe_reads = 0
        rr_free = {0: 0, 1: 0}
        ro_free = {0: 0, 1: 0}
        y_stt_done = 0

        n_sup_total = NMLP // NS
        for tci, (row0, tclen) in enumerate(tcs):
            ntt = tclen // 128
            pieces = cfg.pieces(tclen)
            for i in range(ntt):
                bp = i % 2
                r = row0 + i * 128
                cd = pb.dma("qA", xin[bp][:], src[r:r + 128, :],
                            after=[("dve", ln_in_free[bp]), ("qA", ro_free[bp])])
                pb.ins("dve", lambda e, b=bp: e.reduce_sum(stt[:, 0:1], xin[b][:], axis=AX.X),
                       after=[("qA", cd), ("dve", pb.cnt["dve"]), ("act", pb.cnt["act"])])
                pb.ins("dve", lambda e: e.tensor_scalar(stt[:, 0:1], stt[:, 0:1], 1.0 / D, None, ALU.mult),
                       after=[("dve", pb.cnt["dve"])])
                ln_in_free[bp] = pb.ins("dve", lambda e, b=bp: e.tensor_scalar(xc[:], xin[b][:], stt[:, 0:1], None, ALU.subtract),
                                        after=[("dve", pb.cnt["dve"]), ("act", pb.cnt["act"]),
                                               ("qA", max(ro_free[0], ro_free[1]))])
                pb.ins("act", lambda e, b=bp: e.activation(xbf[b][:], xc[:], AF.Square, accum_out=stt[:, 1:2]),
                       after=[("dve", pb.cnt["dve"]), ("pe", xbf_free[bp]), ("qA", ro_free[bp])])
                pb.ins("act", lambda e: e.activation(stt[:, 2:3], stt[:, 1:2], AF.Sqrt, bias=LN_EPS, scale=1.0 / D),
                       after=[("act", pb.cnt["act"])])
                pb.ins("dve", lambda e: e.reciprocal(stt[:, 3:4], stt[:, 2:3]),
                       after=[("act", pb.cnt["act"]), ("dve", pb.cnt["dve"])])
                pb.ins("act", lambda e, b=bp: e.activation(xbf[b][:], xc[:], AF.Copy, scale=stt[:, 3:4]),
                       after=[("dve", pb.cnt["dve"])])
                xbf_done = pb.cnt["act"]
                for ht in range(nHT):
                    tp = (i * nHT + ht) % 2
                    xbf_free[bp] = pb.ins(
                        "pe", lambda e, ht=ht, tp=tp, b=bp: e.transpose(trp[tp], xbf[b][:, ht * 128:(ht + 1) * 128], ident[:]),
                        after=[("act", xbf_done), ("dve", tr_evac[tp]), ("pe", pe_xT_reads),
                               ("dve", max(dps_dve[0], dps_dve[1]))])
                    tr_evac[tp] = pb.ins(
                        "dve", lambda e, ht=ht, i=i, tp=tp, tclen=tclen: e.tensor_copy(
                            xT[:, ht * tclen + i * 128: ht * tclen + i * 128 + 128], trp[tp]),
                        after=[("pe", pb.cnt["pe"]), ("dve", pb.cnt["dve"])])
            ln_done_dve = pb.cnt["dve"]

            for g in range(NMLP // G):
                if dq is not None and tci == 0:
                    need_d = 16 * ((g * G) // dq.jobs[2]["KCH"] + 1)
                    dq.ensure(2, need_d)
                    pb.wait("sp", "qA", dq.cover_count(2, need_d), exact=True)
                for si in range(n_gsup):
                    s = g * n_gsup + si
                    sp_ = s % 2
                    if si == 1 or n_gsup == 1:
                        for ntg in range(G // 128):
                            wd_loaded = pb.dma("qS", wd[:, ntg * D:(ntg + 1) * D],
                                               WdT[:, g * G + ntg * 128: g * G + ntg * 128 + 128],
                                               transpose=True, bound=(ntg == 0),
                                               after=[("pe", wd_free)] if ntg == 0 else [])
                    if dq is not None and tci == 0:
                        kpn = dq.jobs[0]["k_chunks"] * (NS // 128)
                        margin = min((s + 6) * kpn, len(dq.jobs[0]["work"]))
                        dq.ensure(0, margin)
                        dq.ensure(1, margin)
                        dq.ensure(2, s * len(dq.jobs[2]["work"]) // (NMLP // NS) + 32)
                        need = (s + 1) * kpn
                        pb.wait("sp", "qA", max(dq.cover_count(0, need), dq.cover_count(1, need)), exact=True)
                    rows = nHT * NS
                    pb.dma("qS", wg[sp_][:], WgT[s * rows:(s + 1) * rows, :],
                           transpose=True, after=[("pe", wgu_free[sp_])])
                    wgu_loaded[sp_] = pb.dma("qS", wu[sp_][:], WuT[s * rows:(s + 1) * rows, :],
                                             transpose=True, bound=False)
                    for ntl in range(NS // 128):
                        ng128 = si * (NS // 128) + ntl
                        for pidx, (poff, plen) in enumerate(pieces):
                            pp = (ng128 * len(pieces) + pidx) % 3
                            for ht in range(nHT):
                                pb.ins("pe", lambda e, ht=ht, ntl=ntl, pp=pp, sp_=sp_, poff=poff, plen=plen, tclen=tclen:
                                       e.matmul(gps[pp][:, 0:plen],
                                                wg[sp_][:, ht * NS + ntl * 128: ht * NS + ntl * 128 + 128],
                                                xT[:, ht * tclen + poff: ht * tclen + poff + plen],
                                                start=(ht == 0), stop=(ht == nHT - 1)),
                                       sem=(False if ht < nHT - 1 else "pe"),
                                       after=([("qS", wgu_loaded[sp_]), ("dve", ln_done_dve), ("act", gps_dve[pp])]
                                              if ht == 0 else []))
                            g_pe = pb.cnt["pe"]
                            gps_pe[pp] = g_pe
                            for ht in range(nHT):
                                pb.ins("pe", lambda e, ht=ht, ntl=ntl, pp=pp, sp_=sp_, poff=poff, plen=plen, tclen=tclen:
                                       e.matmul(ups[pp][:, 0:plen],
                                                wu[sp_][:, ht * NS + ntl * 128: ht * NS + ntl * 128 + 128],
                                                xT[:, ht * tclen + poff: ht * tclen + poff + plen],
                                                start=(ht == 0), stop=(ht == nHT - 1)),
                                       sem=(False if ht < nHT - 1 else "pe"),
                                       after=[("act", ups_dve2[pp])] if ht == 0 else [])
                            u_pe = pb.cnt["pe"]
                            ups_pe[pp] = u_pe
                            pe_xT_reads = u_pe
                            wgu_free[sp_] = u_pe
                            cb = (ng128 * len(pieces) + pidx) % 2
                            pb.ins("act", lambda e, pp=pp, plen=plen, cb=cb: e.copy(g_sb[cb][:, 0:plen], gps[pp][:, 0:plen]),
                                   after=[("pe", g_pe), ("dve", pb.cnt["dve"]), ("act", pb.cnt["act"])])
                            gps_dve[pp] = pb.cnt["act"]
                            pb.ins("act", lambda e, pp=pp, plen=plen, cb=cb: e.copy(u_sb[cb][:, 0:plen], ups[pp][:, 0:plen]),
                                   after=[("pe", u_pe), ("dve", pb.cnt["dve"]), ("act", pb.cnt["act"])])
                            ups_dve2[pp] = pb.cnt["act"]
                            pb.ins("act", lambda e, plen=plen, cb=cb: e.activation(gu_sc[:, 0:plen], g_sb[cb][:, 0:plen], AF.Square),
                                   after=[("act", pb.cnt["act"])])
                            pb.ins("dve", lambda e, plen=plen, cb=cb: e.scalar_tensor_tensor(
                                       gu_t2[:, 0:plen], gu_sc[:, 0:plen], GELU_K, g_sb[cb][:, 0:plen], ALU.add, ALU.mult),
                                   after=[("act", pb.cnt["act"]), ("dve", pb.cnt["dve"])])
                            pb.ins("act", lambda e, plen=plen: e.activation(gu_th[:, 0:plen], gu_t2[:, 0:plen], AF.Tanh, scale=GELU_C3),
                                   after=[("dve", pb.cnt["dve"]), ("act", pb.cnt["act"])])
                            pb.ins("dve", lambda e, plen=plen, cb=cb: e.scalar_tensor_tensor(
                                       gu_gl[:, 0:plen], gu_th[:, 0:plen], 1.0, g_sb[cb][:, 0:plen], ALU.add, ALU.mult),
                                   after=[("act", pb.cnt["act"]), ("dve", pb.cnt["dve"])])
                            ups_dve[pp] = pb.ins("dve", lambda e, plen=plen, ng128=ng128, poff=poff, tclen=tclen, cb=cb:
                                                 e.scalar_tensor_tensor(
                                                     inter[:, ng128 * tclen + poff: ng128 * tclen + poff + plen],
                                                     gu_gl[:, 0:plen], 0.5, u_sb[cb][:, 0:plen], ALU.mult, ALU.mult),
                                                 after=[("pe", u_pe), ("dve", pb.cnt["dve"]),
                                                        ("pe", inter_pe_reads if (s % n_gsup == 0 and ntl == 0) else 0)])
                inter_done = pb.cnt["dve"]
                for ts in range(ntt):
                    for hc in range(D // 512):
                        pp = (ts * (D // 512) + hc) % 2
                        for ntg in range(G // 128):
                            pb.ins("pe", lambda e, ts=ts, hc=hc, ntg=ntg, pp=pp, tclen=tclen:
                                   e.matmul(dps[pp][:],
                                            inter[:, ntg * tclen + ts * 128: ntg * tclen + ts * 128 + 128],
                                            wd[:, ntg * D + hc * 512: ntg * D + hc * 512 + 512],
                                            start=(ntg == 0), stop=(ntg == G // 128 - 1)),
                                   sem=(False if ntg < G // 128 - 1 else "pe"),
                                   after=([("dve", inter_done), ("qS", wd_loaded), ("dve", dps_dve[pp])]
                                          if ntg == 0 else []))
                        d_pe = pb.cnt["pe"]
                        dps_pe[pp] = d_pe
                        if g == 0:
                            dps_dve[pp] = pb.ins("dve", lambda e, ts=ts, hc=hc, pp=pp: e.tensor_copy(
                                                     yacc[:, ts * D + hc * 512: ts * D + hc * 512 + 512], dps[pp][:]),
                                                 after=[("pe", d_pe), ("dve", max(pb.cnt["dve"], y_stt_done))])
                        else:
                            dps_dve[pp] = pb.ins("dve", lambda e, ts=ts, hc=hc, pp=pp: e.tensor_tensor(
                                                     yacc[:, ts * D + hc * 512: ts * D + hc * 512 + 512],
                                                     yacc[:, ts * D + hc * 512: ts * D + hc * 512 + 512],
                                                     dps[pp][:], ALU.add),
                                                 after=[("pe", d_pe), ("dve", pb.cnt["dve"])])
                inter_pe_reads = pb.cnt["pe"]
                wd_free = pb.cnt["pe"]
            y_done = pb.cnt["dve"]

            for ts in range(ntt):
                bp = ts % 2
                r = row0 + ts * 128
                cd = pb.dma("qA", rres[bp][:], src[r:r + 128, :],
                            after=[("dve", ln_in_free[bp]), ("qA", ro_free[bp])])
                ln_in_free[bp] = pb.ins("dve", lambda e, ts=ts, bp=bp: e.scalar_tensor_tensor(
                                            rout[bp][:], yacc[:, ts * D:(ts + 1) * D], 0.5, rres[bp][:], ALU.mult, ALU.add),
                                        after=[("qA", cd), ("dve", y_done), ("qA", ro_free[bp]),
                                               ("dve", pb.cnt["dve"]), ("act", pb.cnt["act"]), ("pe", xbf_free[bp])])
                ro_free[bp] = pb.dma("qA", dst[r + dst_row_off:r + dst_row_off + 128, :], rout[bp][:],
                                     after=[("dve", pb.cnt["dve"])])
            y_stt_done = pb.cnt["dve"]
            if dq is not None and tci == 0:
                dq.finish()
        pb.drain_queues()
        pb.flush()


def emit_attn(pb, cfg, R1, R2, Wq, Wk, Wv, Wo, Wq_nat, Wk_nat, Wv_nat, Wo_nat, xaTd,
              qTd, aoTd,
              bq_in, bk_in, bv_in, bo_in, qn4_in, kn4_in, scaleb,
              m01a_in, fila_in, m01b_in, filb_in, cos_in, sin_in, e0, ident):
    nc = pb.nc
    D, NH, NKV, DK, DKV = cfg.D, cfg.NH, cfg.NKV, cfg.DK, cfg.DKV
    nHT, nT, T = cfg.nHT, cfg.nT, cfg.T
    SB = min(4, NH)
    n_sb = NH // SB

    with ExitStack() as outer:
        sbo = lambda n, s, d: outer.enter_context(nc.sbuf_tensor(n, s, d))
        kT = sbo("a_kT", [128, NKV * T], BF16)
        vsb = sbo("a_v", [128, nT * DKV], BF16)
        bqT = sbo("a_bqT", [128, D], BF16)
        bkT = sbo("a_bkT", [128, DKV], BF16)
        bvT = sbo("a_bvT", [128, DKV], BF16)
        boT = sbo("a_boT", [128, D], BF16)
        qn4 = sbo("a_qn4", [128, 512], F32)
        kn4 = sbo("a_kn4", [128, DKV], F32)
        m01a = sbo("a_m01a", [128, 256], F32)
        fila = sbo("a_fila", [128, 256], F32)
        m01b = sbo("a_m01b", [128, 256], F32)
        filb = sbo("a_filb", [128, 256], F32)

        pb.dma("qS", qn4[:], qn4_in)
        pb.dma("qS", kn4[:], kn4_in, bound=False)
        pb.dma("qS", m01a[:], m01a_in, bound=False)
        pb.dma("qS", fila[:], fila_in, bound=False)
        pb.dma("qS", m01b[:], m01b_in, bound=False)
        pb.dma("qS", filb[:], filb_in, bound=False)
        with ExitStack() as stp:
            b32 = stp.enter_context(nc.sbuf_tensor("a_b32", [128, D], F32))
            for btile, srcb, width in ((bqT, bq_in, D), (bkT, bk_in, DKV),
                                       (bvT, bv_in, DKV), (boT, bo_in, D)):
                pb.ins("dve", lambda e, t=btile: e.memset(t[:], 0.0),
                       after=[("dve", pb.cnt["dve"])])
                pb.dma("qS", b32[0:1, 0:width], srcb, after=[("dve", pb.cnt["dve"])])
                pb.ins("dve", lambda e, t=btile, w=width: e.tensor_copy(t[0:1, 0:w], b32[0:1, 0:w]),
                       after=[("qS", pb.cnt["qS"]), ("dve", pb.cnt["dve"])])
            pb.drain_queues()
            pb.flush()

        with ExitStack() as st0:
            sb0 = lambda n, s, d: st0.enter_context(nc.sbuf_tensor(n, s, d))
            wnat = [sb0(f"a_wn{i}", [128, D], BF16) for i in range(2)]
            wn_free = {0: 0, 1: 0}
            for wsrc, wdst, rows, blocked in ((Wq, Wq_nat, D, True), (Wo, Wo_nat, D, True),
                                              (Wk, Wk_nat, DKV, False), (Wv, Wv_nat, DKV, False)):
                for rt in range(rows // 128):
                    bp = rt % 2
                    pb.dma("qG", wnat[bp][:], wsrc[rt * 128:(rt + 1) * 128, :],
                           after=[("qS", wn_free[bp])])
                    if blocked:
                        ch = (rt * 128) // 512
                        o = (rt * 128) % 512
                        dst = wdst.rearrange("(a ht j) c -> a ht j c", ht=nHT, j=512)[ch, :, o:o + 128, :]
                        dst = dst.transpose([1, 0, 2])
                        wn_free[bp] = pb.dma("qS", dst, wnat[bp][:].rearrange("p (ht c) -> p ht c", c=128),
                                             after=[("qG", pb.cnt["qG"])])
                    else:
                        wn_free[bp] = pb.dma("qS", wdst[rt * 128:(rt + 1) * 128, :], wnat[bp][:],
                                             after=[("qG", pb.cnt["qG"])])

            wkT = sb0("a_wkT", [128, nHT * DKV], BF16)
            wvT = sb0("a_wvT", [128, nHT * DKV], BF16)
            for ht in range(nHT):
                pb.dma("qS", wkT[:, ht * DKV:(ht + 1) * DKV], Wk_nat[:, ht * 128:(ht + 1) * 128],
                       transpose=True, bound=(ht == 0))
            for ht in range(nHT):
                pb.dma("qS", wvT[:, ht * DKV:(ht + 1) * DKV], Wv_nat[:, ht * 128:(ht + 1) * 128],
                       transpose=True, bound=(ht == 0))

            xin = [sb0(f"a_xi{i}", [128, D], BF16) for i in range(2)]
            xc = sb0("a_xc", [128, D], F32)
            xbf = [sb0(f"a_xb{i}", [128, D], BF16) for i in range(2)]
            stt = sb0("a_st", [128, 4], F32)
            xstg = [sb0(f"a_xs{i}", [128, nHT * 128], BF16) for i in range(2)]
            trp = [st0.enter_context(nc.psum_tensor(f"a_tr0{i}", [128, 128], BF16)) for i in range(2)]

            ln_in_free = {0: 0, 1: 0}
            xbf_free = {0: 0, 1: 0}
            xstg_written = {0: 0, 1: 0}
            tr_evac = {0: 0, 1: 0}
            for i in range(nT):
                bp = i % 2
                r = i * 128
                cd = pb.dma("qA", xin[bp][:], R1[r:r + 128, :], after=[("dve", ln_in_free[bp])])
                pb.ins("dve", lambda e, b=bp: e.reduce_sum(stt[:, 0:1], xin[b][:], axis=AX.X),
                       after=[("qA", cd), ("dve", pb.cnt["dve"]), ("act", pb.cnt["act"])])
                pb.ins("dve", lambda e: e.tensor_scalar(stt[:, 0:1], stt[:, 0:1], 1.0 / D, None, ALU.mult),
                       after=[("dve", pb.cnt["dve"])])
                ln_in_free[bp] = pb.ins("dve", lambda e, b=bp: e.tensor_scalar(xc[:], xin[b][:], stt[:, 0:1], None, ALU.subtract),
                                        after=[("dve", pb.cnt["dve"]), ("act", pb.cnt["act"])])
                pb.ins("act", lambda e, b=bp: e.activation(xbf[b][:], xc[:], AF.Square, accum_out=stt[:, 1:2]),
                       after=[("dve", pb.cnt["dve"]), ("pe", xbf_free[bp])])
                pb.ins("act", lambda e: e.activation(stt[:, 2:3], stt[:, 1:2], AF.Sqrt, bias=LN_EPS, scale=1.0 / D),
                       after=[("act", pb.cnt["act"])])
                pb.ins("dve", lambda e: e.reciprocal(stt[:, 3:4], stt[:, 2:3]),
                       after=[("act", pb.cnt["act"]), ("dve", pb.cnt["dve"])])
                pb.ins("act", lambda e, b=bp: e.activation(xbf[b][:], xc[:], AF.Copy, scale=stt[:, 3:4]),
                       after=[("dve", pb.cnt["dve"])])
                xbf_done = pb.cnt["act"]
                for ht in range(nHT):
                    tp = (i * nHT + ht) % 2
                    xbf_free[bp] = pb.ins(
                        "pe", lambda e, ht=ht, tp=tp, b=bp: e.transpose(trp[tp][:], xbf[b][:, ht * 128:(ht + 1) * 128], ident[:]),
                        after=[("act", xbf_done), ("act", tr_evac[tp])])
                    tr_evac[tp] = pb.ins(
                        "act", lambda e, ht=ht, bp=bp, tp=tp: e.copy(
                            xstg[bp][:, ht * 128:(ht + 1) * 128], trp[tp][:]),
                        after=[("pe", pb.cnt["pe"]), ("qA", xstg_written[bp])])
                xdst = xaTd.rearrange("(ht p) t -> p ht t", p=128)[:, :, i * 128:(i + 1) * 128]
                xstg_written[bp] = pb.dma("qA", xdst,
                                          xstg[bp][:].rearrange("p (ht n) -> p ht n", n=128),
                                          after=[("act", tr_evac[0]), ("act", tr_evac[1])])
            pb.drain_queues()
            pb.flush()

            qn_nat = sb0("a_qn", [128, 512], F32)
            qrot = sb0("a_qrot", [128, 512], BF16)
            rst = sb0("a_rst", [128, 40], F32)
            rop = sb0("a_rop", [128, max(512, 2 * DKV)], F32)
            cos4 = sb0("a_cos", [128, 256], F32)
            sin4 = sb0("a_sin", [128, 256], F32)
            xas = [sb0(f"a_xa{i}", [128, nHT * 128], BF16) for i in range(2)]
            qps = st0.enter_context(nc.psum_tensor("a_qps", [128, 512], F32))

            xa_free = {0: 0, 1: 0}
            qrot_pe = 0
            tr_evac = {0: 0, 1: 0}
            kv_dve = 0
            for i in range(nT):
                bp = i % 2
                xsrc = xaTd.rearrange("(ht p) t -> p ht t", p=128)[:, :, i * 128:(i + 1) * 128]
                cd = pb.dma("qA", xas[bp][:].rearrange("p (ht n) -> p ht n", n=128), xsrc,
                            after=[("pe", xa_free[bp])])
                pb.dma("qA", cos4[:], cos_in[:, i * 256:(i + 1) * 256], after=[("dve", kv_dve)], bound=False)
                cs = pb.dma("qA", sin4[:], sin_in[:, i * 256:(i + 1) * 256], bound=False)
                for ht in range(nHT):
                    pb.ins("pe", lambda e, ht=ht, b=bp: e.matmul(qps[:, 0:DKV],
                               xas[b][:, ht * 128:(ht + 1) * 128],
                               wkT[:, ht * DKV:(ht + 1) * DKV],
                               start=(ht == 0), stop=False),
                           sem=False, after=[("qA", cd), ("qS", pb.cnt["qS"]), ("act", pb.cnt["act"])] if ht == 0 else [])
                pb.ins("pe", lambda e: e.matmul(qps[:, 0:DKV], e0[:], bkT[:, 0:DKV], start=False, stop=True))
                k_pe = pb.cnt["pe"]
                pb.ins("act", lambda e: e.copy(qn_nat[:, 0:DKV], qps[:, 0:DKV]),
                       after=[("pe", k_pe), ("dve", pb.cnt["dve"]), ("act", pb.cnt["act"])])
                pb.ins("act", lambda e: e.activation(rop[:, 0:DKV], qn_nat[:, 0:DKV], AF.Square),
                       after=[("act", pb.cnt["act"]), ("dve", pb.cnt["dve"])])
                pb.ins("dve", lambda e: e.reduce_sum(rst[:, 0:NKV].unsqueeze(2),
                                                     rop[:, 0:DKV].rearrange("p (h d) -> p h d", d=128), axis=AX.X),
                       after=[("act", pb.cnt["act"]), ("dve", pb.cnt["dve"])])
                pb.ins("act", lambda e: e.activation(rst[:, 4:4 + NKV], rst[:, 0:NKV], AF.Sqrt,
                                                     bias=RMS_EPS, scale=1.0 / DK),
                       after=[("dve", pb.cnt["dve"]), ("act", pb.cnt["act"])])
                pb.ins("dve", lambda e: e.reciprocal(rst[:, 0:NKV], rst[:, 4:4 + NKV]),
                       after=[("act", pb.cnt["act"]), ("dve", pb.cnt["dve"])])
                pb.ins("dve", lambda e: e.tensor_tensor(
                           qn_nat[:, 0:DKV].rearrange("p (h d) -> p h d", d=128),
                           qn_nat[:, 0:DKV].rearrange("p (h d) -> p h d", d=128),
                           rst[:, 0:NKV].unsqueeze(2).broadcast_to([128, NKV, 128]), ALU.mult),
                       after=[("dve", pb.cnt["dve"])])
                pb.ins("dve", lambda e: e.tensor_tensor(qn_nat[:, 0:DKV], qn_nat[:, 0:DKV], kn4[:, 0:DKV], ALU.mult),
                       after=[("dve", pb.cnt["dve"])])
                q1 = qn_nat[:, 0:DKV:2]
                q2 = qn_nat[:, 1:DKV:2]
                hw = DKV // 2
                pb.ins("dve", lambda e, q1=q1, hw=hw: e.tensor_tensor(rop[:, 0:hw], q1, cos4[:, 0:hw], ALU.mult),
                       after=[("qA", cs), ("dve", pb.cnt["dve"])])
                pb.ins("dve", lambda e, q2=q2, hw=hw: e.tensor_tensor(rop[:, hw:2 * hw], q2, sin4[:, 0:hw], ALU.mult))
                pb.ins("dve", lambda e, q1=q1, hw=hw: e.tensor_tensor(rop[:, 2 * hw:3 * hw], q1, sin4[:, 0:hw], ALU.mult))
                pb.ins("dve", lambda e, q2=q2, hw=hw: e.tensor_tensor(rop[:, 3 * hw:4 * hw], q2, cos4[:, 0:hw], ALU.mult),
                       after=[("dve", pb.cnt["dve"])])
                pb.ins("dve", lambda e, hw=hw: e.tensor_tensor(qrot[:, 0:DKV:2], rop[:, 0:hw], rop[:, hw:2 * hw], ALU.subtract),
                       after=[("dve", pb.cnt["dve"]), ("pe", qrot_pe)])
                pb.ins("dve", lambda e, hw=hw: e.tensor_tensor(qrot[:, 1:DKV:2], rop[:, 2 * hw:3 * hw], rop[:, 3 * hw:4 * hw], ALU.add),
                       after=[("dve", pb.cnt["dve"])])
                rope_done = pb.cnt["dve"]
                for kv in range(NKV):
                    tp = (i * NKV + kv) % 2
                    qrot_pe = pb.ins("pe", lambda e, kv=kv, tp=tp: e.transpose(trp[tp][:], qrot[:, kv * 128:(kv + 1) * 128], ident[:]),
                                     after=[("dve", rope_done), ("dve", tr_evac[tp])])
                    tr_evac[tp] = pb.ins("dve", lambda e, kv=kv, i=i, tp=tp: e.tensor_copy(
                                             kT[:, kv * T + i * 128: kv * T + i * 128 + 128], trp[tp][:]),
                                         after=[("pe", pb.cnt["pe"]), ("dve", pb.cnt["dve"])])
                kv_dve = pb.cnt["dve"]
                for ht in range(nHT):
                    pb.ins("pe", lambda e, ht=ht, b=bp: e.matmul(qps[:, 0:DKV],
                               xas[b][:, ht * 128:(ht + 1) * 128],
                               wvT[:, ht * DKV:(ht + 1) * DKV],
                               start=(ht == 0), stop=False),
                           sem=False, after=[("act", pb.cnt["act"])] if ht == 0 else [])
                xa_free[bp] = pb.ins("pe", lambda e: e.matmul(qps[:, 0:DKV], e0[:], bvT[:, 0:DKV], start=False, stop=True))
                pb.ins("act", lambda e, i=i: e.copy(vsb[:, i * DKV:(i + 1) * DKV], qps[:, 0:DKV]),
                       after=[("pe", pb.cnt["pe"]), ("act", pb.cnt["act"])])
            pb.drain_queues()
            pb.flush()

        with ExitStack() as sta:
            sba = lambda n, s, d: sta.enter_context(nc.sbuf_tensor(n, s, d))
            wqTs = sba("a_wq", [128, nHT * 512], BF16)
            xas = [sba(f"a_x3{i}", [128, nHT * 128], BF16) for i in range(2)]
            qn_nat = [sba(f"a_qn3{i}", [128, 512], F32) for i in range(2)]
            qrot = [sba(f"a_qr3{i}", [128, 512], BF16) for i in range(2)]
            qstg = [sba(f"a_qs3{i}", [128, 512], BF16) for i in range(2)]
            rst = [sba(f"a_rs3{i}", [128, 40], F32) for i in range(2)]
            rop = [sba(f"a_ro3{i}", [128, 512], F32) for i in range(2)]
            cos4 = [sba(f"a_co3{i}", [128, 256], F32) for i in range(2)]
            sin4 = [sba(f"a_si3{i}", [128, 256], F32) for i in range(2)]
            trp = [sta.enter_context(nc.psum_tensor(f"a_tr3{i}", [128, 128], BF16)) for i in range(2)]
            qps = [sta.enter_context(nc.psum_tensor(f"a_qp3{i}", [128, 512], F32)) for i in range(2)]

            tr_evac = {0: 0, 1: 0}
            qrot_pe = {0: 0, 1: 0}
            xa_free = {0: 0, 1: 0}
            qps_act = {0: 0, 1: 0}
            qstg_written = {0: 0, 1: 0}

            def a3a_mms(ch, bt, ib, wq_c):
                xsrc = xaTd.rearrange("(ht p) t -> p ht t", p=128)[:, :, bt * 128:(bt + 1) * 128]
                cdx = pb.dma("qA", xas[ib][:].rearrange("p (ht n) -> p ht n", n=128), xsrc,
                             after=[("pe", xa_free[ib])])
                pb.dma("qA", cos4[ib][:], cos_in[:, bt * 256:(bt + 1) * 256],
                       after=[("dve", qrot_pe.get("c" + str(ib), 0))], bound=False)
                pb.dma("qA", sin4[ib][:], sin_in[:, bt * 256:(bt + 1) * 256], bound=False)
                cs = pb.cnt["qA"]
                for ht in range(nHT):
                    pb.ins("pe", lambda e, ht=ht, ib=ib: e.matmul(
                               qps[ib][:],
                               xas[ib][:, ht * 128:(ht + 1) * 128],
                               wqTs[:, ht * 512:(ht + 1) * 512],
                               start=(ht == 0), stop=False),
                           sem=False,
                           after=[("qS", wq_c), ("qA", cdx), ("act", qps_act[ib])] if ht == 0 else [])
                pb.ins("pe", lambda e, ch=ch, ib=ib: e.matmul(qps[ib][:], e0[:], bqT[:, ch * 512:(ch + 1) * 512], start=False, stop=True))
                xa_free[ib] = pb.cnt["pe"]
                return pb.cnt["pe"], cs

            def a3a_chain(ch, bt, ib, q_pe, cs):
                pb.ins("act", lambda e, ib=ib: e.copy(qn_nat[ib][:], qps[ib][:]),
                       after=[("pe", q_pe), ("dve", pb.cnt["dve"]), ("act", pb.cnt["act"])])
                qps_act[ib] = pb.cnt["act"]
                pb.ins("act", lambda e, ib=ib: e.activation(rop[ib][:], qn_nat[ib][:], AF.Square),
                       after=[("act", pb.cnt["act"]), ("dve", pb.cnt["dve"])])
                pb.ins("dve", lambda e, ib=ib: e.reduce_sum(rst[ib][:, 0:4].unsqueeze(2),
                                                            rop[ib][:].rearrange("p (h d) -> p h d", d=128), axis=AX.X),
                       after=[("act", pb.cnt["act"]), ("dve", pb.cnt["dve"])])
                pb.ins("act", lambda e, ib=ib: e.activation(rst[ib][:, 4:8], rst[ib][:, 0:4], AF.Sqrt,
                                                            bias=RMS_EPS, scale=1.0 / DK),
                       after=[("dve", pb.cnt["dve"]), ("act", pb.cnt["act"])])
                pb.ins("dve", lambda e, ib=ib: e.reciprocal(rst[ib][:, 0:4], rst[ib][:, 4:8]),
                       after=[("act", pb.cnt["act"]), ("dve", pb.cnt["dve"])])
                pb.ins("dve", lambda e, ib=ib: e.tensor_tensor(
                           qn_nat[ib][:].rearrange("p (h d) -> p h d", d=128),
                           qn_nat[ib][:].rearrange("p (h d) -> p h d", d=128),
                           rst[ib][:, 0:4].unsqueeze(2).broadcast_to([128, 4, 128]), ALU.mult),
                       after=[("dve", pb.cnt["dve"])])
                pb.ins("dve", lambda e, ib=ib: e.tensor_tensor(qn_nat[ib][:], qn_nat[ib][:], qn4[:], ALU.mult),
                       after=[("dve", pb.cnt["dve"])])
                q1 = qn_nat[ib][:, 0:512:2]
                q2 = qn_nat[ib][:, 1:512:2]
                pb.ins("dve", lambda e, q1=q1, ib=ib: e.tensor_tensor(rop[ib][:, 0:256], q1, cos4[ib][:], ALU.mult),
                       after=[("qA", cs), ("dve", pb.cnt["dve"])])
                pb.ins("dve", lambda e, q2=q2, ib=ib: e.tensor_tensor(rop[ib][:, 256:512], q2, sin4[ib][:], ALU.mult))
                pb.ins("dve", lambda e, ib=ib: e.tensor_tensor(rop[ib][:, 0:256], rop[ib][:, 0:256], rop[ib][:, 256:512], ALU.subtract),
                       after=[("dve", pb.cnt["dve"]), ("pe", qrot_pe[ib])])
                pb.ins("dve", lambda e, ib=ib: e.tensor_copy(qrot[ib][:, 0:512:2], rop[ib][:, 0:256]),
                       after=[("dve", pb.cnt["dve"])])
                pb.ins("dve", lambda e, q1=q1, ib=ib: e.tensor_tensor(rop[ib][:, 0:256], q1, sin4[ib][:], ALU.mult),
                       after=[("dve", pb.cnt["dve"])])
                pb.ins("dve", lambda e, q2=q2, ib=ib: e.tensor_tensor(rop[ib][:, 256:512], q2, cos4[ib][:], ALU.mult))
                pb.ins("dve", lambda e, ib=ib: e.tensor_tensor(qrot[ib][:, 1:512:2], rop[ib][:, 0:256], rop[ib][:, 256:512], ALU.add),
                       after=[("dve", pb.cnt["dve"])])
                rope_done = pb.cnt["dve"]
                qrot_pe["c" + str(ib)] = rope_done
                for h4 in range(4):
                    tp = h4 % 2
                    qrot_pe[ib] = pb.ins("pe", lambda e, h4=h4, tp=tp, ib=ib: e.transpose(
                                             trp[tp][:], qrot[ib][:, h4 * 128:(h4 + 1) * 128], ident[:]),
                                         after=[("dve", rope_done), ("act", tr_evac[tp])])
                    tr_evac[tp] = pb.ins("act", lambda e, h4=h4, tp=tp, ib=ib: e.copy(
                                             qstg[ib][:, h4 * 128:(h4 + 1) * 128], trp[tp][:]),
                                         after=[("pe", pb.cnt["pe"]), ("act", pb.cnt["act"]),
                                                ("qA", qstg_written[ib])])
                qstg_written[ib] = pb.dma("qA", qTd[bt * 128:(bt + 1) * 128, ch * 512:(ch + 1) * 512], qstg[ib][:],
                                          after=[("act", tr_evac[0]), ("act", tr_evac[1])])

            for ch in range(NH // 4):
                rows = nHT * 512
                pb.dma("qS", wqTs[:], Wq_nat[ch * rows:(ch + 1) * rows, :],
                       transpose=True, after=[("pe", pb.cnt["pe"])])
                wq_c = pb.cnt["qS"]
                pend = None
                for bt in range(cfg.halo // 128, nT):
                    ib = bt % 2
                    cur = (ch, bt, ib) + a3a_mms(ch, bt, ib, wq_c)
                    if pend is not None:
                        a3a_chain(pend[0], pend[1], pend[2], pend[3], pend[4])
                    pend = cur
                a3a_chain(pend[0], pend[1], pend[2], pend[3], pend[4])
            pb.drain_queues()
            pb.flush()

        with ExitStack() as stb:
            sbb = lambda n, s, d: stb.enter_context(nc.sbuf_tensor(n, s, d))
            qTt = [sbb(f"a_qT3{i}", [128, NH * 128], BF16) for i in range(2)]
            aoT = sbb("a_aoT", [128, NH * 128], BF16)
            sm1 = [sbb(f"a_sm1{i}", [128, SB * 256], F32) for i in range(2)]
            smE = [sbb(f"a_smE{i}", [128, SB * 256], F32) for i in range(2)]
            smA = [sbb(f"a_smA{i}", [128, SB * 256], BF16) for i in range(2)]
            rst = [sbb(f"a_rsb{i}", [128, 40], F32) for i in range(2)]
            atc = [sbb(f"a_at{i}", [128, 128], BF16) for i in range(2)]
            trp = [stb.enter_context(nc.psum_tensor(f"a_trb{i}", [128, 128], BF16)) for i in range(2)]
            sps = [stb.enter_context(nc.psum_tensor(f"a_sp3{i}", [128, SB * 256], F32)) for i in range(2)]
            ops_ = [stb.enter_context(nc.psum_tensor(f"a_op{i}", [128, 128], F32)) for i in range(2)]

            tr_evac = {0: 0, 1: 0}
            smA_pe = {0: 0, 1: 0}
            sps_dve = {0: 0, 1: 0}
            ops_act = {0: 0, 1: 0}
            ao_written = 0
            qT_free = {0: 0, 1: 0}
            qT_loaded = {0: 0, 1: 0}

            def a3b_smm(bt, sb_, ib):
                if sb_ == 0:
                    qT_loaded[bt % 2] = pb.dma("qS", qTt[bt % 2][:], qTd[bt * 128:(bt + 1) * 128, :],
                                               after=[("pe", qT_free[bt % 2])])
                w0 = (bt - 1) * 128
                for h8 in range(SB):
                    h = sb_ * SB + h8
                    kv = h // (NH // NKV)
                    pb.ins("pe", lambda e, h=h, kv=kv, h8=h8, ib=ib, w0=w0, tb=bt % 2: e.matmul(
                               sps[ib][:, h8 * 256:(h8 + 1) * 256], qTt[tb][:, h * 128:(h + 1) * 128],
                               kT[:, kv * T + w0: kv * T + w0 + 256], start=True, stop=True),
                           sem=(False if h8 < SB - 1 else "pe"),
                           after=[("qS", qT_loaded[bt % 2]), ("dve", sps_dve[ib])] if h8 == 0 else [])
                qT_free[bt % 2] = pb.cnt["pe"]
                return pb.cnt["pe"]

            def a3b_chain_av(bt, sb_, ib, s_pe):
                m01 = m01b if bt == cfg.halo // 128 else m01a
                fil = filb if bt == cfg.halo // 128 else fila
                sm1v = sm1[ib][:].rearrange("p (h k) -> p h k", k=256)
                smEv = smE[ib][:].rearrange("p (h k) -> p h k", k=256)
                smAv = smA[ib][:].rearrange("p (h k) -> p h k", k=256)
                pb.ins("dve", lambda e, m01=m01, ib=ib, sm1v=sm1v: e.scalar_tensor_tensor(
                           sm1v, sps[ib][:].rearrange("p (h k) -> p h k", k=256), scaleb[:, 0:1],
                           m01[:].unsqueeze(1).broadcast_to([128, SB, 256]), ALU.mult, ALU.mult),
                       after=[("pe", s_pe), ("dve", pb.cnt["dve"]), ("act", pb.cnt["act"])])
                sps_dve[ib] = pb.cnt["dve"]
                pb.ins("dve", lambda e, fil=fil, sm1v=sm1v: e.tensor_tensor(
                           sm1v, sm1v, fil[:].unsqueeze(1).broadcast_to([128, SB, 256]), ALU.add),
                       after=[("dve", pb.cnt["dve"])])
                pb.ins("dve", lambda e, ib=ib, sm1v=sm1v: e.reduce_max(rst[ib][:, 8:8 + SB].unsqueeze(2), sm1v, axis=AX.X, negate=True),
                       after=[("dve", pb.cnt["dve"])])
                pb.ins("dve", lambda e, ib=ib, sm1v=sm1v: e.tensor_tensor(
                           sm1v, sm1v, rst[ib][:, 8:8 + SB].unsqueeze(2).broadcast_to([128, SB, 256]), ALU.add),
                       after=[("dve", pb.cnt["dve"])])
                pb.ins("act", lambda e, ib=ib: e.activation(smE[ib][:], sm1[ib][:], AF.Exp),
                       after=[("dve", pb.cnt["dve"]), ("act", pb.cnt["act"])])
                pb.ins("dve", lambda e, ib=ib, smEv=smEv: e.reduce_sum(rst[ib][:, 16:16 + SB].unsqueeze(2), smEv, axis=AX.X),
                       after=[("act", pb.cnt["act"]), ("dve", pb.cnt["dve"])])
                pb.ins("dve", lambda e, ib=ib: e.reciprocal(rst[ib][:, 24:24 + SB], rst[ib][:, 16:16 + SB]),
                       after=[("dve", pb.cnt["dve"])])
                pb.ins("dve", lambda e, ib=ib, smAv=smAv, smEv=smEv: e.tensor_tensor(
                           smAv, smEv, rst[ib][:, 24:24 + SB].unsqueeze(2).broadcast_to([128, SB, 256]), ALU.mult),
                       after=[("dve", pb.cnt["dve"]), ("pe", smA_pe[ib])])
                A_done = pb.cnt["dve"]
                for h8 in range(SB):
                    h = sb_ * SB + h8
                    kv = h // (NH // NKV)
                    for half in range(2):
                        tp = half
                        smA_pe[ib] = pb.ins("pe", lambda e, h8=h8, half=half, tp=tp, ib=ib: e.transpose(
                                                trp[tp][:], smA[ib][:, h8 * 256 + half * 128: h8 * 256 + (half + 1) * 128], ident[:]),
                                            after=[("dve", A_done), ("act", tr_evac[tp])])
                        tr_evac[tp] = pb.ins("act", lambda e, half=half, tp=tp: e.copy(atc[half][:], trp[tp][:]),
                                             after=[("pe", pb.cnt["pe"]), ("act", pb.cnt["act"])])
                    at_done = pb.cnt["act"]
                    op = h % 2
                    for half in range(2):
                        blk_ = bt - 1 + half
                        pb.ins("pe", lambda e, half=half, blk_=blk_, kv=kv, op=op: e.matmul(
                                   ops_[op][:], vsb[:, blk_ * DKV + kv * 128: blk_ * DKV + kv * 128 + 128],
                                   atc[half][:], start=(half == 0), stop=(half == 1)),
                               sem=(False if half == 0 else "pe"),
                               after=[("act", at_done), ("act", ops_act[op])] if half == 0 else [])
                    ops_act[op] = pb.ins("act", lambda e, h=h, op=op: e.copy(aoT[:, h * 128:(h + 1) * 128], ops_[op][:]),
                                         after=[("pe", pb.cnt["pe"]), ("act", pb.cnt["act"]),
                                                ("qA", ao_written if h == 0 else 0)])

            batches = [(bt, sb_) for bt in range(cfg.halo // 128, nT) for sb_ in range(n_sb)]
            pend = None
            for idx, (bt, sb_) in enumerate(batches):
                ib = idx % 2
                s_pe = a3b_smm(bt, sb_, ib)
                if pend is not None:
                    a3b_chain_av(*pend)
                    if pend[1] == n_sb - 1:
                        ao_written = pb.dma("qA", aoTd[pend[0] * 128:(pend[0] + 1) * 128, :], aoT[:],
                                            after=[("act", pb.cnt["act"])])
                pend = (bt, sb_, ib, s_pe)
            a3b_chain_av(*pend)
            ao_written = pb.dma("qA", aoTd[pend[0] * 128:(pend[0] + 1) * 128, :], aoT[:],
                                after=[("act", pb.cnt["act"])])
            pb.drain_queues()
            pb.flush()

        with ExitStack() as stc:
            sbc = lambda n, s, d: stc.enter_context(nc.sbuf_tensor(n, s, d))
            woS = sbc("a_wo", [128, nHT * 512], BF16)
            aoc = [sbc(f"a_ao{i}", [128, NH * 128], BF16) for i in range(2)]
            rres = [sbc(f"a_rr{i}", [128, 512], BF16) for i in range(2)]
            rout = [sbc(f"a_rx{i}", [128, 512], BF16) for i in range(2)]
            pps = [stc.enter_context(nc.psum_tensor(f"a_pp{i}", [128, 512], F32)) for i in range(2)]

            ao_free = {0: 0, 1: 0}
            rr_free = {0: 0, 1: 0}
            ro_free = {0: 0, 1: 0}
            pps_dve = {0: 0, 1: 0}
            for hc in range(D // 512):
                rows = nHT * 512
                pb.dma("qS", woS[:], Wo_nat[hc * rows:(hc + 1) * rows, :],
                       transpose=True, after=[("pe", pb.cnt["pe"])])
                wo_c = pb.cnt["qS"]
                for b in range(cfg.halo // 128, nT):
                    bp = b % 2
                    r = b * 128
                    cda = pb.dma("qS", aoc[bp][:], aoTd[r:r + 128, :], after=[("pe", ao_free[bp])])
                    cd = pb.dma("qA", rres[bp][:], R1[r:r + 128, hc * 512:(hc + 1) * 512],
                                after=[("dve", rr_free[bp])])
                    pp = bp
                    for dot in range(nHT):
                        pb.ins("pe", lambda e, dot=dot, pp=pp, bp=bp: e.matmul(
                                   pps[pp][:], aoc[bp][:, dot * 128:(dot + 1) * 128],
                                   woS[:, dot * 512:(dot + 1) * 512],
                                   start=(dot == 0), stop=False),
                               sem=False,
                               after=[("qS", max(wo_c, cda)), ("dve", pps_dve[pp])] if dot == 0 else [])
                    pb.ins("pe", lambda e, hc=hc, pp=pp: e.matmul(pps[pp][:], e0[:], boT[:, hc * 512:(hc + 1) * 512],
                                                                  start=False, stop=True))
                    ao_free[bp] = pb.cnt["pe"]
                    rr_free[bp] = pps_dve[pp] = pb.ins(
                        "dve", lambda e, pp=pp, bp=bp: e.scalar_tensor_tensor(
                            rout[bp][:], pps[pp][:], 1.0, rres[bp][:], ALU.mult, ALU.add),
                        after=[("pe", pb.cnt["pe"]), ("qA", cd), ("dve", pb.cnt["dve"]), ("qA", ro_free[bp])])
                    ro_free[bp] = pb.dma("qA", R2[r:r + 128, hc * 512:(hc + 1) * 512], rout[bp][:],
                                         after=[("dve", pb.cnt["dve"])])
            pb.drain_queues()
            pb.flush()



def _host_inputs(cfg, core, inputs, n_cores=8):
    import ml_dtypes
    D, T, NH, NKV, DK = cfg.D, cfg.T, cfg.NH, cfg.NKV, cfg.DK
    x = inputs["L0_attn_x"]
    B, S, _ = x.shape
    per_seq = (n_cores // B)
    seq = core // per_seq
    start = (core % per_seq) * cfg.T_own
    lo = start - cfg.halo
    if lo < 0:
        pad = np.zeros((-lo, D), dtype=np.float32)
        xs = np.concatenate([pad, x[seq, 0:start + cfg.T_own]], axis=0)
    else:
        xs = x[seq, lo:start + cfg.T_own]
    xs = np.ascontiguousarray(xs, dtype=np.float32)

    i_ = np.arange(128)[:, None]
    j_ = np.arange(256)[None, :]
    okA = (j_ - i_ >= 64) & (j_ - i_ <= 128)
    okB = okA & (j_ >= 128)
    m01a = okA.astype(np.float32)
    fila = np.where(okA, 0.0, MASK_FILL).astype(np.float32)
    first = (start == 0)
    m01b = (okB if first else okA).astype(np.float32)
    filb = np.where(okB if first else okA, 0.0, MASK_FILL).astype(np.float32)

    pos = np.arange(lo, start + cfg.T_own, dtype=np.int64)
    pos = np.maximum(pos, 0).astype(np.float32)
    inv_freq = (1.0 / (ROPE_BASE ** (np.arange(0, DK, 2, dtype=np.float32) / DK))).astype(np.float32)
    freqs = (pos[:, None] * inv_freq[None, :]).astype(np.float32)
    cost = np.cos(freqs).astype(np.float32)
    sint = np.sin(freqs).astype(np.float32)
    nTt = cfg.nT
    cost = np.tile(cost.reshape(nTt, 128, 1, 64), (1, 1, 4, 1)).transpose(1, 0, 2, 3).reshape(128, nTt * 256)
    sint = np.tile(sint.reshape(nTt, 128, 1, 64), (1, 1, 4, 1)).transpose(1, 0, 2, 3).reshape(128, nTt * 256)

    qn4 = np.broadcast_to(np.tile(inputs["L0_attn_qn_w"], 4)[None, :], (128, 512))
    kn4 = np.broadcast_to(np.tile(inputs["L0_attn_kn_w"], NKV)[None, :], (128, NKV * DK))
    scl = np.broadcast_to(np.asarray(inputs["L0_attn_scale"], dtype=np.float32).reshape(1, 1), (128, 1))

    e0 = np.zeros((128, 128), dtype=np.float32)
    e0[0, :] = 1.0

    return {
        "x": xs,
        "gW": inputs["L0_ffn_gate_W_packed"], "gS": inputs["L0_ffn_gate_scales"],
        "uW": inputs["L0_ffn_up_W_packed"], "uS": inputs["L0_ffn_up_scales"],
        "dW": inputs["L0_ffn_down_W_packed"], "dS": inputs["L0_ffn_down_scales"],
        "Wq": inputs["L0_attn_Wq"], "Wk": inputs["L0_attn_Wk"],
        "Wv": inputs["L0_attn_Wv"], "Wo": inputs["L0_attn_Wo"],
        "bq": inputs["L0_attn_bq"].reshape(1, -1), "bk": inputs["L0_attn_bk"].reshape(1, -1),
        "bv": inputs["L0_attn_bv"].reshape(1, -1), "bo": inputs["L0_attn_bo"].reshape(1, -1),
        "qn4": np.ascontiguousarray(qn4, dtype=np.float32),
        "kn4": np.ascontiguousarray(kn4, dtype=np.float32),
        "scl": np.ascontiguousarray(scl, dtype=np.float32),
        "ident": np.eye(128, dtype=np.float32).astype(ml_dtypes.bfloat16),
        "e0": e0.astype(ml_dtypes.bfloat16),
        "m01a": m01a, "fila": fila, "m01b": m01b, "filb": filb,
        "cost": np.ascontiguousarray(cost), "sint": np.ascontiguousarray(sint),
    }


def kernel(**inputs):
    cfg = CFG
    n_cores = 8
    in_maps = [_host_inputs(cfg, c, inputs) for c in range(n_cores)]
    nc = build_program(cfg, None)
    res = run_bass_kernel_spmd(nc, in_maps, core_ids=list(range(n_cores)))
    x = inputs["L0_attn_x"]
    B, S, D = x.shape
    out = np.empty((B, S, D), dtype=np.float32)
    per_seq = n_cores // B
    for c in range(n_cores):
        seq = c // per_seq
        start = (c % per_seq) * cfg.T_own
        out[seq, start:start + cfg.T_own] = res.results[c]["out"]
    return out


# version 25
# speedup vs baseline: 1.1337x; 1.0365x over previous
import numpy as np
from contextlib import ExitStack
import concourse.bass as bass
import concourse.mybir as mybir
from concourse.bass_utils import run_bass_kernel_spmd

F32 = mybir.dt.float32
BF16 = mybir.dt.bfloat16
I32 = mybir.dt.int32
AF = mybir.ActivationFunctionType
ALU = mybir.AluOpType
AX = mybir.AxisListType

GELU_C1 = 0.7978845608028654
GELU_C3 = 0.7978845608028654 * 0.044715
GELU_K = GELU_C1 / GELU_C3

LN_EPS = 1e-5
RMS_EPS = 1e-8
MASK_FILL = -10000.0
ROPE_BASE = 10000.0


class Cfg:
    def __init__(self, D=2048, NMLP=16384, NH=16, NKV=4, T_own=2048, halo=128,
                 n_super=256, G_super=4):
        self.D = D
        self.NMLP = NMLP
        self.NH = NH
        self.NKV = NKV
        self.DK = 128
        self.DKV = NKV * self.DK
        self.T_own = T_own
        self.halo = halo
        self.T = T_own + halo
        self.nT = self.T // 128
        self.nHT = D // 128
        self.nNT = NMLP // 128
        self.n_super = n_super
        self.G = G_super * n_super
        self.ffn1_tc = self._chunks(0, self.T)
        self.ffn2_tc = self._chunks(halo, T_own)

    def _chunks(self, start, total):
        out = []
        pos = start
        left = total
        while left > 0:
            c = min(768, left)
            out.append((pos, c))
            pos += c
            left -= c
        return out

    @staticmethod
    def pieces(length):
        n = (length + 511) // 512
        assert length % n == 0
        out = []
        pos = 0
        for i in range(n):
            ln = length // n
            out.append((pos, ln))
            pos += ln
        return out


CFG = Cfg()



ENG_OF_Q = {"qS": "sp", "qA": "act", "qG": "pool"}


class PB:

    def __init__(self, nc):
        self.nc = nc
        self.ctx = ExitStack()
        self.sems = {}
        self.cnt = {}
        self.pend = {k: [] for k in ("pe", "act", "dve", "pool", "sp")}
        self.lastw = {}
        for k in ("pe", "act", "dve", "pool", "qS", "qA", "qG"):
            self.sems[k] = self.ctx.enter_context(nc.semaphore(k))
            self.cnt[k] = 0

    def ins(self, eng, make, sem=None, after=()):
        for s, v in after:
            self.wait(eng, s, v)
        if sem is None:
            sem = eng
        if sem is False:
            self.pend[eng].append(lambda e, make=make: make(e))
            return None
        def f(e, make=make, sem=sem):
            make(e).then_inc(self.sems[sem], 1)
        self.pend[eng].append(f)
        self.cnt[sem] += 1
        return self.cnt[sem]

    def raw(self, eng, make, after=()):
        for s, v in after:
            self.wait(eng, s, v)
        self.pend[eng].append(lambda e, make=make: make(e))

    def dma(self, q, out_ap, in_ap, after=(), bound=True, **kw):
        eng = ENG_OF_Q[q]
        if bound:
            self.wait(eng, q, self.cnt[q])
        for s, v in after:
            self.wait(eng, s, v)
        def f(e, o=out_ap, i=in_ap, kw=kw, q=q):
            e.dma_start(o, i, **kw).then_inc(self.sems[q], 16)
        self.pend[eng].append(f)
        self.cnt[q] += 16
        return self.cnt[q]

    def wait(self, eng, sem, val, exact=False):
        if val is None or val <= 0:
            return
        if sem.endswith("!"):
            sem = sem[:-1]
            exact = True
        if sem in ("qS", "qA", "qG") and not exact:
            val = self.cnt[sem]
        key = (eng, sem)
        if self.lastw.get(key, 0) >= val:
            return
        self.lastw[key] = val
        def f(e, s=sem, v=val):
            e.wait_ge(self.sems[s], v)
        self.pend[eng].append(f)

    def flush(self):
        pend, self.pend = self.pend, {k: [] for k in self.pend}
        if not any(pend.values()):
            return
        with self.nc.Block() as blk:
            if pend["sp"]:
                @blk.sync
                def _(e, ops=pend["sp"]):
                    for f in ops:
                        f(e)
            if pend["pe"]:
                @blk.tensor
                def _(e, ops=pend["pe"]):
                    for f in ops:
                        f(e)
            if pend["act"]:
                @blk.scalar
                def _(e, ops=pend["act"]):
                    for f in ops:
                        f(e)
            if pend["dve"]:
                @blk.vector
                def _(e, ops=pend["dve"]):
                    for f in ops:
                        f(e)
            if pend["pool"]:
                @blk.gpsimd
                def _(e, ops=pend["pool"]):
                    for f in ops:
                        f(e)
        for s in ("pe", "act", "dve", "pool"):
            for e2 in ("pe", "act", "dve", "pool", "sp"):
                self.lastw[(e2, s)] = self.cnt[s]

    def drain_queues(self):
        for q in ("qS", "qA", "qG"):
            self.wait(ENG_OF_Q[q], q, self.cnt[q])

    def sb(self, name, shape, dtype):
        return self.ctx.enter_context(self.nc.sbuf_tensor(name, shape, dtype))

    def ps(self, name, shape, dtype=F32):
        return self.ctx.enter_context(self.nc.psum_tensor(name, shape, dtype))




def build_program(cfg, first_tile_masked):
    nc = bass.Bass("TRN2", target_bir_lowering=False)
    D, NMLP, T = cfg.D, cfg.NMLP, cfg.T
    nHT, nNT, nT = cfg.nHT, cfg.nNT, cfg.nT
    NH, NKV, DK, DKV = cfg.NH, cfg.NKV, cfg.DK, cfg.DKV

    x_in = nc.dram_tensor("x", [T, D], F32, kind="ExternalInput").ap()
    gW = nc.dram_tensor("gW", [NMLP, D // 2], I32, kind="ExternalInput").ap()
    gS = nc.dram_tensor("gS", [NMLP, D // 32], F32, kind="ExternalInput").ap()
    uW = nc.dram_tensor("uW", [NMLP, D // 2], I32, kind="ExternalInput").ap()
    uS = nc.dram_tensor("uS", [NMLP, D // 32], F32, kind="ExternalInput").ap()
    dW = nc.dram_tensor("dW", [D, NMLP // 2], I32, kind="ExternalInput").ap()
    dS = nc.dram_tensor("dS", [D, NMLP // 32], F32, kind="ExternalInput").ap()
    Wq = nc.dram_tensor("Wq", [D, D], F32, kind="ExternalInput").ap()
    Wk = nc.dram_tensor("Wk", [DKV, D], F32, kind="ExternalInput").ap()
    Wv = nc.dram_tensor("Wv", [DKV, D], F32, kind="ExternalInput").ap()
    Wo = nc.dram_tensor("Wo", [D, D], F32, kind="ExternalInput").ap()
    bq_in = nc.dram_tensor("bq", [1, D], F32, kind="ExternalInput").ap()
    bk_in = nc.dram_tensor("bk", [1, DKV], F32, kind="ExternalInput").ap()
    bv_in = nc.dram_tensor("bv", [1, DKV], F32, kind="ExternalInput").ap()
    bo_in = nc.dram_tensor("bo", [1, D], F32, kind="ExternalInput").ap()
    qn4_in = nc.dram_tensor("qn4", [128, 512], F32, kind="ExternalInput").ap()
    kn4_in = nc.dram_tensor("kn4", [128, DKV], F32, kind="ExternalInput").ap()
    scale_in = nc.dram_tensor("scl", [128, 1], F32, kind="ExternalInput").ap()
    ident_in = nc.dram_tensor("ident", [128, 128], BF16, kind="ExternalInput").ap()
    e0_in = nc.dram_tensor("e0", [128, 128], BF16, kind="ExternalInput").ap()
    m01a_in = nc.dram_tensor("m01a", [128, 256], F32, kind="ExternalInput").ap()
    fila_in = nc.dram_tensor("fila", [128, 256], F32, kind="ExternalInput").ap()
    m01b_in = nc.dram_tensor("m01b", [128, 256], F32, kind="ExternalInput").ap()
    filb_in = nc.dram_tensor("filb", [128, 256], F32, kind="ExternalInput").ap()
    cos_in = nc.dram_tensor("cost", [128, nT * 256], F32, kind="ExternalInput").ap()
    sin_in = nc.dram_tensor("sint", [128, nT * 256], F32, kind="ExternalInput").ap()
    out = nc.dram_tensor("out", [cfg.T_own, D], F32, kind="ExternalOutput").ap()

    Wg_nat = nc.dram_tensor("WgN", [NMLP * (D // 128), 128], BF16, kind="Internal").ap()
    Wu_nat = nc.dram_tensor("WuN", [NMLP * (D // 128), 128], BF16, kind="Internal").ap()
    Wd_nat = nc.dram_tensor("WdN", [D, NMLP], BF16, kind="Internal").ap()
    Wq_nat = nc.dram_tensor("WqN", [D * (D // 128), 128], BF16, kind="Internal").ap()
    Wk_nat = nc.dram_tensor("WkN", [DKV, D], BF16, kind="Internal").ap()
    Wv_nat = nc.dram_tensor("WvN", [DKV, D], BF16, kind="Internal").ap()
    Wo_nat = nc.dram_tensor("WoN", [D * (D // 128), 128], BF16, kind="Internal").ap()
    xaTd = nc.dram_tensor("xaTd", [D, T], BF16, kind="Internal").ap()
    qTd = nc.dram_tensor("qTd", [T, NH * 128], BF16, kind="Internal").ap()
    aoTd = nc.dram_tensor("aoTd", [T, NH * 128], BF16, kind="Internal").ap()
    R1 = nc.dram_tensor("R1", [T, D], BF16, kind="Internal").ap()
    R2 = nc.dram_tensor("R2", [T, D], BF16, kind="Internal").ap()

    pb = PB(nc)

    ident = pb.sb("identS", [128, 128], BF16)
    e0 = pb.sb("e0S", [128, 128], BF16)
    scaleb = pb.sb("scaleS", [128, 1], F32)
    pb.ident = ident

    const_vals = sorted({LN_EPS, RMS_EPS, 1.0 / D, 1.0 / DK, GELU_C3, 1.0, 0.0})
    for ci, v in enumerate(const_vals):
        ct = pb.sb(f"cst{ci}", [128, 1], F32)
        pb.ins("pool", lambda e, t=ct, v=v: e.memset(t[:], float(v)), sem="pool")
        nc.const_aps.register(F32, float(v), ct[:, 0:1])

    pb.dma("qS", ident[:], ident_in)
    pb.dma("qS", e0[:], e0_in)
    pb.dma("qS", scaleb[:], scale_in)
    pb.drain_queues()
    pb.flush()

    dq = DequantDriver(pb, cfg, [
        (gW, gS, Wg_nat, NMLP, D, "g", True),
        (uW, uS, Wu_nat, NMLP, D, "u", True),
        (dW, dS, Wd_nat, D, NMLP, "d", False),
    ])
    dq.finish()
    pb.drain_queues()
    pb.flush()
    dq.ctx.close()

    emit_ffn_pass(pb, cfg, src=x_in, src_is_f32=True, dst=R1, dst_is_f32=False,
                  dst_row_off=0, tcs=cfg.ffn1_tc, WgT=Wg_nat, WuT=Wu_nat, WdT=Wd_nat, tag="f1")
    pb.flush()

    emit_attn(pb, cfg, R1=R1, R2=R2, Wq=Wq, Wk=Wk, Wv=Wv, Wo=Wo,
              Wq_nat=Wq_nat, Wk_nat=Wk_nat, Wv_nat=Wv_nat, Wo_nat=Wo_nat, xaTd=xaTd,
              qTd=qTd, aoTd=aoTd,
              bq_in=bq_in, bk_in=bk_in, bv_in=bv_in, bo_in=bo_in,
              qn4_in=qn4_in, kn4_in=kn4_in, scaleb=scaleb,
              m01a_in=m01a_in, fila_in=fila_in, m01b_in=m01b_in, filb_in=filb_in,
              cos_in=cos_in, sin_in=sin_in, e0=e0, ident=ident)
    pb.flush()

    emit_ffn_pass(pb, cfg, src=R2, src_is_f32=False, dst=out, dst_is_f32=True,
                  dst_row_off=-cfg.halo, tcs=cfg.ffn2_tc, WgT=Wg_nat, WuT=Wu_nat, WdT=Wd_nat, tag="f2")

    pb.drain_queues()
    pb.flush()
    pb.ctx.close()
    return nc


class DequantDriver:

    def __init__(self, pb, cfg, jobs):
        self.pb = pb
        self.cfg = cfg
        self.ctx = ExitStack()
        sb = lambda n, s, d: self.ctx.enter_context(pb.nc.sbuf_tensor(n, s, d))
        self.pk = [sb(f"dq_pk{i}", [128, 1024], I32) for i in range(2)]
        self.sc = [sb(f"dq_sc{i}", [128, 64], F32) for i in range(2)]
        self.blo2 = [sb(f"dq_bl{i}", [128, 1024], I32) for i in range(2)]
        self.bhi2 = [sb(f"dq_bh{i}", [128, 1024], I32) for i in range(2)]
        self.btmp = sb("dq_btmp", [128, 1024], I32)
        self.wn = [sb(f"dq_wn{i}", [128, 2048], BF16) for i in range(2)]
        self.pk_free = {0: 0, 1: 0}
        self.pool_lo = {0: 0, 1: 0}
        self.pool_hi = {0: 0, 1: 0}
        self.w_written = {0: 0, 1: 0}
        self.step = 0
        self.jobs = []
        self.emitted = []
        self.covered = []
        for packed, scales, w_nat, N, K, tag, blocked in jobs:
            KCH = min(K, 2048)
            k_chunks = K // KCH
            n_tiles = N // 128
            if blocked:
                work = [(nt, kc) for nt in range(n_tiles) for kc in range(k_chunks)]
            else:
                work = [(nt, kc) for kc in range(k_chunks) for nt in range(n_tiles)]
            self.jobs.append(dict(packed=packed, scales=scales, w_nat=w_nat, D=K,
                                  KCH=KCH, words=KCH // 2, k_chunks=k_chunks,
                                  blocked=blocked, work=work))
            self.emitted.append(0)
            self.covered.append([])

    def ensure(self, ji, n):
        J = self.jobs[ji]
        n = min(n, len(J["work"]))
        while self.emitted[ji] < n:
            nt, kc = J["work"][self.emitted[ji]]
            self._tile(J, nt, kc)
            self.covered[ji].append(self.pb.cnt["qA"])
            self.emitted[ji] += 1

    def cover_count(self, ji, items):
        items = min(items, len(self.jobs[ji]["work"]))
        if items <= 0:
            return 0
        return self.covered[ji][items - 1]

    def finish(self):
        for ji in range(len(self.jobs)):
            self.ensure(ji, 10 ** 9)

    def _tile(self, J, nt, kc):
        pb = self.pb
        KCH, words = J["KCH"], J["words"]
        bp = self.step % 2
        self.step += 1
        row0 = nt * 128
        pk, sc, wn = self.pk[bp], self.sc[bp], self.wn[bp]
        blo, bhi, btmp = self.blo2[bp], self.bhi2[bp], self.btmp
        W = words
        pb.dma("qS", pk[:, 0:W], J["packed"][row0:row0 + 128, kc * W:(kc + 1) * W],
               after=[("dve", self.pk_free[bp]), ("pool", self.pool_hi[bp])])
        pb.dma("qS", sc[:, 0:KCH // 32], J["scales"][row0:row0 + 128, kc * (KCH // 32):(kc + 1) * (KCH // 32)],
               bound=False)
        pb.ins("dve", lambda E, sc=sc, K2=KCH // 32: E.tensor_scalar(sc[:, 0:K2], sc[:, 0:K2], float(2.0 ** 126), None, ALU.mult),
               after=[("qS", pb.cnt["qS"]), ("dve", pb.cnt["dve"])])
        scv = sc[:, 0:KCH // 32].unsqueeze(2).broadcast_to([128, KCH // 32, 16])
        we = wn[:, 0:KCH:2].rearrange("p (a b) -> p a b", b=16)
        wo = wn[:, 1:KCH:2].rearrange("p (a b) -> p a b", b=16)
        pb.ins("dve", lambda E, pk=pk, blo=blo, W=W: E.tensor_scalar(blo[:, 0:W], pk[:, 0:W], 8, 28, ALU.bitwise_and, ALU.logical_shift_left),
               after=[("dve", pb.cnt["dve"]), ("pool", self.pool_lo[bp])])
        pb.ins("dve", lambda E, pk=pk, btmp=btmp, W=W: E.tensor_scalar(btmp[:, 0:W], pk[:, 0:W], 7, 22, ALU.bitwise_and, ALU.logical_shift_left),
               after=[("dve", pb.cnt["dve"])])
        lo_or = pb.ins("dve", lambda E, blo=blo, btmp=btmp, W=W: E.tensor_tensor(blo[:, 0:W], blo[:, 0:W], btmp[:, 0:W], ALU.bitwise_or),
                       after=[("dve", pb.cnt["dve"])])
        self.pool_lo[bp] = pb.ins(
            "pool", lambda E, we=we, scv=scv, blo=blo, W=W: E.tensor_tensor(we, blo[:, 0:W].bitcast(F32), scv, ALU.mult),
            after=[("dve", lo_or), ("qA!", self.w_written[bp]), ("pool", pb.cnt["pool"])])
        pb.ins("dve", lambda E, pk=pk, bhi=bhi, W=W: E.tensor_scalar(bhi[:, 0:W], pk[:, 0:W], 0x80, 24, ALU.bitwise_and, ALU.logical_shift_left),
               after=[("dve", pb.cnt["dve"]), ("pool", self.pool_hi[bp])])
        self.pk_free[bp] = pb.ins(
            "dve", lambda E, pk=pk, btmp=btmp, W=W: E.tensor_scalar(btmp[:, 0:W], pk[:, 0:W], 0x70, 18, ALU.bitwise_and, ALU.logical_shift_left),
            after=[("dve", pb.cnt["dve"])])
        hi_or = pb.ins("dve", lambda E, bhi=bhi, btmp=btmp, W=W: E.tensor_tensor(bhi[:, 0:W], bhi[:, 0:W], btmp[:, 0:W], ALU.bitwise_or),
                       after=[("dve", pb.cnt["dve"])])
        self.pool_hi[bp] = pb.ins(
            "pool", lambda E, wo=wo, scv=scv, bhi=bhi, W=W: E.tensor_tensor(wo, bhi[:, 0:W].bitcast(F32), scv, ALU.mult),
            after=[("dve", hi_or), ("pool", pb.cnt["pool"])])
        if J["blocked"]:
            NS = self.cfg.n_super
            nht_full = J["D"] // 128
            nht = KCH // 128
            sblk = row0 // NS
            o = row0 % NS
            dst = J["w_nat"].rearrange("(s ht n) c -> s ht n c", ht=nht_full, n=NS)[
                sblk, kc * nht:(kc + 1) * nht, o:o + 128, :]
            self.w_written[bp] = pb.dma("qA", dst.transpose([1, 0, 2]),
                                        wn[:, 0:KCH].rearrange("p (ht c) -> p ht c", c=128),
                                        after=[("pool", self.pool_hi[bp])])
        else:
            self.w_written[bp] = pb.dma("qA", J["w_nat"][row0:row0 + 128, kc * KCH:(kc + 1) * KCH], wn[:, 0:KCH],
                                        after=[("pool", self.pool_hi[bp])])


def emit_ffn_pass(pb, cfg, src, src_is_f32, dst, dst_is_f32, dst_row_off, tcs,
                  WgT, WuT, WdT, tag, dq=None):
    nc = pb.nc
    D, NMLP = cfg.D, cfg.NMLP
    nHT = cfg.nHT
    NS = cfg.n_super
    G = cfg.G
    n_gsup = G // NS
    nG = NMLP // G
    maxtc = max(l for _, l in tcs)

    with ExitStack() as st:
        sbl = lambda n, s, d: st.enter_context(nc.sbuf_tensor(n, s, d))
        psl = lambda n, s, d=F32: st.enter_context(nc.psum_tensor(n, s, d))
        xT = sbl(f"xT{tag}", [128, nHT * maxtc], BF16)
        yacc = sbl(f"y{tag}", [128, (maxtc // 128) * D], F32)
        inter = sbl(f"int{tag}", [128, (G // 128) * maxtc], BF16)
        wg = [sbl(f"wg{tag}{i}", [128, nHT * NS], BF16) for i in range(2)]
        wu = [sbl(f"wu{tag}{i}", [128, nHT * NS], BF16) for i in range(2)]
        wd = sbl(f"wd{tag}", [128, (G // 128) * D], BF16)
        xin = [sbl(f"lx{tag}{i}", [128, D], F32 if src_is_f32 else BF16) for i in range(2)]
        xc = sbl(f"lc{tag}", [128, D], F32)
        xbf = [sbl(f"lb{tag}{i}", [128, D], BF16) for i in range(2)]
        stt = sbl(f"ls{tag}", [128, 4], F32)
        rres = xin
        xc2 = sbl(f"l2{tag}", [128, D], F32) if dst_is_f32 else None
        rout = [xc, xc2] if dst_is_f32 else xbf
        g_sb = [sbl(f"ge{tag}{i}", [128, 512], F32) for i in range(2)]
        u_sb = [sbl(f"ue{tag}{i}", [128, 512], F32) for i in range(2)]
        gu_sc = sbl(f"gu{tag}", [128, 512], F32)
        gu_t2 = sbl(f"gt{tag}", [128, 512], F32)
        gu_th = sbl(f"gh{tag}", [128, 512], F32)
        gu_gl = sbl(f"gg{tag}", [128, 512], F32)
        gps = [psl(f"gp{tag}{i}", [128, 512]) for i in range(3)]
        ups = [psl(f"up{tag}{i}", [128, 512]) for i in range(3)]
        dps = [psl(f"dp{tag}{i}", [128, 512]) for i in range(2)]
        trp = [dps[i][:, 0:64].bitcast(BF16) for i in range(2)]

        ident = pb.ident
        pe_xT_reads = 0
        ln_in_free = {0: 0, 1: 0}
        xbf_free = {0: 0, 1: 0}
        tr_evac = {0: 0, 1: 0}
        wgu_free = {0: 0, 1: 0}
        wgu_loaded = {0: 0, 1: 0}
        wd_loaded = 0
        wd_free = 0
        gps_pe = {0: 0, 1: 0, 2: 0}
        gps_dve = {0: 0, 1: 0, 2: 0}
        ups_pe = {0: 0, 1: 0, 2: 0}
        ups_dve = {0: 0, 1: 0, 2: 0}
        ups_dve2 = {0: 0, 1: 0, 2: 0}
        dps_pe = {0: 0, 1: 0}
        dps_dve = {0: 0, 1: 0}
        inter_pe_reads = 0
        rr_free = {0: 0, 1: 0}
        ro_free = {0: 0, 1: 0}
        y_stt_done = 0

        n_sup_total = NMLP // NS
        for tci, (row0, tclen) in enumerate(tcs):
            ntt = tclen // 128
            pieces = cfg.pieces(tclen)
            for i in range(ntt):
                bp = i % 2
                r = row0 + i * 128
                cd = pb.dma("qA", xin[bp][:], src[r:r + 128, :],
                            after=[("dve", ln_in_free[bp]), ("qA", ro_free[bp])])
                pb.ins("dve", lambda e, b=bp: e.reduce_sum(stt[:, 0:1], xin[b][:], axis=AX.X),
                       after=[("qA", cd), ("dve", pb.cnt["dve"]), ("act", pb.cnt["act"])])
                pb.ins("dve", lambda e: e.tensor_scalar(stt[:, 0:1], stt[:, 0:1], 1.0 / D, None, ALU.mult),
                       after=[("dve", pb.cnt["dve"])])
                ln_in_free[bp] = pb.ins("dve", lambda e, b=bp: e.tensor_scalar(xc[:], xin[b][:], stt[:, 0:1], None, ALU.subtract),
                                        after=[("dve", pb.cnt["dve"]), ("act", pb.cnt["act"]),
                                               ("qA", max(ro_free[0], ro_free[1]))])
                pb.ins("act", lambda e, b=bp: e.activation(xbf[b][:], xc[:], AF.Square, accum_out=stt[:, 1:2]),
                       after=[("dve", pb.cnt["dve"]), ("pe", xbf_free[bp]), ("qA", ro_free[bp])])
                pb.ins("act", lambda e: e.activation(stt[:, 2:3], stt[:, 1:2], AF.Sqrt, bias=LN_EPS, scale=1.0 / D),
                       after=[("act", pb.cnt["act"])])
                pb.ins("dve", lambda e: e.reciprocal(stt[:, 3:4], stt[:, 2:3]),
                       after=[("act", pb.cnt["act"]), ("dve", pb.cnt["dve"])])
                pb.ins("act", lambda e, b=bp: e.activation(xbf[b][:], xc[:], AF.Copy, scale=stt[:, 3:4]),
                       after=[("dve", pb.cnt["dve"])])
                xbf_done = pb.cnt["act"]
                for ht in range(nHT):
                    tp = (i * nHT + ht) % 2
                    xbf_free[bp] = pb.ins(
                        "pe", lambda e, ht=ht, tp=tp, b=bp: e.transpose(trp[tp], xbf[b][:, ht * 128:(ht + 1) * 128], ident[:]),
                        after=[("act", xbf_done), ("dve", tr_evac[tp]), ("pe", pe_xT_reads),
                               ("dve", max(dps_dve[0], dps_dve[1]))])
                    tr_evac[tp] = pb.ins(
                        "dve", lambda e, ht=ht, i=i, tp=tp, tclen=tclen: e.tensor_copy(
                            xT[:, ht * tclen + i * 128: ht * tclen + i * 128 + 128], trp[tp]),
                        after=[("pe", pb.cnt["pe"]), ("dve", pb.cnt["dve"])])
            ln_done_dve = pb.cnt["dve"]

            for g in range(NMLP // G):
                if dq is not None and tci == 0:
                    need_d = 16 * ((g * G) // dq.jobs[2]["KCH"] + 1)
                    dq.ensure(2, need_d)
                    pb.wait("sp", "qA", dq.cover_count(2, need_d), exact=True)
                for si in range(n_gsup):
                    s = g * n_gsup + si
                    sp_ = s % 2
                    if si == 1 or n_gsup == 1:
                        for ntg in range(G // 128):
                            wd_loaded = pb.dma("qS", wd[:, ntg * D:(ntg + 1) * D],
                                               WdT[:, g * G + ntg * 128: g * G + ntg * 128 + 128],
                                               transpose=True, bound=(ntg == 0),
                                               after=[("pe", wd_free)] if ntg == 0 else [])
                    if dq is not None and tci == 0:
                        kpn = dq.jobs[0]["k_chunks"] * (NS // 128)
                        margin = min((s + 6) * kpn, len(dq.jobs[0]["work"]))
                        dq.ensure(0, margin)
                        dq.ensure(1, margin)
                        dq.ensure(2, s * len(dq.jobs[2]["work"]) // (NMLP // NS) + 32)
                        need = (s + 1) * kpn
                        pb.wait("sp", "qA", max(dq.cover_count(0, need), dq.cover_count(1, need)), exact=True)
                    rows = nHT * NS
                    pb.dma("qS", wg[sp_][:], WgT[s * rows:(s + 1) * rows, :],
                           transpose=True, after=[("pe", wgu_free[sp_])])
                    wgu_loaded[sp_] = pb.dma("qS", wu[sp_][:], WuT[s * rows:(s + 1) * rows, :],
                                             transpose=True, bound=False)
                    for ntl in range(NS // 128):
                        ng128 = si * (NS // 128) + ntl
                        for pidx, (poff, plen) in enumerate(pieces):
                            pp = (ng128 * len(pieces) + pidx) % 3
                            for ht in range(nHT):
                                pb.ins("pe", lambda e, ht=ht, ntl=ntl, pp=pp, sp_=sp_, poff=poff, plen=plen, tclen=tclen:
                                       e.matmul(gps[pp][:, 0:plen],
                                                wg[sp_][:, ht * NS + ntl * 128: ht * NS + ntl * 128 + 128],
                                                xT[:, ht * tclen + poff: ht * tclen + poff + plen],
                                                start=(ht == 0), stop=(ht == nHT - 1)),
                                       sem=(False if ht < nHT - 1 else "pe"),
                                       after=([("qS", wgu_loaded[sp_]), ("dve", ln_done_dve), ("act", gps_dve[pp])]
                                              if ht == 0 else []))
                            g_pe = pb.cnt["pe"]
                            gps_pe[pp] = g_pe
                            for ht in range(nHT):
                                pb.ins("pe", lambda e, ht=ht, ntl=ntl, pp=pp, sp_=sp_, poff=poff, plen=plen, tclen=tclen:
                                       e.matmul(ups[pp][:, 0:plen],
                                                wu[sp_][:, ht * NS + ntl * 128: ht * NS + ntl * 128 + 128],
                                                xT[:, ht * tclen + poff: ht * tclen + poff + plen],
                                                start=(ht == 0), stop=(ht == nHT - 1)),
                                       sem=(False if ht < nHT - 1 else "pe"),
                                       after=[("act", ups_dve2[pp])] if ht == 0 else [])
                            u_pe = pb.cnt["pe"]
                            ups_pe[pp] = u_pe
                            pe_xT_reads = u_pe
                            wgu_free[sp_] = u_pe
                            cb = (ng128 * len(pieces) + pidx) % 2
                            pb.ins("act", lambda e, pp=pp, plen=plen, cb=cb: e.copy(g_sb[cb][:, 0:plen], gps[pp][:, 0:plen]),
                                   after=[("pe", g_pe), ("dve", pb.cnt["dve"]), ("act", pb.cnt["act"])])
                            gps_dve[pp] = pb.cnt["act"]
                            pb.ins("act", lambda e, pp=pp, plen=plen, cb=cb: e.copy(u_sb[cb][:, 0:plen], ups[pp][:, 0:plen]),
                                   after=[("pe", u_pe), ("dve", pb.cnt["dve"]), ("act", pb.cnt["act"])])
                            ups_dve2[pp] = pb.cnt["act"]
                            pb.ins("act", lambda e, plen=plen, cb=cb: e.activation(gu_sc[:, 0:plen], g_sb[cb][:, 0:plen], AF.Square),
                                   after=[("act", pb.cnt["act"])])
                            pb.ins("dve", lambda e, plen=plen, cb=cb: e.scalar_tensor_tensor(
                                       gu_t2[:, 0:plen], gu_sc[:, 0:plen], GELU_K, g_sb[cb][:, 0:plen], ALU.add, ALU.mult),
                                   after=[("act", pb.cnt["act"]), ("dve", pb.cnt["dve"])])
                            pb.ins("act", lambda e, plen=plen: e.activation(gu_th[:, 0:plen], gu_t2[:, 0:plen], AF.Tanh, scale=GELU_C3),
                                   after=[("dve", pb.cnt["dve"]), ("act", pb.cnt["act"])])
                            pb.ins("dve", lambda e, plen=plen, cb=cb: e.scalar_tensor_tensor(
                                       gu_gl[:, 0:plen], gu_th[:, 0:plen], 1.0, g_sb[cb][:, 0:plen], ALU.add, ALU.mult),
                                   after=[("act", pb.cnt["act"]), ("dve", pb.cnt["dve"])])
                            ups_dve[pp] = pb.ins("dve", lambda e, plen=plen, ng128=ng128, poff=poff, tclen=tclen, cb=cb:
                                                 e.scalar_tensor_tensor(
                                                     inter[:, ng128 * tclen + poff: ng128 * tclen + poff + plen],
                                                     gu_gl[:, 0:plen], 0.5, u_sb[cb][:, 0:plen], ALU.mult, ALU.mult),
                                                 after=[("pe", u_pe), ("dve", pb.cnt["dve"]),
                                                        ("pe", inter_pe_reads if (s % n_gsup == 0 and ntl == 0) else 0)])
                inter_done = pb.cnt["dve"]
                for ts in range(ntt):
                    for hc in range(D // 512):
                        pp = (ts * (D // 512) + hc) % 2
                        for ntg in range(G // 128):
                            pb.ins("pe", lambda e, ts=ts, hc=hc, ntg=ntg, pp=pp, tclen=tclen:
                                   e.matmul(dps[pp][:],
                                            inter[:, ntg * tclen + ts * 128: ntg * tclen + ts * 128 + 128],
                                            wd[:, ntg * D + hc * 512: ntg * D + hc * 512 + 512],
                                            start=(ntg == 0), stop=(ntg == G // 128 - 1)),
                                   sem=(False if ntg < G // 128 - 1 else "pe"),
                                   after=([("dve", inter_done), ("qS", wd_loaded), ("dve", dps_dve[pp])]
                                          if ntg == 0 else []))
                        d_pe = pb.cnt["pe"]
                        dps_pe[pp] = d_pe
                        if g == 0:
                            dps_dve[pp] = pb.ins("dve", lambda e, ts=ts, hc=hc, pp=pp: e.tensor_copy(
                                                     yacc[:, ts * D + hc * 512: ts * D + hc * 512 + 512], dps[pp][:]),
                                                 after=[("pe", d_pe), ("dve", max(pb.cnt["dve"], y_stt_done))])
                        else:
                            dps_dve[pp] = pb.ins("dve", lambda e, ts=ts, hc=hc, pp=pp: e.tensor_tensor(
                                                     yacc[:, ts * D + hc * 512: ts * D + hc * 512 + 512],
                                                     yacc[:, ts * D + hc * 512: ts * D + hc * 512 + 512],
                                                     dps[pp][:], ALU.add),
                                                 after=[("pe", d_pe), ("dve", pb.cnt["dve"])])
                inter_pe_reads = pb.cnt["pe"]
                wd_free = pb.cnt["pe"]
            y_done = pb.cnt["dve"]

            for ts in range(ntt):
                bp = ts % 2
                r = row0 + ts * 128
                cd = pb.dma("qA", rres[bp][:], src[r:r + 128, :],
                            after=[("dve", ln_in_free[bp]), ("qA", ro_free[bp])])
                ln_in_free[bp] = pb.ins("dve", lambda e, ts=ts, bp=bp: e.scalar_tensor_tensor(
                                            rout[bp][:], yacc[:, ts * D:(ts + 1) * D], 0.5, rres[bp][:], ALU.mult, ALU.add),
                                        after=[("qA", cd), ("dve", y_done), ("qA", ro_free[bp]),
                                               ("dve", pb.cnt["dve"]), ("act", pb.cnt["act"]), ("pe", xbf_free[bp])])
                ro_free[bp] = pb.dma("qA", dst[r + dst_row_off:r + dst_row_off + 128, :], rout[bp][:],
                                     after=[("dve", pb.cnt["dve"])])
            y_stt_done = pb.cnt["dve"]
            if dq is not None and tci == 0:
                dq.finish()
        pb.drain_queues()
        pb.flush()


def emit_attn(pb, cfg, R1, R2, Wq, Wk, Wv, Wo, Wq_nat, Wk_nat, Wv_nat, Wo_nat, xaTd,
              qTd, aoTd,
              bq_in, bk_in, bv_in, bo_in, qn4_in, kn4_in, scaleb,
              m01a_in, fila_in, m01b_in, filb_in, cos_in, sin_in, e0, ident):
    nc = pb.nc
    D, NH, NKV, DK, DKV = cfg.D, cfg.NH, cfg.NKV, cfg.DK, cfg.DKV
    nHT, nT, T = cfg.nHT, cfg.nT, cfg.T
    SB = min(4, NH)
    n_sb = NH // SB

    with ExitStack() as outer:
        sbo = lambda n, s, d: outer.enter_context(nc.sbuf_tensor(n, s, d))
        kT = sbo("a_kT", [128, NKV * T], BF16)
        vsb = sbo("a_v", [128, nT * DKV], BF16)
        bqT = sbo("a_bqT", [128, D], BF16)
        bkT = sbo("a_bkT", [128, DKV], BF16)
        bvT = sbo("a_bvT", [128, DKV], BF16)
        boT = sbo("a_boT", [128, D], BF16)
        qn4 = sbo("a_qn4", [128, 512], F32)
        kn4 = sbo("a_kn4", [128, DKV], F32)
        m01a = sbo("a_m01a", [128, 256], F32)
        fila = sbo("a_fila", [128, 256], F32)
        m01b = sbo("a_m01b", [128, 256], F32)
        filb = sbo("a_filb", [128, 256], F32)

        pb.dma("qS", qn4[:], qn4_in)
        pb.dma("qS", kn4[:], kn4_in, bound=False)
        pb.dma("qS", m01a[:], m01a_in, bound=False)
        pb.dma("qS", fila[:], fila_in, bound=False)
        pb.dma("qS", m01b[:], m01b_in, bound=False)
        pb.dma("qS", filb[:], filb_in, bound=False)
        with ExitStack() as stp:
            b32 = stp.enter_context(nc.sbuf_tensor("a_b32", [128, D], F32))
            for btile, srcb, width in ((bqT, bq_in, D), (bkT, bk_in, DKV),
                                       (bvT, bv_in, DKV), (boT, bo_in, D)):
                pb.ins("dve", lambda e, t=btile: e.memset(t[:], 0.0),
                       after=[("dve", pb.cnt["dve"])])
                pb.dma("qS", b32[0:1, 0:width], srcb, after=[("dve", pb.cnt["dve"])])
                pb.ins("dve", lambda e, t=btile, w=width: e.tensor_copy(t[0:1, 0:w], b32[0:1, 0:w]),
                       after=[("qS", pb.cnt["qS"]), ("dve", pb.cnt["dve"])])
            pb.drain_queues()
            pb.flush()

        with ExitStack() as st0:
            sb0 = lambda n, s, d: st0.enter_context(nc.sbuf_tensor(n, s, d))
            wnat = [sb0(f"a_wn{i}", [128, D], BF16) for i in range(2)]
            wn_free = {0: 0, 1: 0}
            for wsrc, wdst, rows, blocked in ((Wq, Wq_nat, D, True), (Wo, Wo_nat, D, True),
                                              (Wk, Wk_nat, DKV, False), (Wv, Wv_nat, DKV, False)):
                for rt in range(rows // 128):
                    bp = rt % 2
                    pb.dma("qG", wnat[bp][:], wsrc[rt * 128:(rt + 1) * 128, :],
                           after=[("qS", wn_free[bp])])
                    if blocked:
                        ch = (rt * 128) // 512
                        o = (rt * 128) % 512
                        dst = wdst.rearrange("(a ht j) c -> a ht j c", ht=nHT, j=512)[ch, :, o:o + 128, :]
                        dst = dst.transpose([1, 0, 2])
                        wn_free[bp] = pb.dma("qS", dst, wnat[bp][:].rearrange("p (ht c) -> p ht c", c=128),
                                             after=[("qG", pb.cnt["qG"])])
                    else:
                        wn_free[bp] = pb.dma("qS", wdst[rt * 128:(rt + 1) * 128, :], wnat[bp][:],
                                             after=[("qG", pb.cnt["qG"])])

            wkT = sb0("a_wkT", [128, nHT * DKV], BF16)
            wvT = sb0("a_wvT", [128, nHT * DKV], BF16)
            for ht in range(nHT):
                pb.dma("qS", wkT[:, ht * DKV:(ht + 1) * DKV], Wk_nat[:, ht * 128:(ht + 1) * 128],
                       transpose=True, bound=(ht == 0))
            for ht in range(nHT):
                pb.dma("qS", wvT[:, ht * DKV:(ht + 1) * DKV], Wv_nat[:, ht * 128:(ht + 1) * 128],
                       transpose=True, bound=(ht == 0))

            xin = [sb0(f"a_xi{i}", [128, D], BF16) for i in range(2)]
            xc = sb0("a_xc", [128, D], F32)
            xbf = [sb0(f"a_xb{i}", [128, D], BF16) for i in range(2)]
            stt = sb0("a_st", [128, 4], F32)
            xstg = [sb0(f"a_xs{i}", [128, nHT * 128], BF16) for i in range(2)]
            trp = [st0.enter_context(nc.psum_tensor(f"a_tr0{i}", [128, 128], BF16)) for i in range(2)]

            ln_in_free = {0: 0, 1: 0}
            xbf_free = {0: 0, 1: 0}
            xstg_written = {0: 0, 1: 0}
            tr_evac = {0: 0, 1: 0}
            for i in range(nT):
                bp = i % 2
                r = i * 128
                cd = pb.dma("qA", xin[bp][:], R1[r:r + 128, :], after=[("dve", ln_in_free[bp])])
                pb.ins("dve", lambda e, b=bp: e.reduce_sum(stt[:, 0:1], xin[b][:], axis=AX.X),
                       after=[("qA", cd), ("dve", pb.cnt["dve"]), ("act", pb.cnt["act"])])
                pb.ins("dve", lambda e: e.tensor_scalar(stt[:, 0:1], stt[:, 0:1], 1.0 / D, None, ALU.mult),
                       after=[("dve", pb.cnt["dve"])])
                ln_in_free[bp] = pb.ins("dve", lambda e, b=bp: e.tensor_scalar(xc[:], xin[b][:], stt[:, 0:1], None, ALU.subtract),
                                        after=[("dve", pb.cnt["dve"]), ("act", pb.cnt["act"])])
                pb.ins("act", lambda e, b=bp: e.activation(xbf[b][:], xc[:], AF.Square, accum_out=stt[:, 1:2]),
                       after=[("dve", pb.cnt["dve"]), ("pe", xbf_free[bp])])
                pb.ins("act", lambda e: e.activation(stt[:, 2:3], stt[:, 1:2], AF.Sqrt, bias=LN_EPS, scale=1.0 / D),
                       after=[("act", pb.cnt["act"])])
                pb.ins("dve", lambda e: e.reciprocal(stt[:, 3:4], stt[:, 2:3]),
                       after=[("act", pb.cnt["act"]), ("dve", pb.cnt["dve"])])
                pb.ins("act", lambda e, b=bp: e.activation(xbf[b][:], xc[:], AF.Copy, scale=stt[:, 3:4]),
                       after=[("dve", pb.cnt["dve"])])
                xbf_done = pb.cnt["act"]
                for ht in range(nHT):
                    tp = (i * nHT + ht) % 2
                    xbf_free[bp] = pb.ins(
                        "pe", lambda e, ht=ht, tp=tp, b=bp: e.transpose(trp[tp][:], xbf[b][:, ht * 128:(ht + 1) * 128], ident[:]),
                        after=[("act", xbf_done), ("act", tr_evac[tp])])
                    tr_evac[tp] = pb.ins(
                        "act", lambda e, ht=ht, bp=bp, tp=tp: e.copy(
                            xstg[bp][:, ht * 128:(ht + 1) * 128], trp[tp][:]),
                        after=[("pe", pb.cnt["pe"]), ("qA", xstg_written[bp])])
                xdst = xaTd.rearrange("(ht p) t -> p ht t", p=128)[:, :, i * 128:(i + 1) * 128]
                xstg_written[bp] = pb.dma("qA", xdst,
                                          xstg[bp][:].rearrange("p (ht n) -> p ht n", n=128),
                                          after=[("act", tr_evac[0]), ("act", tr_evac[1])])
            pb.drain_queues()
            pb.flush()

            qn_nat = sb0("a_qn", [128, 512], F32)
            qrot = sb0("a_qrot", [128, 512], BF16)
            rst = sb0("a_rst", [128, 40], F32)
            rop = sb0("a_rop", [128, max(512, 2 * DKV)], F32)
            cos4 = sb0("a_cos", [128, 256], F32)
            sin4 = sb0("a_sin", [128, 256], F32)
            xas = [sb0(f"a_xa{i}", [128, nHT * 128], BF16) for i in range(2)]
            qps = st0.enter_context(nc.psum_tensor("a_qps", [128, 512], F32))

            xa_free = {0: 0, 1: 0}
            qrot_pe = 0
            tr_evac = {0: 0, 1: 0}
            kv_dve = 0
            for i in range(nT):
                bp = i % 2
                xsrc = xaTd.rearrange("(ht p) t -> p ht t", p=128)[:, :, i * 128:(i + 1) * 128]
                cd = pb.dma("qA", xas[bp][:].rearrange("p (ht n) -> p ht n", n=128), xsrc,
                            after=[("pe", xa_free[bp])])
                pb.dma("qA", cos4[:], cos_in[:, i * 256:(i + 1) * 256], after=[("dve", kv_dve)], bound=False)
                cs = pb.dma("qA", sin4[:], sin_in[:, i * 256:(i + 1) * 256], bound=False)
                for ht in range(nHT):
                    pb.ins("pe", lambda e, ht=ht, b=bp: e.matmul(qps[:, 0:DKV],
                               xas[b][:, ht * 128:(ht + 1) * 128],
                               wkT[:, ht * DKV:(ht + 1) * DKV],
                               start=(ht == 0), stop=False),
                           sem=False, after=[("qA", cd), ("qS", pb.cnt["qS"]), ("act", pb.cnt["act"])] if ht == 0 else [])
                pb.ins("pe", lambda e: e.matmul(qps[:, 0:DKV], e0[:], bkT[:, 0:DKV], start=False, stop=True))
                k_pe = pb.cnt["pe"]
                pb.ins("act", lambda e: e.copy(qn_nat[:, 0:DKV], qps[:, 0:DKV]),
                       after=[("pe", k_pe), ("dve", pb.cnt["dve"]), ("act", pb.cnt["act"])])
                pb.ins("act", lambda e: e.activation(rop[:, 0:DKV], qn_nat[:, 0:DKV], AF.Square),
                       after=[("act", pb.cnt["act"]), ("dve", pb.cnt["dve"])])
                pb.ins("dve", lambda e: e.reduce_sum(rst[:, 0:NKV].unsqueeze(2),
                                                     rop[:, 0:DKV].rearrange("p (h d) -> p h d", d=128), axis=AX.X),
                       after=[("act", pb.cnt["act"]), ("dve", pb.cnt["dve"])])
                pb.ins("act", lambda e: e.activation(rst[:, 4:4 + NKV], rst[:, 0:NKV], AF.Sqrt,
                                                     bias=RMS_EPS, scale=1.0 / DK),
                       after=[("dve", pb.cnt["dve"]), ("act", pb.cnt["act"])])
                pb.ins("dve", lambda e: e.reciprocal(rst[:, 0:NKV], rst[:, 4:4 + NKV]),
                       after=[("act", pb.cnt["act"]), ("dve", pb.cnt["dve"])])
                pb.ins("dve", lambda e: e.tensor_tensor(
                           qn_nat[:, 0:DKV].rearrange("p (h d) -> p h d", d=128),
                           qn_nat[:, 0:DKV].rearrange("p (h d) -> p h d", d=128),
                           rst[:, 0:NKV].unsqueeze(2).broadcast_to([128, NKV, 128]), ALU.mult),
                       after=[("dve", pb.cnt["dve"])])
                pb.ins("dve", lambda e: e.tensor_tensor(qn_nat[:, 0:DKV], qn_nat[:, 0:DKV], kn4[:, 0:DKV], ALU.mult),
                       after=[("dve", pb.cnt["dve"])])
                q1 = qn_nat[:, 0:DKV:2]
                q2 = qn_nat[:, 1:DKV:2]
                hw = DKV // 2
                pb.ins("dve", lambda e, q1=q1, hw=hw: e.tensor_tensor(rop[:, 0:hw], q1, cos4[:, 0:hw], ALU.mult),
                       after=[("qA", cs), ("dve", pb.cnt["dve"])])
                pb.ins("dve", lambda e, q2=q2, hw=hw: e.tensor_tensor(rop[:, hw:2 * hw], q2, sin4[:, 0:hw], ALU.mult))
                pb.ins("dve", lambda e, q1=q1, hw=hw: e.tensor_tensor(rop[:, 2 * hw:3 * hw], q1, sin4[:, 0:hw], ALU.mult))
                pb.ins("dve", lambda e, q2=q2, hw=hw: e.tensor_tensor(rop[:, 3 * hw:4 * hw], q2, cos4[:, 0:hw], ALU.mult),
                       after=[("dve", pb.cnt["dve"])])
                pb.ins("dve", lambda e, hw=hw: e.tensor_tensor(qrot[:, 0:DKV:2], rop[:, 0:hw], rop[:, hw:2 * hw], ALU.subtract),
                       after=[("dve", pb.cnt["dve"]), ("pe", qrot_pe)])
                pb.ins("dve", lambda e, hw=hw: e.tensor_tensor(qrot[:, 1:DKV:2], rop[:, 2 * hw:3 * hw], rop[:, 3 * hw:4 * hw], ALU.add),
                       after=[("dve", pb.cnt["dve"])])
                rope_done = pb.cnt["dve"]
                for kv in range(NKV):
                    tp = (i * NKV + kv) % 2
                    qrot_pe = pb.ins("pe", lambda e, kv=kv, tp=tp: e.transpose(trp[tp][:], qrot[:, kv * 128:(kv + 1) * 128], ident[:]),
                                     after=[("dve", rope_done), ("dve", tr_evac[tp])])
                    tr_evac[tp] = pb.ins("dve", lambda e, kv=kv, i=i, tp=tp: e.tensor_copy(
                                             kT[:, kv * T + i * 128: kv * T + i * 128 + 128], trp[tp][:]),
                                         after=[("pe", pb.cnt["pe"]), ("dve", pb.cnt["dve"])])
                kv_dve = pb.cnt["dve"]
                for ht in range(nHT):
                    pb.ins("pe", lambda e, ht=ht, b=bp: e.matmul(qps[:, 0:DKV],
                               xas[b][:, ht * 128:(ht + 1) * 128],
                               wvT[:, ht * DKV:(ht + 1) * DKV],
                               start=(ht == 0), stop=False),
                           sem=False, after=[("act", pb.cnt["act"])] if ht == 0 else [])
                xa_free[bp] = pb.ins("pe", lambda e: e.matmul(qps[:, 0:DKV], e0[:], bvT[:, 0:DKV], start=False, stop=True))
                pb.ins("act", lambda e, i=i: e.copy(vsb[:, i * DKV:(i + 1) * DKV], qps[:, 0:DKV]),
                       after=[("pe", pb.cnt["pe"]), ("act", pb.cnt["act"])])
            pb.drain_queues()
            pb.flush()

        with ExitStack() as sta:
            sba = lambda n, s, d: sta.enter_context(nc.sbuf_tensor(n, s, d))
            wqTs = sba("a_wq", [128, nHT * 512], BF16)
            xas = [sba(f"a_x3{i}", [128, nHT * 128], BF16) for i in range(2)]
            qn_nat = [sba(f"a_qn3{i}", [128, 512], F32) for i in range(2)]
            qrot = [sba(f"a_qr3{i}", [128, 512], BF16) for i in range(2)]
            qstg = [sba(f"a_qs3{i}", [128, 512], BF16) for i in range(2)]
            rst = [sba(f"a_rs3{i}", [128, 40], F32) for i in range(2)]
            rop = [sba(f"a_ro3{i}", [128, 512], F32) for i in range(2)]
            cos4 = [sba(f"a_co3{i}", [128, 256], F32) for i in range(2)]
            sin4 = [sba(f"a_si3{i}", [128, 256], F32) for i in range(2)]
            trp = [sta.enter_context(nc.psum_tensor(f"a_tr3{i}", [128, 128], BF16)) for i in range(2)]
            qps = [sta.enter_context(nc.psum_tensor(f"a_qp3{i}", [128, 512], F32)) for i in range(2)]

            tr_evac = {0: 0, 1: 0}
            qrot_pe = {0: 0, 1: 0}
            xa_free = {0: 0, 1: 0}
            qps_act = {0: 0, 1: 0}
            qstg_written = {0: 0, 1: 0}

            def a3a_mms(ch, bt, ib, wq_c):
                xsrc = xaTd.rearrange("(ht p) t -> p ht t", p=128)[:, :, bt * 128:(bt + 1) * 128]
                cdx = pb.dma("qA", xas[ib][:].rearrange("p (ht n) -> p ht n", n=128), xsrc,
                             after=[("pe", xa_free[ib])])
                pb.dma("qA", cos4[ib][:], cos_in[:, bt * 256:(bt + 1) * 256],
                       after=[("dve", qrot_pe.get("c" + str(ib), 0))], bound=False)
                pb.dma("qA", sin4[ib][:], sin_in[:, bt * 256:(bt + 1) * 256], bound=False)
                cs = pb.cnt["qA"]
                for ht in range(nHT):
                    pb.ins("pe", lambda e, ht=ht, ib=ib: e.matmul(
                               qps[ib][:],
                               xas[ib][:, ht * 128:(ht + 1) * 128],
                               wqTs[:, ht * 512:(ht + 1) * 512],
                               start=(ht == 0), stop=False),
                           sem=False,
                           after=[("qS", wq_c), ("qA", cdx), ("act", qps_act[ib])] if ht == 0 else [])
                pb.ins("pe", lambda e, ch=ch, ib=ib: e.matmul(qps[ib][:], e0[:], bqT[:, ch * 512:(ch + 1) * 512], start=False, stop=True))
                xa_free[ib] = pb.cnt["pe"]
                return pb.cnt["pe"], cs

            def a3a_chain(ch, bt, ib, q_pe, cs):
                pb.ins("act", lambda e, ib=ib: e.copy(qn_nat[ib][:], qps[ib][:]),
                       after=[("pe", q_pe), ("dve", pb.cnt["dve"]), ("act", pb.cnt["act"])])
                qps_act[ib] = pb.cnt["act"]
                pb.ins("act", lambda e, ib=ib: e.activation(rop[ib][:], qn_nat[ib][:], AF.Square),
                       after=[("act", pb.cnt["act"]), ("dve", pb.cnt["dve"])])
                pb.ins("dve", lambda e, ib=ib: e.reduce_sum(rst[ib][:, 0:4].unsqueeze(2),
                                                            rop[ib][:].rearrange("p (h d) -> p h d", d=128), axis=AX.X),
                       after=[("act", pb.cnt["act"]), ("dve", pb.cnt["dve"])])
                pb.ins("act", lambda e, ib=ib: e.activation(rst[ib][:, 4:8], rst[ib][:, 0:4], AF.Sqrt,
                                                            bias=RMS_EPS, scale=1.0 / DK),
                       after=[("dve", pb.cnt["dve"]), ("act", pb.cnt["act"])])
                pb.ins("dve", lambda e, ib=ib: e.reciprocal(rst[ib][:, 0:4], rst[ib][:, 4:8]),
                       after=[("act", pb.cnt["act"]), ("dve", pb.cnt["dve"])])
                pb.ins("dve", lambda e, ib=ib: e.tensor_tensor(
                           qn_nat[ib][:].rearrange("p (h d) -> p h d", d=128),
                           qn_nat[ib][:].rearrange("p (h d) -> p h d", d=128),
                           rst[ib][:, 0:4].unsqueeze(2).broadcast_to([128, 4, 128]), ALU.mult),
                       after=[("dve", pb.cnt["dve"])])
                pb.ins("dve", lambda e, ib=ib: e.tensor_tensor(qn_nat[ib][:], qn_nat[ib][:], qn4[:], ALU.mult),
                       after=[("dve", pb.cnt["dve"])])
                q1 = qn_nat[ib][:, 0:512:2]
                q2 = qn_nat[ib][:, 1:512:2]
                pb.ins("dve", lambda e, q1=q1, ib=ib: e.tensor_tensor(rop[ib][:, 0:256], q1, cos4[ib][:], ALU.mult),
                       after=[("qA", cs), ("dve", pb.cnt["dve"])])
                pb.ins("dve", lambda e, q2=q2, ib=ib: e.tensor_tensor(rop[ib][:, 256:512], q2, sin4[ib][:], ALU.mult))
                pb.ins("dve", lambda e, ib=ib: e.tensor_tensor(rop[ib][:, 0:256], rop[ib][:, 0:256], rop[ib][:, 256:512], ALU.subtract),
                       after=[("dve", pb.cnt["dve"]), ("pe", qrot_pe[ib])])
                pb.ins("dve", lambda e, ib=ib: e.tensor_copy(qrot[ib][:, 0:512:2], rop[ib][:, 0:256]),
                       after=[("dve", pb.cnt["dve"])])
                pb.ins("dve", lambda e, q1=q1, ib=ib: e.tensor_tensor(rop[ib][:, 0:256], q1, sin4[ib][:], ALU.mult),
                       after=[("dve", pb.cnt["dve"])])
                pb.ins("dve", lambda e, q2=q2, ib=ib: e.tensor_tensor(rop[ib][:, 256:512], q2, cos4[ib][:], ALU.mult))
                pb.ins("dve", lambda e, ib=ib: e.tensor_tensor(qrot[ib][:, 1:512:2], rop[ib][:, 0:256], rop[ib][:, 256:512], ALU.add),
                       after=[("dve", pb.cnt["dve"])])
                rope_done = pb.cnt["dve"]
                qrot_pe["c" + str(ib)] = rope_done
                for h4 in range(4):
                    tp = h4 % 2
                    qrot_pe[ib] = pb.ins("pe", lambda e, h4=h4, tp=tp, ib=ib: e.transpose(
                                             trp[tp][:], qrot[ib][:, h4 * 128:(h4 + 1) * 128], ident[:]),
                                         after=[("dve", rope_done), ("act", tr_evac[tp])])
                    tr_evac[tp] = pb.ins("act", lambda e, h4=h4, tp=tp, ib=ib: e.copy(
                                             qstg[ib][:, h4 * 128:(h4 + 1) * 128], trp[tp][:]),
                                         after=[("pe", pb.cnt["pe"]), ("act", pb.cnt["act"]),
                                                ("qA", qstg_written[ib])])
                qstg_written[ib] = pb.dma("qA", qTd[bt * 128:(bt + 1) * 128, ch * 512:(ch + 1) * 512], qstg[ib][:],
                                          after=[("act", tr_evac[0]), ("act", tr_evac[1])])

            for ch in range(NH // 4):
                rows = nHT * 512
                pb.dma("qS", wqTs[:], Wq_nat[ch * rows:(ch + 1) * rows, :],
                       transpose=True, after=[("pe", pb.cnt["pe"])])
                wq_c = pb.cnt["qS"]
                pend = None
                for bt in range(cfg.halo // 128, nT):
                    ib = bt % 2
                    cur = (ch, bt, ib) + a3a_mms(ch, bt, ib, wq_c)
                    if pend is not None:
                        a3a_chain(pend[0], pend[1], pend[2], pend[3], pend[4])
                    pend = cur
                a3a_chain(pend[0], pend[1], pend[2], pend[3], pend[4])
            pb.drain_queues()
            pb.flush()

        with ExitStack() as stb:
            sbb = lambda n, s, d: stb.enter_context(nc.sbuf_tensor(n, s, d))
            qTt = [sbb(f"a_qT3{i}", [128, NH * 128], BF16) for i in range(2)]
            aoT = sbb("a_aoT", [128, NH * 128], BF16)
            sm1 = [sbb(f"a_sm1{i}", [128, SB * 256], F32) for i in range(2)]
            smE = [sbb(f"a_smE{i}", [128, SB * 256], F32) for i in range(2)]
            smA = [sbb(f"a_smA{i}", [128, SB * 256], BF16) for i in range(2)]
            rst = [sbb(f"a_rsb{i}", [128, 40], F32) for i in range(2)]
            atc = [sbb(f"a_at{i}", [128, 128], BF16) for i in range(2)]
            trp = [stb.enter_context(nc.psum_tensor(f"a_trb{i}", [128, 128], BF16)) for i in range(2)]
            sps = [stb.enter_context(nc.psum_tensor(f"a_sp3{i}", [128, SB * 256], F32)) for i in range(2)]
            ops_ = [stb.enter_context(nc.psum_tensor(f"a_op{i}", [128, 128], F32)) for i in range(2)]

            tr_evac = {0: 0, 1: 0}
            smA_pe = {0: 0, 1: 0}
            sps_dve = {0: 0, 1: 0}
            ops_act = {0: 0, 1: 0}
            ao_written = 0
            qT_free = {0: 0, 1: 0}
            qT_loaded = {0: 0, 1: 0}

            def a3b_smm(bt, sb_, ib):
                if sb_ == 0:
                    qT_loaded[bt % 2] = pb.dma("qS", qTt[bt % 2][:], qTd[bt * 128:(bt + 1) * 128, :],
                                               after=[("pe", qT_free[bt % 2])])
                w0 = (bt - 1) * 128
                for h8 in range(SB):
                    h = sb_ * SB + h8
                    kv = h // (NH // NKV)
                    pb.ins("pe", lambda e, h=h, kv=kv, h8=h8, ib=ib, w0=w0, tb=bt % 2: e.matmul(
                               sps[ib][:, h8 * 256:(h8 + 1) * 256], qTt[tb][:, h * 128:(h + 1) * 128],
                               kT[:, kv * T + w0: kv * T + w0 + 256], start=True, stop=True),
                           sem=(False if h8 < SB - 1 else "pe"),
                           after=[("qS", qT_loaded[bt % 2]), ("dve", sps_dve[ib])] if h8 == 0 else [])
                qT_free[bt % 2] = pb.cnt["pe"]
                return pb.cnt["pe"]

            def a3b_chain_av(bt, sb_, ib, s_pe):
                m01 = m01b if bt == cfg.halo // 128 else m01a
                fil = filb if bt == cfg.halo // 128 else fila
                sm1v = sm1[ib][:].rearrange("p (h k) -> p h k", k=256)
                smEv = smE[ib][:].rearrange("p (h k) -> p h k", k=256)
                smAv = smA[ib][:].rearrange("p (h k) -> p h k", k=256)
                pb.ins("dve", lambda e, m01=m01, ib=ib, sm1v=sm1v: e.scalar_tensor_tensor(
                           sm1v, sps[ib][:].rearrange("p (h k) -> p h k", k=256), scaleb[:, 0:1],
                           m01[:].unsqueeze(1).broadcast_to([128, SB, 256]), ALU.mult, ALU.mult),
                       after=[("pe", s_pe), ("dve", pb.cnt["dve"]), ("act", pb.cnt["act"])])
                sps_dve[ib] = pb.cnt["dve"]
                pb.ins("dve", lambda e, fil=fil, sm1v=sm1v: e.tensor_tensor(
                           sm1v, sm1v, fil[:].unsqueeze(1).broadcast_to([128, SB, 256]), ALU.add),
                       after=[("dve", pb.cnt["dve"])])
                pb.ins("dve", lambda e, ib=ib, sm1v=sm1v: e.reduce_max(rst[ib][:, 8:8 + SB].unsqueeze(2), sm1v, axis=AX.X, negate=True),
                       after=[("dve", pb.cnt["dve"])])
                pb.ins("dve", lambda e, ib=ib, sm1v=sm1v: e.tensor_tensor(
                           sm1v, sm1v, rst[ib][:, 8:8 + SB].unsqueeze(2).broadcast_to([128, SB, 256]), ALU.add),
                       after=[("dve", pb.cnt["dve"])])
                pb.ins("act", lambda e, ib=ib: e.activation(smE[ib][:], sm1[ib][:], AF.Exp),
                       after=[("dve", pb.cnt["dve"]), ("act", pb.cnt["act"])])
                pb.ins("dve", lambda e, ib=ib, smEv=smEv: e.reduce_sum(rst[ib][:, 16:16 + SB].unsqueeze(2), smEv, axis=AX.X),
                       after=[("act", pb.cnt["act"]), ("dve", pb.cnt["dve"])])
                pb.ins("dve", lambda e, ib=ib: e.reciprocal(rst[ib][:, 24:24 + SB], rst[ib][:, 16:16 + SB]),
                       after=[("dve", pb.cnt["dve"])])
                pb.ins("dve", lambda e, ib=ib, smAv=smAv, smEv=smEv: e.tensor_tensor(
                           smAv, smEv, rst[ib][:, 24:24 + SB].unsqueeze(2).broadcast_to([128, SB, 256]), ALU.mult),
                       after=[("dve", pb.cnt["dve"]), ("pe", smA_pe[ib])])
                A_done = pb.cnt["dve"]
                for h8 in range(SB):
                    h = sb_ * SB + h8
                    kv = h // (NH // NKV)
                    for half in range(2):
                        tp = half
                        smA_pe[ib] = pb.ins("pe", lambda e, h8=h8, half=half, tp=tp, ib=ib: e.transpose(
                                                trp[tp][:], smA[ib][:, h8 * 256 + half * 128: h8 * 256 + (half + 1) * 128], ident[:]),
                                            after=[("dve", A_done), ("act", tr_evac[tp])])
                        tr_evac[tp] = pb.ins("act", lambda e, half=half, tp=tp: e.copy(atc[half][:], trp[tp][:]),
                                             after=[("pe", pb.cnt["pe"]), ("act", pb.cnt["act"])])
                    at_done = pb.cnt["act"]
                    op = h % 2
                    for half in range(2):
                        blk_ = bt - 1 + half
                        pb.ins("pe", lambda e, half=half, blk_=blk_, kv=kv, op=op: e.matmul(
                                   ops_[op][:], vsb[:, blk_ * DKV + kv * 128: blk_ * DKV + kv * 128 + 128],
                                   atc[half][:], start=(half == 0), stop=(half == 1)),
                               sem=(False if half == 0 else "pe"),
                               after=[("act", at_done), ("act", ops_act[op])] if half == 0 else [])
                    ops_act[op] = pb.ins("act", lambda e, h=h, op=op: e.copy(aoT[:, h * 128:(h + 1) * 128], ops_[op][:]),
                                         after=[("pe", pb.cnt["pe"]), ("act", pb.cnt["act"]),
                                                ("qA", ao_written if h == 0 else 0)])

            batches = [(bt, sb_) for bt in range(cfg.halo // 128, nT) for sb_ in range(n_sb)]
            pend = None
            for idx, (bt, sb_) in enumerate(batches):
                ib = idx % 2
                s_pe = a3b_smm(bt, sb_, ib)
                if pend is not None:
                    a3b_chain_av(*pend)
                    if pend[1] == n_sb - 1:
                        ao_written = pb.dma("qA", aoTd[pend[0] * 128:(pend[0] + 1) * 128, :], aoT[:],
                                            after=[("act", pb.cnt["act"])])
                pend = (bt, sb_, ib, s_pe)
            a3b_chain_av(*pend)
            ao_written = pb.dma("qA", aoTd[pend[0] * 128:(pend[0] + 1) * 128, :], aoT[:],
                                after=[("act", pb.cnt["act"])])
            pb.drain_queues()
            pb.flush()

        with ExitStack() as stc:
            sbc = lambda n, s, d: stc.enter_context(nc.sbuf_tensor(n, s, d))
            woS = sbc("a_wo", [128, nHT * 512], BF16)
            aoc = [sbc(f"a_ao{i}", [128, NH * 128], BF16) for i in range(2)]
            rres = [sbc(f"a_rr{i}", [128, 512], BF16) for i in range(2)]
            rout = [sbc(f"a_rx{i}", [128, 512], BF16) for i in range(2)]
            pps = [stc.enter_context(nc.psum_tensor(f"a_pp{i}", [128, 512], F32)) for i in range(2)]

            ao_free = {0: 0, 1: 0}
            rr_free = {0: 0, 1: 0}
            ro_free = {0: 0, 1: 0}
            pps_dve = {0: 0, 1: 0}
            for hc in range(D // 512):
                rows = nHT * 512
                pb.dma("qS", woS[:], Wo_nat[hc * rows:(hc + 1) * rows, :],
                       transpose=True, after=[("pe", pb.cnt["pe"])])
                wo_c = pb.cnt["qS"]
                for b in range(cfg.halo // 128, nT):
                    bp = b % 2
                    r = b * 128
                    cda = pb.dma("qS", aoc[bp][:], aoTd[r:r + 128, :], after=[("pe", ao_free[bp])])
                    cd = pb.dma("qA", rres[bp][:], R1[r:r + 128, hc * 512:(hc + 1) * 512],
                                after=[("dve", rr_free[bp])])
                    pp = bp
                    for dot in range(nHT):
                        pb.ins("pe", lambda e, dot=dot, pp=pp, bp=bp: e.matmul(
                                   pps[pp][:], aoc[bp][:, dot * 128:(dot + 1) * 128],
                                   woS[:, dot * 512:(dot + 1) * 512],
                                   start=(dot == 0), stop=False),
                               sem=False,
                               after=[("qS", max(wo_c, cda)), ("dve", pps_dve[pp])] if dot == 0 else [])
                    pb.ins("pe", lambda e, hc=hc, pp=pp: e.matmul(pps[pp][:], e0[:], boT[:, hc * 512:(hc + 1) * 512],
                                                                  start=False, stop=True))
                    ao_free[bp] = pb.cnt["pe"]
                    rr_free[bp] = pps_dve[pp] = pb.ins(
                        "dve", lambda e, pp=pp, bp=bp: e.scalar_tensor_tensor(
                            rout[bp][:], pps[pp][:], 1.0, rres[bp][:], ALU.mult, ALU.add),
                        after=[("pe", pb.cnt["pe"]), ("qA", cd), ("dve", pb.cnt["dve"]), ("qA", ro_free[bp])])
                    ro_free[bp] = pb.dma("qA", R2[r:r + 128, hc * 512:(hc + 1) * 512], rout[bp][:],
                                         after=[("dve", pb.cnt["dve"])])
            pb.drain_queues()
            pb.flush()



def _host_inputs(cfg, core, inputs, n_cores=8):
    import ml_dtypes
    D, T, NH, NKV, DK = cfg.D, cfg.T, cfg.NH, cfg.NKV, cfg.DK
    x = inputs["L0_attn_x"]
    B, S, _ = x.shape
    per_seq = (n_cores // B)
    seq = core // per_seq
    start = (core % per_seq) * cfg.T_own
    lo = start - cfg.halo
    if lo < 0:
        pad = np.zeros((-lo, D), dtype=np.float32)
        xs = np.concatenate([pad, x[seq, 0:start + cfg.T_own]], axis=0)
    else:
        xs = x[seq, lo:start + cfg.T_own]
    xs = np.ascontiguousarray(xs, dtype=np.float32)

    i_ = np.arange(128)[:, None]
    j_ = np.arange(256)[None, :]
    okA = (j_ - i_ >= 64) & (j_ - i_ <= 128)
    okB = okA & (j_ >= 128)
    m01a = okA.astype(np.float32)
    fila = np.where(okA, 0.0, MASK_FILL).astype(np.float32)
    first = (start == 0)
    m01b = (okB if first else okA).astype(np.float32)
    filb = np.where(okB if first else okA, 0.0, MASK_FILL).astype(np.float32)

    pos = np.arange(lo, start + cfg.T_own, dtype=np.int64)
    pos = np.maximum(pos, 0).astype(np.float32)
    inv_freq = (1.0 / (ROPE_BASE ** (np.arange(0, DK, 2, dtype=np.float32) / DK))).astype(np.float32)
    freqs = (pos[:, None] * inv_freq[None, :]).astype(np.float32)
    cost = np.cos(freqs).astype(np.float32)
    sint = np.sin(freqs).astype(np.float32)
    nTt = cfg.nT
    cost = np.tile(cost.reshape(nTt, 128, 1, 64), (1, 1, 4, 1)).transpose(1, 0, 2, 3).reshape(128, nTt * 256)
    sint = np.tile(sint.reshape(nTt, 128, 1, 64), (1, 1, 4, 1)).transpose(1, 0, 2, 3).reshape(128, nTt * 256)

    qn4 = np.broadcast_to(np.tile(inputs["L0_attn_qn_w"], 4)[None, :], (128, 512))
    kn4 = np.broadcast_to(np.tile(inputs["L0_attn_kn_w"], NKV)[None, :], (128, NKV * DK))
    scl = np.broadcast_to(np.asarray(inputs["L0_attn_scale"], dtype=np.float32).reshape(1, 1), (128, 1))

    e0 = np.zeros((128, 128), dtype=np.float32)
    e0[0, :] = 1.0

    return {
        "x": xs,
        "gW": inputs["L0_ffn_gate_W_packed"], "gS": inputs["L0_ffn_gate_scales"],
        "uW": inputs["L0_ffn_up_W_packed"], "uS": inputs["L0_ffn_up_scales"],
        "dW": inputs["L0_ffn_down_W_packed"], "dS": inputs["L0_ffn_down_scales"],
        "Wq": inputs["L0_attn_Wq"], "Wk": inputs["L0_attn_Wk"],
        "Wv": inputs["L0_attn_Wv"], "Wo": inputs["L0_attn_Wo"],
        "bq": inputs["L0_attn_bq"].reshape(1, -1), "bk": inputs["L0_attn_bk"].reshape(1, -1),
        "bv": inputs["L0_attn_bv"].reshape(1, -1), "bo": inputs["L0_attn_bo"].reshape(1, -1),
        "qn4": np.ascontiguousarray(qn4, dtype=np.float32),
        "kn4": np.ascontiguousarray(kn4, dtype=np.float32),
        "scl": np.ascontiguousarray(scl, dtype=np.float32),
        "ident": np.eye(128, dtype=np.float32).astype(ml_dtypes.bfloat16),
        "e0": e0.astype(ml_dtypes.bfloat16),
        "m01a": m01a, "fila": fila, "m01b": m01b, "filb": filb,
        "cost": np.ascontiguousarray(cost), "sint": np.ascontiguousarray(sint),
    }


def kernel(**inputs):
    cfg = CFG
    n_cores = 8
    in_maps = [_host_inputs(cfg, c, inputs) for c in range(n_cores)]
    nc = build_program(cfg, None)
    res = run_bass_kernel_spmd(nc, in_maps, core_ids=list(range(n_cores)))
    x = inputs["L0_attn_x"]
    B, S, D = x.shape
    out = np.empty((B, S, D), dtype=np.float32)
    per_seq = n_cores // B
    for c in range(n_cores):
        seq = c // per_seq
        start = (c % per_seq) * cfg.T_own
        out[seq, start:start + cfg.T_own] = res.results[c]["out"]
    return out
